# version 17
# baseline (speedup 1.0000x reference)
import contextlib
import numpy as np
import concourse.bass as bass
import concourse.mybir as mybir
from concourse.bass_utils import run_bass_kernel_spmd

F32 = mybir.dt.float32
BF16 = mybir.dt.bfloat16
AF = mybir.ActivationFunctionType
ALU = mybir.AluOpType
AXX = mybir.AxisListType.X

D = 2048
S = 2048
NH = 8
DQ = 128
DV = 256
NCH = 16
DH = 8192
NKT = 16
NTT = 16
QSCALE = float(DQ) ** -0.5
EPS_MH = 1e-6
EPS_LN = 1e-5
NEGBIG = -1.0e30

_DEBUG = False

_ENGINES = ("sync", "tensor", "vector", "scalar", "gpsimd")


DMA_SEMS = ("cst", "dmx", "dw0", "dw1", "ds1", "dsg", "dgt", "dmp", "ddc",
            "dgr", "dus", "dds", "dwx", "dsw", "dnl", "dsq",
            "dc0", "dc1", "dsh0", "dsh1", "dwo", "dp0", "dp1", "dsn", "dst",
            "dlt", "du0", "du1", "dsd0", "dsd1", "dhb", "dd0", "dd1", "dd2", "dd3", "dso0", "dso1",
            "do0", "do1", "dou")
ALL_SEMS = DMA_SEMS + ("pe", "vec", "sca", "gps", "bar")


class _Build:

    def __init__(self):
        self.ops = {e: [] for e in _ENGINES}
        self.cnt = {s: 0 for s in ALL_SEMS}

    def fence(self, sem):
        v = self.cnt[sem]
        return (sem, v) if v else None

    def emit(self, eng, fn=None, waits=(), inc=None, n=1):
        tok = None
        if inc is not None:
            self.cnt[inc] += n
            tok = self.cnt[inc]
        ws = []
        for w in waits:
            if w is None:
                continue
            s, v = w
            if v is None:
                continue
            ws.append((s, v))
        self.ops[eng].append((fn, tuple(ws), inc, n))
        return tok

    def barrier(self):
        drains = [self.fence(sm) for sm in DMA_SEMS]
        own = {"tensor": "pe", "vector": "vec", "scalar": "sca", "gpsimd": "gps"}
        tgt = self.cnt["bar"] + len(_ENGINES)
        for e in _ENGINES:
            w = list(drains)
            es = own.get(e)
            if es and self.cnt[es]:
                w.append((es, self.cnt[es]))
            self.emit(e, lambda eng: eng.nop(), waits=w, inc="bar")
        for e in _ENGINES:
            self.emit(e, None, waits=[("bar", tgt)])

    def flush(self, blk, handles, lastw):
        for ename in _ENGINES:
            ops = self.ops[ename]
            if not ops:
                continue

            def body(eng, ops=ops, ename=ename):
                lw = lastw[ename]
                for fn, waits, inc, n in ops:
                    for s, v in waits:
                        if lw.get(s, 0) < v:
                            eng.wait_ge(handles[s], v)
                            lw[s] = v
                    if fn is None:
                        continue
                    ins = fn(eng)
                    if inc is not None:
                        ins.then_inc(handles[inc], n)

            getattr(blk, ename)(body)
            self.ops[ename] = []


def _dma(dst, src):
    return lambda eng, d=dst, s=src: eng.dma_start(out=d, in_=s)


def _dma_nc(dst, src):
    return lambda eng, d=dst, s=src: eng.dma_start(out=d, in_=s, allow_slow_non_contiguous=True)


def build_nc(debug=False):
    nc = bass.Bass("TRN2", target_bir_lowering=False)

    xT_d = nc.dram_tensor("xT", [D, S], F32, kind="ExternalInput").ap()
    xn_d = nc.dram_tensor("xn", [S, D], F32, kind="ExternalInput").ap()
    wq_t = nc.dram_tensor("wq_t", [8, 128, D], F32, kind="ExternalInput").ap()
    wk_t = nc.dram_tensor("wk_t", [8, 128, D], F32, kind="ExternalInput").ap()
    wv_t = nc.dram_tensor("wv_t", [16, 128, D], F32, kind="ExternalInput").ap()
    wog_t = nc.dram_tensor("wog_t", [16, 128, D], F32, kind="ExternalInput").ap()
    wif_t = nc.dram_tensor("wif_t", [128, 256], F32, kind="ExternalInput").ap()
    bif_d = nc.dram_tensor("bif", [16, 1], F32, kind="ExternalInput").ap()
    gam_d = nc.dram_tensor("gam", [D, 1], F32, kind="ExternalInput").ap()
    woutT_d = nc.dram_tensor("woutT", [D, D], F32, kind="ExternalInput").ap()
    up_t = nc.dram_tensor("up_t", [64, 128, D], F32, kind="ExternalInput").ap()
    dn_t = nc.dram_tensor("dn_t", [16, 128, DH], F32, kind="ExternalInput").ap()
    upsc_d = nc.dram_tensor("upsc", [DH, 1], F32, kind="ExternalInput").ap()
    dnsc_d = nc.dram_tensor("dnsc", [D, 1], F32, kind="ExternalInput").ap()
    osc_d = nc.dram_tensor("osc", [1, 1], F32, kind="ExternalInput").ap()
    ident_d = nc.dram_tensor("ident", [128, 128], F32, kind="ExternalInput").ap()
    cmask_d = nc.dram_tensor("cmask", [128, 128], F32, kind="ExternalInput").ap()
    ones_d = nc.dram_tensor("ones_r", [1, 1024], F32, kind="ExternalInput").ap()

    spill_kind = "ExternalOutput" if debug else "Internal"
    qT_d = nc.dram_tensor("qT_s", [NH * DQ, S], F32, kind=spill_kind).ap()
    kT_d = nc.dram_tensor("kT_s", [NH * DQ, S], F32, kind=spill_kind).ap()
    vT_d = nc.dram_tensor("vT_s", [NH * DV, S], F32, kind=spill_kind).ap()
    sgT_d = nc.dram_tensor("sgT_s", [D, S], F32, kind=spill_kind).ap()
    hoT_d = nc.dram_tensor("hoT_s", [D, S], F32, kind=spill_kind).ap()
    ln1T_d = nc.dram_tensor("ln1T_s", [D, S], F32, kind=spill_kind).ap()
    lnh_d = nc.dram_tensor("lnh_s", [NKT, 128, S], BF16, kind="Internal").ap()
    lnl_d = nc.dram_tensor("lnl_s", [NKT, 128, S], BF16, kind="Internal").ap()
    woh_d = nc.dram_tensor("woh_s", [NKT, 128, S], BF16, kind="Internal").ap()
    wol_d = nc.dram_tensor("wol_s", [NKT, 128, S], BF16, kind="Internal").ap()
    dnsg_d = nc.dram_tensor("dnsg_s", [16, 128, DH], BF16, kind="Internal").ap()
    ln1n_d = nc.dram_tensor("ln1n_s", [S, D], F32, kind=spill_kind).ap()
    hidT_h_d = nc.dram_tensor("hidT_h_s", [DH, S], BF16, kind="Internal").ap()
    hidT_l_d = nc.dram_tensor("hidT_l_s", [DH, S], BF16, kind="Internal").ap()
    oT_d = nc.dram_tensor("oT_s", [D, S], F32, kind=spill_kind).ap()
    gate_d = nc.dram_tensor("gate_s", [3, 8, S], F32, kind=spill_kind).ap()
    gch_d = nc.dram_tensor("gch_s", [128, 128], F32, kind=spill_kind).ap()
    negM_d = nc.dram_tensor("negM_s", [128, 128], F32, kind=spill_kind).ap()
    out_d = nc.dram_tensor("out", [S, D], F32, kind="ExternalOutput").ap()

    b = _Build()
    bank_free = {}

    def ps_wait(bank):
        return [bank_free[bank]] if bank in bank_free else []

    with contextlib.ExitStack() as top:
        sems = {}
        for s in ALL_SEMS:
            sems[s] = top.enter_context(nc.semaphore(s))
        ps = [top.enter_context(nc.psum_tensor(f"ps{i}", [128, 512], F32)) for i in range(8)]
        ident = top.enter_context(nc.sbuf_tensor("ident_sb", [128, 128], F32))
        cmask = top.enter_context(nc.sbuf_tensor("cmask_sb", [128, 128], F32))
        ones_sb = top.enter_context(nc.sbuf_tensor("ones_sb", [1, 128], F32))
        lastw = {e: {} for e in _ENGINES}

        with nc.Block() as blk:
            b.emit("sync", _dma(ident[:], ident_d), inc="cst", n=16)
            b.emit("sync", _dma(cmask[:], cmask_d), inc="cst", n=16)
            b.emit("sync", _dma(ones_sb[:], ones_d[0:1, 0:128]), inc="cst", n=16)

            g_st = contextlib.ExitStack()
            wT = g_st.enter_context(nc.sbuf_tensor("wT_sb", [128, 128], F32))
            EscT = g_st.enter_context(nc.sbuf_tensor("EscT_sb", [128, 128], F32))
            enmT = g_st.enter_context(nc.sbuf_tensor("enmT_sb", [128, 128], F32))
            decb = g_st.enter_context(nc.sbuf_tensor("decb_sb", [128, 128], F32))
            gam_sb = g_st.enter_context(nc.sbuf_tensor("gam_sb", [128, 16], F32))

            with contextlib.ExitStack() as st:
                mglob = st.enter_context(nc.sbuf_tensor("mglob_sb", [8, S], F32))
                flc = st.enter_context(nc.sbuf_tensor("flc_sb", [128, 128], F32))
                ic = st.enter_context(nc.sbuf_tensor("ic_sb", [128, 128], F32))
                mcc = st.enter_context(nc.sbuf_tensor("mc_sb", [128, 128], F32))
                beta = st.enter_context(nc.sbuf_tensor("beta_sb", [128, 128], F32))
                gch = st.enter_context(nc.sbuf_tensor("gch_sb", [128, 128], F32))
                Mch = st.enter_context(nc.sbuf_tensor("Mch_sb", [128, 128], F32))
                negM = st.enter_context(nc.sbuf_tensor("negM_sb", [128, 128], F32))
                negMT = st.enter_context(nc.sbuf_tensor("negMT_sb", [128, 1], F32))
                mprev = st.enter_context(nc.sbuf_tensor("mprev_sb", [128, 1], F32))
                wst = st.enter_context(nc.sbuf_tensor("wst_sb", [128, 128], F32))
                Ee = st.enter_context(nc.sbuf_tensor("Ee_sb", [128, 128], F32))
                enm = st.enter_context(nc.sbuf_tensor("enm_sb", [128, 128], F32))
                ET = st.enter_context(nc.sbuf_tensor("ET_sb", [128, 128], F32))
                decrow = st.enter_context(nc.sbuf_tensor("decrow_sb", [1, 128], F32))
                xh_sb = st.enter_context(nc.sbuf_tensor("xh_sb", [128, NKT * S], BF16))
                xl_sb = st.enter_context(nc.sbuf_tensor("xl_sb", [128, NKT * S], BF16))
                xstage = [st.enter_context(nc.sbuf_tensor("xstage0", [128, S], F32))]
                wbuf = [st.enter_context(nc.sbuf_tensor("wbuf0", [128, D], F32))]
                wh = [st.enter_context(nc.sbuf_tensor(f"wh{i}", [128, D], BF16)) for i in range(2)]
                wl = [st.enter_context(nc.sbuf_tensor(f"wl{i}", [128, D], BF16)) for i in range(2)]
                strip = [st.enter_context(nc.sbuf_tensor("strip0", [128, S], F32))]
                wif_sb = st.enter_context(nc.sbuf_tensor("wif_sb", [128, 256], F32))
                wifh = st.enter_context(nc.sbuf_tensor("wifh_sb", [128, 256], BF16))
                wifl = st.enter_context(nc.sbuf_tensor("wifl_sb", [128, 256], BF16))
                bi_sb = st.enter_context(nc.sbuf_tensor("bi_sb", [8, 1], F32))
                bf_sb = st.enter_context(nc.sbuf_tensor("bf_sb", [8, 1], F32))
                bi15 = st.enter_context(nc.sbuf_tensor("bi15_sb", [8, 1], F32))
                bf15 = st.enter_context(nc.sbuf_tensor("bf15_sb", [8, 1], F32))
                isc = st.enter_context(nc.sbuf_tensor("isc_sb", [8, S], F32))
                fsc = st.enter_context(nc.sbuf_tensor("fsc_sb", [8, S], F32))

                xsp = {}
                xsp_tok = {}
                for k in range(NKT):
                    kp = 0
                    b.emit("sync",
                           _dma(xstage[kp][:], xT_d[k * 128:(k + 1) * 128, :]),
                           waits=[xsp.get(kp)], inc="dmx", n=16)
                    f_xs = b.fence("dmx")
                    b.emit("vector",
                           lambda eng, o=xh_sb[:, k * S:(k + 1) * S], i=xstage[kp][:]:
                           eng.tensor_copy(o, i),
                           waits=[f_xs], inc="vec")
                    tk = b.emit("vector",
                                lambda eng, o=xl_sb[:, k * S:(k + 1) * S], i0=xstage[kp][:],
                                i1=xh_sb[:, k * S:(k + 1) * S]:
                                eng.tensor_tensor(o, i0, i1, ALU.subtract),
                                waits=[], inc="vec")
                    xsp[kp] = ("vec", tk)
                    xsp_tok[k] = ("vec", tk)
                f_xT = ("vec", b.cnt["vec"])
                b.emit("sync", _dma(wif_sb[:], wif_t), inc="cst", n=16)
                f_wif = b.fence("cst")
                b.emit("vector", lambda eng: eng.tensor_copy(wifh[:], wif_sb[:]),
                       waits=[f_wif], inc="vec")
                t_wifl = b.emit("vector",
                                lambda eng: eng.tensor_tensor(wifl[:], wif_sb[:], wifh[:], ALU.subtract),
                                waits=[], inc="vec")
                b.emit("sync", _dma(bi_sb[:], bif_d[0:8, :]), inc="cst", n=16)
                b.emit("sync", _dma(bf_sb[:], bif_d[8:16, :]), inc="cst", n=16)
                b.emit("sync",
                       _dma_nc(gam_sb[:].rearrange("d (hh x) -> d hh x", x=1),
                               gam_d.rearrange("(hh d) x -> d hh x", d=128)),
                       inc="cst", n=16)
                f_cst1 = b.fence("cst")
                t_bi15 = b.emit("vector",
                                lambda eng: eng.tensor_scalar(bi15[:], bi_sb[:], 1.0 / 15.0, None, ALU.mult),
                                waits=[f_cst1], inc="vec")
                t_bf15 = b.emit("vector",
                                lambda eng: eng.tensor_scalar(bf15[:], bf_sb[:], 1.0 / 15.0, None, ALU.mult),
                                waits=[f_cst1], inc="vec")

                gtoks = []
                for half, (bb, dstt, btok) in enumerate([(bi15, isc, t_bi15), (bf15, fsc, t_bf15)]):
                    for n in range(4):
                        pp = 2 + half
                        tok = None
                        for k in range(NKT):
                            waits = [xsp_tok[k]]
                            if k == 0:
                                waits += ps_wait(pp) + [("vec", t_wifl)]
                            csl2 = slice(k * 16 + half * 8, k * 16 + half * 8 + 8)
                            rh = xh_sb[:, k * S + n * 512: k * S + n * 512 + 512]
                            rl = xl_sb[:, k * S + n * 512: k * S + n * 512 + 512]
                            b.emit("tensor",
                                   lambda eng, o=ps[pp][0:8, 0:512], l=wifh[:, csl2], r=rh,
                                   st_=(k == 0):
                                   eng.matmul(o, l, r, start=st_, stop=False),
                                   waits=waits)
                            b.emit("tensor",
                                   lambda eng, o=ps[pp][0:8, 0:512], l=wifh[:, csl2], r=rl:
                                   eng.matmul(o, l, r, start=False, stop=False))
                            tok = b.emit("tensor",
                                         lambda eng, o=ps[pp][0:8, 0:512], l=wifl[:, csl2], r=rh,
                                         sp_=(k == NKT - 1):
                                         eng.matmul(o, l, r, start=False, stop=sp_),
                                         inc="pe" if k == NKT - 1 else None)
                        te = b.emit("scalar",
                                    lambda eng, o=dstt[:, n * 512:(n + 1) * 512], i=ps[pp][0:8, 0:512],
                                    bias=bb[:]:
                                    eng.activation(o, i, AF.Tanh, bias=bias, scale=1.0 / 15.0),
                                    waits=[("pe", tok), ("vec", btok)], inc="sca")
                        bank_free[pp] = ("sca", te)
                        gtoks.append(("sca", te))

                t_ii = b.emit("vector",
                              lambda eng: eng.tensor_scalar(isc[:], isc[:], 15.0, None, ALU.mult),
                              waits=gtoks[:4], inc="vec")
                t_sf = b.emit("scalar",
                              lambda eng: eng.activation(fsc[:], fsc[:], AF.Sigmoid, scale=15.0),
                              waits=gtoks[4:], inc="sca")
                t_fl = b.emit("scalar",
                              lambda eng: eng.activation(fsc[:], fsc[:], AF.Ln),
                              waits=[("sca", t_sf)], inc="sca")
                t_m = b.emit("vector",
                             lambda eng: eng.tensor_tensor_scan(mglob[:], fsc[:], isc[:], 0.0, ALU.add, ALU.max),
                             waits=[("sca", t_fl), ("vec", t_ii)], inc="vec")
                b.emit("gpsimd", _dma(gate_d[0], fsc[:]), waits=[("sca", t_fl)], inc="dsg", n=16)
                b.emit("gpsimd", _dma(gate_d[1], isc[:]), waits=[("vec", t_ii)], inc="dsg", n=16)
                b.emit("gpsimd", _dma(gate_d[2], mglob[:]), waits=[("vec", t_m)], inc="dsg", n=16)
                f_dsg = b.fence("dsg")
                b.emit("scalar",
                       _dma(flc[:], gate_d[0].rearrange("h (c t) -> (h c) t", t=128)),
                       waits=[f_dsg], inc="dgt", n=16)
                b.emit("scalar",
                       _dma(ic[:], gate_d[1].rearrange("h (c t) -> (h c) t", t=128)),
                       waits=[f_dsg], inc="dgt", n=16)
                b.emit("scalar",
                       _dma(mcc[:], gate_d[2].rearrange("h (c t) -> (h c) t", t=128)),
                       waits=[f_dsg], inc="dgt", n=16)
                f_gt1 = b.fence("dgt")
                t_beta = b.emit("vector",
                                lambda eng: eng.tensor_tensor_scan(beta[:], flc[:], flc[:], 0.0, ALU.add, ALU.bypass),
                                waits=[f_gt1], inc="vec")
                t_g = b.emit("vector",
                             lambda eng: eng.tensor_sub(gch[:], ic[:], beta[:]),
                             waits=[f_gt1, ("vec", t_beta)], inc="vec")
                t_M = b.emit("vector",
                             lambda eng: eng.tensor_sub(Mch[:], mcc[:], beta[:]),
                             waits=[f_gt1, ("vec", t_beta)], inc="vec")
                t_negM = b.emit("vector",
                                lambda eng: eng.tensor_scalar(negM[:], Mch[:], -1.0, None, ALU.mult),
                                waits=[("vec", t_M)], inc="vec")
                t_negMT = b.emit("vector",
                                 lambda eng: eng.tensor_scalar(negMT[:], Mch[:, 127:128], -1.0, None, ALU.mult),
                                 waits=[("vec", t_M)], inc="vec")
                b.emit("gpsimd", _dma(gch_d, gch[:]), waits=[("vec", t_g)], inc="dsg", n=16)
                b.emit("gpsimd", _dma(negM_d, negM[:]), waits=[("vec", t_negM)], inc="dsg", n=16)
                t_mpz = b.emit("vector", lambda eng: eng.memset(mprev[:], 0.0), inc="vec")
                for h in range(NH):
                    b.emit("scalar",
                           _dma_nc(mprev[h * 16 + 1:h * 16 + 16, 0:1],
                                   mcc[h * 16:h * 16 + 15, 127:128]),
                           waits=[f_gt1, ("vec", t_mpz)], inc="dmp", n=16)
                f_mprev = b.fence("dmp")
                t_wst = b.emit("scalar",
                               lambda eng: eng.activation(wst[:], gch[:], AF.Exp, bias=negMT[:]),
                               waits=[("vec", t_g), ("vec", t_negMT)], inc="sca")
                t_E = b.emit("scalar",
                             lambda eng: eng.activation(Ee[:], Mch[:], AF.Exp, bias=mprev[:], scale=-1.0),
                             waits=[("vec", t_M), f_mprev], inc="sca")
                t_enm = b.emit("scalar",
                               lambda eng: eng.activation(enm[:], mcc[:], AF.Exp, scale=-1.0),
                               waits=[f_gt1], inc="sca")


                projs = [(wq_t, 8, qT_d, "copy"), (wk_t, 8, kT_d, "copy"),
                         (wv_t, 16, vT_d, "copy"), (wog_t, 16, sgT_d, "sigmoid")]
                wbuf_busy = {}
                wsplit_busy = {}
                gidx = 0
                for w_src, n_tiles, dst, mode in projs:
                    for m in range(n_tiles):
                        wp = gidx % 2
                        wb = 0
                        sp = 0
                        gidx += 1
                        b.emit("sync", _dma(wbuf[wb][:], w_src[m]),
                               waits=[wbuf_busy.get(wb)], inc="dw0", n=16)
                        f_w = b.fence("dw0")
                        b.emit("gpsimd",
                               lambda eng, o=wh[wp][:], i=wbuf[wb][:]: eng.tensor_copy(o, i),
                               waits=[f_w, wsplit_busy.get(wp)], inc="gps")
                        t_wsp = b.emit("gpsimd",
                                       lambda eng, o=wl[wp][:], i0=wbuf[wb][:], i1=wh[wp][:]:
                                       eng.tensor_tensor(o, i0, i1, ALU.subtract),
                                       waits=[], inc="gps")
                        wbuf_busy[wb] = ("gps", t_wsp)
                        evac_toks = []
                        for n in range(4):
                            pp = n % 2
                            tok = None
                            for k in range(NKT):
                                waits = [xsp_tok[k]]
                                if k == 0:
                                    waits += ps_wait(pp) + [("gps", t_wsp), b.fence("ds1")]
                                rh = xh_sb[:, k * S + n * 512: k * S + n * 512 + 512]
                                rl = xl_sb[:, k * S + n * 512: k * S + n * 512 + 512]
                                lh = wh[wp][:, k * 128:(k + 1) * 128]
                                ll = wl[wp][:, k * 128:(k + 1) * 128]
                                b.emit("tensor",
                                       lambda eng, o=ps[pp][:, 0:512], l=lh, r=rh, st_=(k == 0):
                                       eng.matmul(o, l, r, start=st_, stop=False),
                                       waits=waits)
                                b.emit("tensor",
                                       lambda eng, o=ps[pp][:, 0:512], l=lh, r=rl:
                                       eng.matmul(o, l, r, start=False, stop=False))
                                tok = b.emit("tensor",
                                             lambda eng, o=ps[pp][:, 0:512], l=ll, r=rh,
                                             sp_=(k == NKT - 1):
                                             eng.matmul(o, l, r, start=False, stop=sp_),
                                             inc="pe" if k == NKT - 1 else None)
                            wsplit_busy[wp] = ("pe", tok)
                            dstr = strip[sp][:, n * 512:(n + 1) * 512]
                            if mode == "sigmoid":
                                te = b.emit("scalar",
                                            lambda eng, o=dstr, i=ps[pp][:, 0:512]:
                                            eng.activation(o, i, AF.Sigmoid),
                                            waits=[("pe", tok)], inc="sca")
                                bank_free[pp] = ("sca", te)
                            elif pp == 0:
                                te = b.emit("scalar",
                                            lambda eng, o=dstr, i=ps[pp][:, 0:512]: eng.copy(o, i),
                                            waits=[("pe", tok)], inc="sca")
                                bank_free[pp] = ("sca", te)
                            else:
                                te = b.emit("vector",
                                            lambda eng, o=dstr, i=ps[pp][:, 0:512]: eng.tensor_copy(o, i),
                                            waits=[("pe", tok)], inc="vec")
                                bank_free[pp] = ("vec", te)
                            evac_toks.append(bank_free[pp])
                        b.emit("gpsimd", _dma(dst[m * 128:(m + 1) * 128, :], strip[sp][:]),
                               waits=evac_toks, inc="ds1", n=16)

                wo_sp_busy = None
                for k in range(NKT):
                    b.emit("sync", _dma(wbuf[0][:], woutT_d[k * 128:(k + 1) * 128, :]),
                           waits=[wbuf_busy.get(0)], inc="dwx", n=16)
                    f_wx = b.fence("dwx")
                    b.emit("vector",
                           lambda eng, o=wh[0][:], i=wbuf[0][:]: eng.tensor_copy(o, i),
                           waits=[f_wx, wo_sp_busy, wsplit_busy.get(0) if k == 0 else None], inc="vec")
                    t_wsp2 = b.emit("vector",
                                    lambda eng, o=wl[0][:], i0=wbuf[0][:], i1=wh[0][:]:
                                    eng.tensor_tensor(o, i0, i1, ALU.subtract),
                                    waits=[], inc="vec")
                    wbuf_busy[0] = ("vec", t_wsp2)
                    b.emit("gpsimd", _dma(woh_d[k], wh[0][:]), waits=[("vec", t_wsp2)], inc="dsw", n=16)
                    b.emit("gpsimd", _dma(wol_d[k], wl[0][:]), waits=[("vec", t_wsp2)], inc="dsw", n=16)
                    wo_sp_busy = b.fence("dsw")
                f_wodram = b.fence("dsw")

                tr_tok = {}
                for idx, (src, dst, wtok) in enumerate([(wst, wT, ("sca", t_wst)),
                                                        (Ee, ET, ("sca", t_E)),
                                                        (enm, enmT, ("sca", t_enm))]):
                    bank = 6 + (idx % 2)
                    tok = b.emit("tensor",
                                 lambda eng, o=ps[bank][:, 0:128], i=src[:]:
                                 eng.transpose(o, i, ident[:]),
                                 waits=ps_wait(bank) + [wtok, f_cst1], inc="pe")
                    te = b.emit("scalar",
                                lambda eng, o=dst[:], i=ps[bank][:, 0:128]: eng.copy(o, i),
                                waits=[("pe", tok)], inc="sca")
                    bank_free[bank] = ("sca", te)
                    tr_tok[dst.name] = ("sca", te)
                tok_wT, tok_ET, tok_enmT = tr_tok["wT_sb"], tr_tok["ET_sb"], tr_tok["enmT_sb"]

                t_EscT = b.emit("vector",
                                lambda eng: eng.tensor_scalar(EscT[:], ET[:], QSCALE, None, ALU.mult),
                                waits=[tok_ET], inc="vec")
                b.emit("sync", _dma(decrow[:], ET[127:128, :]), waits=[tok_ET], inc="ddc", n=16)
                tok = b.emit("tensor",
                             lambda eng: eng.matmul(ps[6][:, 0:128], ones_sb[0:1, 0:128], decrow[:],
                                                    start=True, stop=True),
                             waits=ps_wait(6) + [b.fence("ddc"), f_cst1], inc="pe")
                t_decb = b.emit("scalar",
                                lambda eng: eng.copy(decb[:], ps[6][:, 0:128]),
                                waits=[("pe", tok)], inc="sca")
                bank_free[6] = ("sca", t_decb)

                b.flush(blk, sems, lastw)

            b.barrier()

            with contextlib.ExitStack() as st:
                qc = [st.enter_context(nc.sbuf_tensor(f"qc{i}", [128, NH * 128], F32)) for i in range(2)]
                kc = [st.enter_context(nc.sbuf_tensor(f"kc{i}", [128, NH * 128], F32)) for i in range(2)]
                vTc = [st.enter_context(nc.sbuf_tensor(f"vTc{i}", [128, 2 * NH * 128], F32)) for i in range(2)]
                sgc = [st.enter_context(nc.sbuf_tensor(f"sgc{i}", [128, 2 * NH * 128], F32)) for i in range(2)]
                glhs = [st.enter_context(nc.sbuf_tensor(f"glhs{i}", [2, NH * 128], F32)) for i in range(2)]
                grhs = [st.enter_context(nc.sbuf_tensor(f"grhs{i}", [2, NH * 128], F32)) for i in range(2)]
                CA = st.enter_context(nc.sbuf_tensor("CA_sb", [128, 2 * NH * 264], F32))
                VA = [st.enter_context(nc.sbuf_tensor(f"VA{i}", [128, 264], F32)) for i in range(2)]
                kw = [st.enter_context(nc.sbuf_tensor(f"kw{i}", [128, 128], F32)) for i in range(2)]
                wm = [st.enter_context(nc.sbuf_tensor(f"wm{i}", [128, 128], F32)) for i in range(2)]
                we = [st.enter_context(nc.sbuf_tensor(f"we{i}", [128, 128], F32)) for i in range(2)]
                sw = [st.enter_context(nc.sbuf_tensor(f"sw{i}", [128, 128], F32)) for i in range(2)]
                bsb = [st.enter_context(nc.sbuf_tensor(f"bs{i}", [128, 264], F32)) for i in range(2)]
                hsb = [st.enter_context(nc.sbuf_tensor(f"hs{i}", [128, 264], F32)) for i in range(2)]
                hf = [st.enter_context(nc.sbuf_tensor(f"hf{i}", [128, 256], F32)) for i in range(2)]
                hd = [st.enter_context(nc.sbuf_tensor(f"hd{i}", [128, 256], F32)) for i in range(2)]
                hn = [st.enter_context(nc.sbuf_tensor(f"hn{i}", [128, 256], F32)) for i in range(2)]
                scr = st.enter_context(nc.sbuf_tensor("scr_sb", [128, 16], F32))
                junk = st.enter_context(nc.sbuf_tensor("junk2_sb", [128, 256], F32))
                dnst = [st.enter_context(nc.sbuf_tensor(f"dnst{i}", [128, 4096], F32)) for i in range(2)]
                epsc = st.enter_context(nc.sbuf_tensor("epsc_sb", [128, 1], F32))
                dnsg = [st.enter_context(nc.sbuf_tensor(f"dnsg{i}", [128, 4096], BF16)) for i in range(2)]
                hout = [st.enter_context(nc.sbuf_tensor(f"hout{i}", [128, 2 * NH * 128], F32)) for i in range(2)]

                def ca(par, h):
                    off = (par * NH + h) * 264
                    return CA[:, off:off + 257]

                t_caz = b.emit("vector", lambda eng: eng.memset(CA[:], 0.0), inc="vec")
                t_eps = b.emit("vector", lambda eng: eng.memset(epsc[:], EPS_MH), inc="vec")
                t_va0 = b.emit("vector", lambda eng: eng.memset(VA[0][:, 256:264], 1.0), inc="vec")
                t_va1 = b.emit("vector", lambda eng: eng.memset(VA[1][:, 256:264], 1.0), inc="vec")
                t_gl0 = b.emit("vector", lambda eng: eng.memset(glhs[0][0:1, :], 1.0), inc="vec")
                t_gl1 = b.emit("vector", lambda eng: eng.memset(glhs[1][0:1, :], 1.0), inc="vec")
                b.emit("sync", _dma(grhs[0][1:2, :], ones_d), inc="dgr", n=16)
                b.emit("sync", _dma(grhs[1][1:2, :], ones_d), inc="dgr", n=16)
                f_cst2 = b.fence("dgr")

                chunk_busy = {}
                c_tok = {h: ("vec", t_caz) for h in range(NH)}
                va_busy = {0: ("vec", t_va0), 1: ("vec", t_va1)}
                kw_busy = {}
                sbusy = {}
                gl_busy = {0: ("vec", t_gl0), 1: ("vec", t_gl1)}
                gr_busy = {}
                ld = {}

                dn_busy = {}
                dnsg_busy = {}
                for c in range(NCH):
                    pc = c % 2
                    csl = slice(c * 128, (c + 1) * 128)
                    for half2 in range(2):
                        sidx = 2 * c + half2
                        ft2, hh2 = sidx // 2, sidx % 2
                        dnp = sidx % 2
                        dns = "dnl" if dnp == 0 else "dsq"
                        b.emit("sync", _dma(dnst[dnp][:], dn_t[ft2, :, hh2 * 4096:(hh2 + 1) * 4096]),
                               waits=[dn_busy.get(dnp)], inc=dns, n=16)
                        f_dn = b.fence(dns)
                        t_sgn = b.emit("scalar",
                                       lambda eng, o=dnsg[dnp][:], i=dnst[dnp][:]:
                                       eng.activation(o, i, AF.Sign),
                                       waits=[f_dn, dnsg_busy.get(dnp)], inc="sca")
                        dn_busy[dnp] = ("sca", t_sgn)
                        b.emit("gpsimd",
                               _dma(dnsg_d[ft2, :, hh2 * 4096:(hh2 + 1) * 4096], dnsg[dnp][:]),
                               waits=[("sca", t_sgn)], inc="dsg", n=16)
                        dnsg_busy[dnp] = b.fence("dsg")
                    dcp = "dc0" if pc == 0 else "dc1"
                    b.emit("sync",
                           _dma(qc[pc][:].rearrange("d (hh t) -> d hh t", hh=8),
                                qT_d[:, csl].rearrange("(hh d) t -> d hh t", d=128)),
                           waits=[chunk_busy.get(("q", pc))], inc=dcp, n=16)
                    b.emit("sync",
                           _dma(kc[pc][:].rearrange("d (hh t) -> d hh t", hh=8),
                                kT_d[:, csl].rearrange("(hh d) t -> d hh t", d=128)),
                           waits=[chunk_busy.get(("k", pc))], inc=dcp, n=16)
                    b.emit("sync",
                           _dma(vTc[pc][:].rearrange("d (hh t) -> d hh t", hh=16),
                                vT_d[:, csl].rearrange("(hh d) t -> d hh t", d=128)),
                           waits=[chunk_busy.get(("v", pc))], inc=dcp, n=16)
                    b.emit("sync",
                           _dma(sgc[pc][:].rearrange("d (hh t) -> d hh t", hh=16),
                                sgT_d[:, csl].rearrange("(hh d) t -> d hh t", d=128)),
                           waits=[chunk_busy.get(("sg", pc))], inc=dcp, n=16)
                    b.emit("sync",
                           _dma(glhs[pc][1:2, :].rearrange("o (h t) -> o h t", h=8),
                                gch_d.rearrange("(h c) t -> c h t", c=16)[c:c + 1]),
                           waits=[gl_busy.get(pc)], inc=dcp, n=16)
                    b.emit("sync",
                           _dma(grhs[pc][0:1, :].rearrange("o (h t) -> o h t", h=8),
                                negM_d.rearrange("(h c) t -> c h t", c=16)[c:c + 1]),
                           waits=[gr_busy.get(pc), f_cst2], inc=dcp, n=16)
                    f_c = b.fence(dcp)

                    hb = c % 2
                    cpar = c % 2
                    hout_evac_last = None

                    def stageA(h):
                        hc = h * 16 + c
                        hp = h % 2
                        hsl = slice(h * 128, (h + 1) * 128)
                        bt = 6 + hp
                        bS = 2 + hp
                        b.emit("tensor",
                               lambda eng, o=ps[bt][:, 0:128], i=kc[pc][:, hsl]:
                               eng.transpose(o, i, ident[:]),
                               waits=ps_wait(bt) + [f_c], inc=None)
                        b.emit("tensor",
                               lambda eng, o=ps[bt][:, 128:256], i=vTc[pc][:, (2 * h) * 128:(2 * h + 1) * 128]:
                               eng.transpose(o, i, ident[:]))
                        tok = b.emit("tensor",
                                     lambda eng, o=ps[bt][:, 256:384], i=vTc[pc][:, (2 * h + 1) * 128:(2 * h + 2) * 128]:
                                     eng.transpose(o, i, ident[:]),
                                     inc="pe")
                        tokS = b.emit("tensor",
                                      lambda eng, o=ps[bS][:, 0:128], l=kc[pc][:, hsl], r=qc[pc][:, hsl]:
                                      eng.matmul(o, l, r, start=True, stop=True),
                                      waits=ps_wait(bS) + [f_c], inc=None)
                        tokW = b.emit("tensor",
                                      lambda eng, o=ps[bS][:, 128:256], l=glhs[pc][0:2, hsl], r=grhs[pc][0:2, hsl]:
                                      eng.matmul(o, l, r, start=True, stop=True),
                                      waits=[("vec", t_gl0 if pc == 0 else t_gl1)],
                                      inc="pe")
                        te = b.emit("scalar",
                                    lambda eng, o=kw[hp][:], i=ps[bt][:, 0:128], w=wT[:, hc:hc + 1]:
                                    eng.mul(o, i, w),
                                    waits=[("pe", tok), tok_wT, kw_busy.get(hp)], inc="sca")
                        te2 = b.emit("scalar",
                                     lambda eng, o=VA[hp][:, 0:256], i=ps[bt][:, 128:384]: eng.copy(o, i),
                                     waits=[va_busy.get(hp)], inc="sca")
                        tv = b.emit("vector",
                                    lambda eng, o=wm[hp][:], i0=ps[bS][:, 128:256]:
                                    eng.tensor_add(o, i0, cmask[:]),
                                    waits=[("pe", tokW), f_cst1, sbusy.get(("wm", hp))], inc="vec")
                        ta = b.emit("scalar",
                                    lambda eng, o=we[hp][:], i=wm[hp][:]: eng.activation(o, i, AF.Exp),
                                    waits=[("vec", tv), sbusy.get(("we", hp))], inc="sca")
                        tsw = b.emit("vector",
                                     lambda eng, o=sw[hp][:], i0=ps[bS][:, 0:128], w=we[hp][:]:
                                     eng.scalar_tensor_tensor(o, i0, QSCALE, w, ALU.mult, ALU.mult),
                                     waits=[("sca", ta), sbusy.get(("sw", hp))], inc="vec")
                        bank_free[bS] = ("vec", tsw)
                        sbusy[("wm", hp)] = ("sca", ta)
                        return dict(hc=hc, hp=hp, hsl=hsl, bt=bt,
                                    tok_kw=("sca", te), tok_va=("sca", te2),
                                    stage1_free=("sca", te2), tsw=tsw)

                    def stageB(st1):
                        nonlocal hout_evac_last
                        hc, hp, hsl, bt = st1["hc"], st1["hp"], st1["hsl"], st1["bt"]
                        h = hc // 16
                        tok_kw, tok_va, stage1_free, tsw = (st1["tok_kw"], st1["tok_va"],
                                                            st1["stage1_free"], st1["tsw"])
                        tokA = b.emit("tensor",
                                      lambda eng, o=ps[4][:, 0:257], l=sw[hp][:], r=VA[hp][:, 0:257]:
                                      eng.matmul(o, l, r, start=True, stop=True),
                                      waits=ps_wait(4) + [("vec", tsw), tok_va], inc="pe")
                        tokB = b.emit("tensor",
                                      lambda eng, o=ps[5][:, 0:257], l=qc[pc][:, hsl], r=ca(cpar, h):
                                      eng.matmul(o, l, r, start=True, stop=True),
                                      waits=ps_wait(5) + [f_c, c_tok[h]], inc="pe")
                        cb = hp
                        tokC = b.emit("tensor",
                                      lambda eng, o=ps[cb][:, 0:257], l=kw[hp][:], r=VA[hp][:, 0:257]:
                                      eng.matmul(o, l, r, start=True, stop=True),
                                      waits=ps_wait(cb) + [tok_kw, tok_va], inc="pe")
                        kw_busy[hp] = ("pe", tokC)
                        va_busy[hp] = ("pe", tokC)
                        tb = b.emit("scalar",
                                    lambda eng, o=bsb[hp][:, 0:257], i=ps[5][:, 0:257], e=EscT[:, hc:hc + 1]:
                                    eng.mul(o, i, e),
                                    waits=[("pe", tokB), ("vec", t_EscT), sbusy.get(("bs", hp))], inc="sca")
                        bank_free[5] = ("sca", tb)
                        th = b.emit("vector",
                                    lambda eng, o=hsb[hp][:, 0:257], a=ps[4][:, 0:257], bb=bsb[hp][:, 0:257]:
                                    eng.tensor_add(o, a, bb),
                                    waits=[("pe", tokA), ("sca", tb), sbusy.get(("hs", hp))], inc="vec")
                        bank_free[4] = ("vec", th)
                        sbusy[("bs", hp)] = ("vec", th)
                        tcu = b.emit("vector",
                                     lambda eng, o=ca(1 - cpar, h), i0=ca(cpar, h), dcol=decb[:, hc:hc + 1],
                                     p=ps[cb][:, 0:257]:
                                     eng.scalar_tensor_tensor(o, i0, dcol, p, ALU.mult, ALU.add),
                                     waits=[("pe", tokC), ("sca", t_decb), c_tok[h]], inc="vec")
                        bank_free[cb] = ("vec", tcu)
                        c_tok[h] = ("vec", tcu)
                        td0 = b.emit("scalar",
                                     lambda eng, o=scr[:, hp * 8:hp * 8 + 1], i=hsb[hp][:, 256:257]:
                                     eng.activation(o, i, AF.Abs),
                                     waits=[("vec", th)], inc="sca")
                        td2 = b.emit("vector",
                                     lambda eng, o=scr[:, hp * 8 + 2:hp * 8 + 3], i=scr[:, hp * 8:hp * 8 + 1],
                                     e=enmT[:, hc:hc + 1]:
                                     eng.tensor_scalar(o, i, e, EPS_MH, ALU.max, ALU.add),
                                     waits=[("sca", td0), tok_enmT], inc="vec")
                        td3 = b.emit("vector",
                                     lambda eng, o=scr[:, hp * 8 + 3:hp * 8 + 4], i=scr[:, hp * 8 + 2:hp * 8 + 3]:
                                     eng.reciprocal(o, i),
                                     waits=[("vec", td2)], inc="vec")
                        tf = b.emit("vector",
                                    lambda eng, o=hf[hp][:], i=hsb[hp][:, 0:256], r=scr[:, hp * 8 + 3:hp * 8 + 4]:
                                    eng.tensor_scalar(o, i, r, None, ALU.mult),
                                    waits=[("vec", td3), sbusy.get(("hf", hp))], inc="vec")
                        sbusy[("hs", hp)] = ("vec", tf)
                        tm1 = b.emit("vector",
                                     lambda eng, o=scr[:, hp * 8 + 4:hp * 8 + 5], i=hf[hp][:]:
                                     eng.reduce_sum(o, i, axis=AXX),
                                     waits=[("vec", tf)], inc="vec")
                        tm2 = b.emit("vector",
                                     lambda eng, o=scr[:, hp * 8 + 5:hp * 8 + 6], i=scr[:, hp * 8 + 4:hp * 8 + 5]:
                                     eng.tensor_scalar(o, i, -1.0 / 256.0, None, ALU.mult),
                                     waits=[("vec", tm1)], inc="vec")
                        tm4 = b.emit("scalar",
                                     lambda eng, o=junk[:, 0:256], i=hf[hp][:], nm=scr[:, hp * 8 + 5:hp * 8 + 6],
                                     a=scr[:, hp * 8 + 6:hp * 8 + 7]:
                                     eng.activation(o, i, AF.Square, bias=nm, accum_out=a),
                                     waits=[("vec", tm2)], inc="sca")
                        tm6 = b.emit("scalar",
                                     lambda eng, o=scr[:, hp * 8 + 6:hp * 8 + 7], i=scr[:, hp * 8 + 6:hp * 8 + 7]:
                                     eng.activation(o, i, AF.Ln, bias=epsc[:], scale=1.0 / 256.0),
                                     waits=[("sca", tm4), ("vec", t_eps)], inc="sca")
                        tm7 = b.emit("scalar",
                                     lambda eng, o=scr[:, hp * 8 + 6:hp * 8 + 7], i=scr[:, hp * 8 + 6:hp * 8 + 7]:
                                     eng.activation(o, i, AF.Exp, scale=-0.5),
                                     waits=[("sca", tm6)], inc="sca")
                        tm8 = b.emit("vector",
                                     lambda eng, o=hn[hp][:], i=hf[hp][:], nm=scr[:, hp * 8 + 5:hp * 8 + 6],
                                     r=scr[:, hp * 8 + 6:hp * 8 + 7]:
                                     eng.tensor_scalar(o, i, nm, r, ALU.add, ALU.mult),
                                     waits=[("sca", tm7), sbusy.get(("hn", hp))], inc="vec")
                        sbusy[("hf", hp)] = ("vec", tm8)
                        b.emit("tensor",
                               lambda eng, o=ps[bt][:, 0:128], i=hn[hp][:, 0:128]:
                               eng.transpose(o, i, ident[:]),
                               waits=[stage1_free, ("vec", tm8)], inc=None)
                        tok = b.emit("tensor",
                                     lambda eng, o=ps[bt][:, 128:256], i=hn[hp][:, 128:256]:
                                     eng.transpose(o, i, ident[:]),
                                     inc="pe")
                        f_dsh = b.fence("dsh0" if hb == 0 else "dsh1") if c >= 2 else None
                        hel = None
                        for half in range(2):
                            col = 2 * h + half
                            hel = b.emit("vector",
                                         lambda eng, o=hout[hb][:, col * 128:(col + 1) * 128],
                                         i0=ps[bt][:, half * 128:(half + 1) * 128],
                                         g=gam_sb[:, col:col + 1],
                                         s2=sgc[pc][:, col * 128:(col + 1) * 128]:
                                         eng.scalar_tensor_tensor(o, i0, g, s2, ALU.mult, ALU.mult),
                                         waits=[("pe", tok), f_cst1, f_c, f_dsh],
                                         inc="vec")
                        bank_free[bt] = ("vec", hel)
                        sbusy[("hn", hp)] = ("vec", hel)
                        sbusy[("sw", hp)] = ("pe", tokA)
                        sbusy[("we", hp)] = ("vec", tsw)
                        hout_evac_last = hel

                    prev = None
                    for h in range(NH):
                        cur = stageA(h)
                        if prev is not None:
                            stageB(prev)
                        prev = cur
                    stageB(prev)
                    b.emit("gpsimd",
                           _dma(hoT_d[:, csl].rearrange("(hh d) t -> d hh t", d=128),
                                hout[hb][:].rearrange("d (hh t) -> d hh t", hh=16)),
                           waits=[("vec", hout_evac_last)], inc="dsh0" if hb == 0 else "dsh1", n=16)
                    gl_busy[pc] = ("pe", b.cnt["pe"])
                    gr_busy[pc] = ("pe", b.cnt["pe"])
                    chunk_busy[("q", pc)] = ("pe", b.cnt["pe"])
                    chunk_busy[("k", pc)] = ("pe", b.cnt["pe"])
                    chunk_busy[("v", pc)] = ("pe", b.cnt["pe"])
                    chunk_busy[("sg", pc)] = ("vec", b.cnt["vec"])
                f_dnsg = b.fence("dsg")
                f_hout0 = b.fence("dsh0")
                f_hout1 = b.fence("dsh1")
                b.flush(blk, sems, lastw)
            g_st.close()

            b.barrier()

            with contextlib.ExitStack() as st:
                woh_sb = st.enter_context(nc.sbuf_tensor("woh_sb", [128, NKT * S], BF16))
                wol_sb = st.enter_context(nc.sbuf_tensor("wol_sb", [128, NKT * S], BF16))
                hoT = st.enter_context(nc.sbuf_tensor("hoT_sb", [128, S], F32))
                lnhTt = st.enter_context(nc.sbuf_tensor("lnhTt_sb", [128, S], BF16))
                lnlTt = st.enter_context(nc.sbuf_tensor("lnlTt_sb", [128, S], BF16))
                hoh = [st.enter_context(nc.sbuf_tensor(f"hoh{i}", [128, 2048], BF16)) for i in range(2)]
                hol = [st.enter_context(nc.sbuf_tensor(f"hol{i}", [128, 2048], BF16)) for i in range(2)]
                xres = [st.enter_context(nc.sbuf_tensor("xres0", [128, 2048], F32))]
                ztile = st.enter_context(nc.sbuf_tensor("ztile_sb", [128, 2048], F32))
                zd = ztile
                lnt = st.enter_context(nc.sbuf_tensor("lnt_sb", [128, 2048], F32))
                lnTt = st.enter_context(nc.sbuf_tensor("lnTt_sb", [128, 2048], F32))
                cols = st.enter_context(nc.sbuf_tensor("cols_sb", [128, 4], F32))
                junk = st.enter_context(nc.sbuf_tensor("junk3_sb", [128, 2048], F32))

                wo_tok = {}
                for k in range(NKT):
                    b.emit("sync", _dma(woh_sb[:, k * S:(k + 1) * S], woh_d[k]),
                           waits=[f_wodram], inc="dwo", n=16)
                    b.emit("sync", _dma(wol_sb[:, k * S:(k + 1) * S], wol_d[k]),
                           waits=[f_wodram], inc="dwo", n=16)
                    wo_tok[k] = b.fence("dwo")
                f_wo = wo_tok[NKT - 1]

                ho_busy = {}
                ho_split_busy = {}
                x_busy = {}
                tr_busy = None
                lnTt_busy = None
                for m in range(NTT):
                    mp = m % 2
                    dpp = "dp0" if mp == 0 else "dp1"
                    b.emit("sync",
                           _dma(hoT[:].rearrange("p (fb t) -> p fb t", fb=16),
                                hoT_d[:, m * 128:(m + 1) * 128]
                                .rearrange("(fb p) t -> p fb t", p=128)),
                           waits=[f_hout0, f_hout1, ho_busy.get(0) or f_wo], inc=dpp, n=16)
                    b.emit("sync", _dma(xres[0][:], xn_d[m * 128:(m + 1) * 128, :]),
                           waits=[x_busy.get(0)], inc=dpp, n=16)
                    f_p = b.fence(dpp)
                    b.emit("vector",
                           lambda eng, o=hoh[mp][:], i=hoT[:]: eng.tensor_copy(o, i),
                           waits=[f_p, ho_split_busy.get(mp)], inc="vec")
                    t_hsp = b.emit("vector",
                                   lambda eng, o=hol[mp][:], i0=hoT[:], i1=hoh[mp][:]:
                                   eng.tensor_tensor(o, i0, i1, ALU.subtract),
                                   waits=[], inc="vec")
                    ho_busy[0] = ("vec", t_hsp)
                    ztoks = []
                    for n in range(4):
                        pp = n % 2
                        tok = None
                        for k in range(NKT):
                            waits = [wo_tok[k]]
                            if k == 0:
                                waits += ps_wait(pp) + [("vec", t_hsp)]
                            lh = hoh[mp][:, k * 128:(k + 1) * 128]
                            ll = hol[mp][:, k * 128:(k + 1) * 128]
                            rh = woh_sb[:, k * S + n * 512: k * S + n * 512 + 512]
                            rl = wol_sb[:, k * S + n * 512: k * S + n * 512 + 512]
                            b.emit("tensor",
                                   lambda eng, o=ps[pp][:, 0:512], l=lh, r=rh, st_=(k == 0):
                                   eng.matmul(o, l, r, start=st_, stop=False),
                                   waits=waits)
                            b.emit("tensor",
                                   lambda eng, o=ps[pp][:, 0:512], l=lh, r=rl:
                                   eng.matmul(o, l, r, start=False, stop=False))
                            tok = b.emit("tensor",
                                         lambda eng, o=ps[pp][:, 0:512], l=ll, r=rh,
                                         sp_=(k == NKT - 1):
                                         eng.matmul(o, l, r, start=False, stop=sp_),
                                         inc="pe" if k == NKT - 1 else None)
                        tz = b.emit("vector",
                                    lambda eng, o=ztile[:, n * 512:(n + 1) * 512], a=ps[pp][:, 0:512],
                                    x=xres[0][:, n * 512:(n + 1) * 512]:
                                    eng.tensor_add(o, a, x),
                                    waits=[("pe", tok), f_p], inc="vec")
                        bank_free[pp] = ("vec", tz)
                        ztoks.append(("vec", tz))
                    ho_split_busy[mp] = ("pe", b.cnt["pe"])
                    x_busy[0] = ztoks[-1]
                    l1 = b.emit("vector",
                                lambda eng: eng.reduce_sum(cols[:, 0:1], ztile[:], axis=AXX),
                                waits=ztoks, inc="vec")
                    l2 = b.emit("vector",
                                lambda eng: eng.tensor_scalar(cols[:, 0:1], cols[:, 0:1], 1.0 / D, None, ALU.mult),
                                waits=[("vec", l1)], inc="vec")
                    l3 = b.emit("vector",
                                lambda eng: eng.tensor_scalar(zd[:], ztile[:], cols[:, 0:1], None, ALU.subtract),
                                waits=[("vec", l2)], inc="vec")
                    l4 = b.emit("scalar",
                                lambda eng: eng.activation(junk[:], zd[:], AF.Square, accum_out=cols[:, 1:2]),
                                waits=[("vec", l3)], inc="sca")
                    l5 = b.emit("vector",
                                lambda eng: eng.tensor_scalar(cols[:, 1:2], cols[:, 1:2], 1.0 / D, EPS_LN,
                                                              ALU.mult, ALU.add),
                                waits=[("sca", l4)], inc="vec")
                    l6 = b.emit("scalar",
                                lambda eng: eng.activation(cols[:, 2:3], cols[:, 1:2], AF.Ln),
                                waits=[("vec", l5)], inc="sca")
                    l7 = b.emit("scalar",
                                lambda eng: eng.activation(cols[:, 2:3], cols[:, 2:3], AF.Exp, scale=-0.5),
                                waits=[("sca", l6)], inc="sca")
                    l8 = b.emit("vector",
                                lambda eng: eng.tensor_scalar(lnt[:], zd[:], cols[:, 2:3], None, ALU.mult),
                                waits=[("sca", l7), b.fence("dsn"), tr_busy], inc="vec")
                    b.emit("gpsimd", _dma(ln1n_d[m * 128:(m + 1) * 128, :], lnt[:]),
                           waits=[("vec", l8)], inc="dsn", n=16)
                    trtoks = []
                    for fb in range(NKT):
                        bank = 6 if fb % 2 == 0 else 7
                        tok = b.emit("tensor",
                                     lambda eng, o=ps[bank][:, 0:128], i=lnt[:, fb * 128:(fb + 1) * 128]:
                                     eng.transpose(o, i, ident[:]),
                                     waits=ps_wait(bank) + [("vec", l8), tr_busy, lnTt_busy], inc="pe")
                        te = b.emit("scalar",
                                    lambda eng, o=lnTt[:, fb * 128:(fb + 1) * 128], i=ps[bank][:, 0:128]:
                                    eng.copy(o, i),
                                    waits=[("pe", tok)], inc="sca")
                        bank_free[bank] = ("sca", te)
                        trtoks.append(("sca", te))
                    tr_busy = ("pe", b.cnt["pe"])
                    b.emit("vector",
                           lambda eng, o=lnhTt[:], i=lnTt[:]: eng.tensor_copy(o, i),
                           waits=trtoks + [b.fence("dst")], inc="vec")
                    t_lsp = b.emit("vector",
                                   lambda eng, o=lnlTt[:], i0=lnTt[:], i1=lnhTt[:]:
                                   eng.tensor_tensor(o, i0, i1, ALU.subtract),
                                   waits=[], inc="vec")
                    lnTt_busy = ("vec", t_lsp)
                    b.emit("gpsimd",
                           _dma(lnh_d.rearrange("fb p t -> p fb t")[:, :, m * 128:(m + 1) * 128],
                                lnhTt[:].rearrange("p (fb t) -> p fb t", fb=16)),
                           waits=[("vec", t_lsp)], inc="dst", n=16)
                    b.emit("gpsimd",
                           _dma(lnl_d.rearrange("fb p t -> p fb t")[:, :, m * 128:(m + 1) * 128],
                                lnlTt[:].rearrange("p (fb t) -> p fb t", fb=16)),
                           waits=[("vec", t_lsp)], inc="dst", n=16)
                f_ln = b.fence("dst")
                f_lnn = b.fence("dsn")
                b.flush(blk, sems, lastw)

            b.barrier()

            with contextlib.ExitStack() as st:
                lnh_sb = st.enter_context(nc.sbuf_tensor("lnh_sb", [128, NKT * S], BF16))
                lnl_sb = st.enter_context(nc.sbuf_tensor("lnl_sb", [128, NKT * S], BF16))
                uw = [st.enter_context(nc.sbuf_tensor(f"uw{i}", [128, D], F32)) for i in range(2)]
                usg = [st.enter_context(nc.sbuf_tensor(f"usg{i}", [128, D], BF16)) for i in range(2)]
                hidf = [st.enter_context(nc.sbuf_tensor("hidf0", [128, S], F32))]
                hidh = [st.enter_context(nc.sbuf_tensor(f"hidh{i}", [128, S], BF16)) for i in range(2)]
                hidl = [st.enter_context(nc.sbuf_tensor(f"hidl{i}", [128, S], BF16)) for i in range(2)]
                upsc_sb = st.enter_context(nc.sbuf_tensor("upsc_sb", [128, 64], F32))

                b.emit("sync",
                       _dma_nc(upsc_sb[:].rearrange("p (ht x) -> p ht x", x=1),
                               upsc_d.rearrange("(ht p) x -> p ht x", p=128)),
                       inc="dus", n=16)
                f_us = b.fence("dus")
                ln_tok = {}
                for k in range(NKT):
                    b.emit("sync", _dma(lnh_sb[:, k * S:(k + 1) * S], lnh_d[k]),
                           waits=[f_ln], inc="dlt", n=16)
                    b.emit("sync", _dma(lnl_sb[:, k * S:(k + 1) * S], lnl_d[k]),
                           waits=[f_ln], inc="dlt", n=16)
                    ln_tok[k] = b.fence("dlt")
                f_lt = ln_tok[NKT - 1]

                uw_busy = {}
                usg_busy = {}
                hidf_busy = {}
                for ht in range(64):
                    wp = ht % 2
                    dup = "du0" if wp == 0 else "du1"
                    b.emit("sync", _dma(uw[wp][:], up_t[ht]),
                           waits=[uw_busy.get(wp)], inc=dup, n=16)
                    t_sg2 = b.emit("scalar",
                                   lambda eng, o=usg[wp][:], i=uw[wp][:]: eng.activation(o, i, AF.Sign),
                                   waits=[b.fence(dup), usg_busy.get(wp)], inc="sca")
                    uw_busy[wp] = ("sca", t_sg2)
                    etoks = []
                    for n in range(4):
                        pp = n % 2
                        tok = None
                        for k in range(NKT):
                            waits = [ln_tok[k]]
                            if k == 0:
                                waits += ps_wait(pp) + [("sca", t_sg2), b.fence("dsd0" if wp == 0 else "dsd1")]
                            lg = usg[wp][:, k * 128:(k + 1) * 128]
                            rh = lnh_sb[:, k * S + n * 512: k * S + n * 512 + 512]
                            rl = lnl_sb[:, k * S + n * 512: k * S + n * 512 + 512]
                            b.emit("tensor",
                                   lambda eng, o=ps[pp][:, 0:512], l=lg, r=rh, st_=(k == 0):
                                   eng.matmul(o, l, r, start=st_, stop=False),
                                   waits=waits)
                            tok = b.emit("tensor",
                                         lambda eng, o=ps[pp][:, 0:512], l=lg, r=rl,
                                         sp_=(k == NKT - 1):
                                         eng.matmul(o, l, r, start=False, stop=sp_),
                                         inc="pe" if k == NKT - 1 else None)
                        nsl = slice(n * 512, (n + 1) * 512)
                        te = b.emit("scalar",
                                    lambda eng, o=hidf[0][:, nsl], i=ps[pp][:, 0:512],
                                    sc=upsc_sb[:, ht:ht + 1]:
                                    eng.activation(o, i, AF.Relu, scale=sc),
                                    waits=[("pe", tok), f_us, hidf_busy.get(0)], inc="sca")
                        bank_free[pp] = ("sca", te)
                        b.emit("vector",
                               lambda eng, o=hidh[wp][:, nsl], i=hidf[0][:, nsl]:
                               eng.tensor_copy(o, i),
                               waits=[("sca", te)], inc="vec")
                        tv3 = b.emit("vector",
                                     lambda eng, o=hidl[wp][:, nsl], i0=hidf[0][:, nsl],
                                     i1=hidh[wp][:, nsl]:
                                     eng.tensor_tensor(o, i0, i1, ALU.subtract),
                                     waits=[], inc="vec")
                        etoks.append(("vec", tv3))
                    usg_busy[wp] = ("pe", b.cnt["pe"])
                    hidf_busy[0] = etoks[-1]
                    dsp = "dsd0" if wp == 0 else "dsd1"
                    b.emit("gpsimd", _dma(hidT_h_d[ht * 128:(ht + 1) * 128, :], hidh[wp][:]),
                           waits=etoks, inc=dsp, n=16)
                    b.emit("gpsimd", _dma(hidT_l_d[ht * 128:(ht + 1) * 128, :], hidl[wp][:]),
                           waits=etoks, inc=dsp, n=16)
                f_hid = None
                f_hid0 = b.fence("dsd0")
                f_hid1 = b.fence("dsd1")
                b.flush(blk, sems, lastw)

            b.barrier()

            with contextlib.ExitStack() as st:
                hbh_sb = st.enter_context(nc.sbuf_tensor("hbh_sb", [128, 64 * 512], BF16))
                hbl_sb = st.enter_context(nc.sbuf_tensor("hbl_sb", [128, 64 * 512], BF16))
                dsg = [st.enter_context(nc.sbuf_tensor(f"dsg{i}", [128, 4096], BF16)) for i in range(4)]
                oT = [st.enter_context(nc.sbuf_tensor(f"oT{i}", [128, 512], F32)) for i in range(2)]
                dnsc_sb = st.enter_context(nc.sbuf_tensor("dnsc_sb", [128, 16], F32))
                dnsc2 = st.enter_context(nc.sbuf_tensor("dnsc2_sb", [128, 16], F32))
                osc_sb = st.enter_context(nc.sbuf_tensor("osc_sb", [1, 1], F32))
                oscc = st.enter_context(nc.sbuf_tensor("oscc_sb", [128, 1], F32))
                ob = [st.enter_context(nc.sbuf_tensor("ob0", [128, 2048], F32))]
                lb = [st.enter_context(nc.sbuf_tensor("lb0", [128, 2048], F32))]
                ftile = st.enter_context(nc.sbuf_tensor("ftile_sb", [128, 2048], F32))
                z2 = st.enter_context(nc.sbuf_tensor("z2_sb", [128, 2048], F32))
                z2d = z2
                otile = st.enter_context(nc.sbuf_tensor("otile_sb", [128, 2048], F32))
                cols2 = st.enter_context(nc.sbuf_tensor("cols2_sb", [128, 4], F32))
                junk = otile
                ob_busy = {}
                lb_busy = {}
                f_busy = None

                b.emit("sync",
                       _dma_nc(dnsc_sb[:].rearrange("p (ft x) -> p ft x", x=1),
                               dnsc_d.rearrange("(ft p) x -> p ft x", p=128)),
                       inc="dds", n=16)
                b.emit("sync", _dma(osc_sb[:], osc_d), inc="dds", n=16)
                f_cst5 = b.fence("dds")
                tok = b.emit("tensor",
                             lambda eng: eng.matmul(ps[6][:, 0:1], ones_sb[0:1, 0:128], osc_sb[:],
                                                    start=True, stop=True),
                             waits=ps_wait(6) + [f_cst5], inc="pe")
                t_ob = b.emit("scalar", lambda eng: eng.copy(oscc[:], ps[6][:, 0:1]),
                              waits=[("pe", tok)], inc="sca")
                bank_free[6] = ("sca", t_ob)
                t_d2 = b.emit("vector",
                              lambda eng: eng.tensor_scalar(dnsc2[:], dnsc_sb[:], oscc[:], None, ALU.mult),
                              waits=[f_cst5, ("sca", t_ob)], inc="vec")

                hb_busy = None
                dw_busy = {}
                dsg_busy = {}
                for tc in range(4):
                    hb_tok = {}
                    for kr in range(8):
                        b.emit("sync",
                               _dma(hbh_sb[:, kr * 8 * 512:(kr + 1) * 8 * 512]
                                    .rearrange("p (k t) -> p k t", k=8),
                                    hidT_h_d[kr * 1024:(kr + 1) * 1024, tc * 512:(tc + 1) * 512]
                                    .rearrange("(k p) t -> p k t", p=128)),
                               waits=[f_hid0, f_hid1, hb_busy], inc="dhb", n=16)
                        b.emit("sync",
                               _dma(hbl_sb[:, kr * 8 * 512:(kr + 1) * 8 * 512]
                                    .rearrange("p (k t) -> p k t", k=8),
                                    hidT_l_d[kr * 1024:(kr + 1) * 1024, tc * 512:(tc + 1) * 512]
                                    .rearrange("(k p) t -> p k t", p=128)),
                               waits=[f_hid0, f_hid1, hb_busy], inc="dhb", n=16)
                        hb_tok[kr] = b.fence("dhb")
                    for ft in range(16):
                        pp = ft % 2
                        fb4 = (ft % 2) * 2
                        half_toks = []
                        for hh in range(2):
                            ddh = f"dd{fb4 + hh}"
                            b.emit("sync",
                                   _dma(dsg[fb4 + hh][:], dnsg_d[ft, :, hh * 4096:(hh + 1) * 4096]),
                                   waits=[dsg_busy.get(fb4 + hh), f_dnsg], inc=ddh, n=16)
                            half_toks.append(b.fence(ddh))
                        tok = None
                        for k in range(64):
                            hh = k // 32
                            waits = [hb_tok[k // 8]] if k % 8 == 0 else []
                            if k == 0:
                                waits += ps_wait(pp) + [half_toks[0]]
                            if k == 32:
                                waits += [half_toks[1]]
                            lg = dsg[fb4 + hh][:, (k % 32) * 128:(k % 32 + 1) * 128]
                            b.emit("tensor",
                                   lambda eng, o=ps[pp][:, 0:512], l=lg,
                                   r=hbh_sb[:, k * 512:(k + 1) * 512], st_=(k == 0):
                                   eng.matmul(o, l, r, start=st_, stop=False),
                                   waits=waits)
                            tok = b.emit("tensor",
                                         lambda eng, o=ps[pp][:, 0:512], l=lg,
                                         r=hbl_sb[:, k * 512:(k + 1) * 512], sp_=(k == 63):
                                         eng.matmul(o, l, r, start=False, stop=sp_),
                                         inc="pe" if k == 63 else None)
                        dsg_busy[fb4] = ("pe", tok)
                        dsg_busy[fb4 + 1] = ("pe", tok)
                        op = ft % 2
                        if op == 0:
                            te = b.emit("scalar",
                                        lambda eng, o=oT[op][:], i=ps[pp][:, 0:512], sc=dnsc2[:, ft:ft + 1]:
                                        eng.activation(o, i, AF.Copy, scale=sc),
                                        waits=[("pe", tok), ("vec", t_d2), b.fence("dso0")], inc="sca")
                            bank_free[pp] = ("sca", te)
                        else:
                            te = b.emit("vector",
                                        lambda eng, o=oT[op][:], i=ps[pp][:, 0:512], sc=dnsc2[:, ft:ft + 1]:
                                        eng.tensor_scalar(o, i, sc, None, ALU.mult),
                                        waits=[("pe", tok), ("vec", t_d2), b.fence("dso1")], inc="vec")
                            bank_free[pp] = ("vec", te)
                        b.emit("gpsimd",
                               _dma(oT_d[ft * 128:(ft + 1) * 128, tc * 512:(tc + 1) * 512], oT[op][:]),
                               waits=[bank_free[pp]], inc="dso0" if op == 0 else "dso1", n=16)
                    hb_busy = ("pe", b.cnt["pe"])
                    f_tc0 = b.fence("dso0")
                    f_tc1 = b.fence("dso1")
                    for m in range(4 * tc, 4 * tc + 4):
                        mp = m % 2
                        dop = "do0" if mp == 0 else "do1"
                        b.emit("sync",
                               _dma(ob[0][:].rearrange("p (fb t) -> p fb t", fb=16),
                                    oT_d[:, m * 128:(m + 1) * 128]
                                    .rearrange("(fb p) t -> p fb t", p=128)),
                               waits=[f_tc0, f_tc1, ob_busy.get(0)], inc=dop, n=16)
                        b.emit("sync", _dma(lb[0][:], ln1n_d[m * 128:(m + 1) * 128, :]),
                               waits=[f_lnn, lb_busy.get(0)], inc=dop, n=16)
                        f_p6 = b.fence(dop)
                        trtoks = []
                        for fb in range(NKT):
                            bank = 6 if fb % 2 == 0 else 7
                            tok = b.emit("tensor",
                                         lambda eng, o=ps[bank][:, 0:128], i=ob[0][:, fb * 128:(fb + 1) * 128]:
                                         eng.transpose(o, i, ident[:]),
                                         waits=ps_wait(bank) + [f_p6, f_busy], inc="pe")
                            te = b.emit("scalar",
                                        lambda eng, o=ftile[:, fb * 128:(fb + 1) * 128], i=ps[bank][:, 0:128]:
                                        eng.copy(o, i),
                                        waits=[("pe", tok)], inc="sca")
                            bank_free[bank] = ("sca", te)
                            trtoks.append(("sca", te))
                        ob_busy[0] = ("pe", b.cnt["pe"])
                        tz = b.emit("vector",
                                    lambda eng, l=lb[0][:]: eng.tensor_add(z2[:], ftile[:], l),
                                    waits=trtoks + [f_p6], inc="vec")
                        f_busy = ("vec", tz)
                        lb_busy[0] = ("vec", tz)
                        l1 = b.emit("vector",
                                    lambda eng: eng.reduce_sum(cols2[:, 0:1], z2[:], axis=AXX),
                                    waits=[("vec", tz)], inc="vec")
                        l2 = b.emit("vector",
                                    lambda eng: eng.tensor_scalar(cols2[:, 0:1], cols2[:, 0:1], 1.0 / D, None, ALU.mult),
                                    waits=[("vec", l1)], inc="vec")
                        l3 = b.emit("vector",
                                    lambda eng: eng.tensor_scalar(z2d[:], z2[:], cols2[:, 0:1], None, ALU.subtract),
                                    waits=[("vec", l2)], inc="vec")
                        l4 = b.emit("scalar",
                                    lambda eng: eng.activation(junk[:], z2d[:], AF.Square, accum_out=cols2[:, 1:2]),
                                    waits=[("vec", l3), b.fence("dou")], inc="sca")
                        l5 = b.emit("vector",
                                    lambda eng: eng.tensor_scalar(cols2[:, 1:2], cols2[:, 1:2], 1.0 / D, EPS_LN,
                                                                  ALU.mult, ALU.add),
                                    waits=[("sca", l4)], inc="vec")
                        l6 = b.emit("scalar",
                                    lambda eng: eng.activation(cols2[:, 2:3], cols2[:, 1:2], AF.Ln),
                                    waits=[("vec", l5)], inc="sca")
                        l7 = b.emit("scalar",
                                    lambda eng: eng.activation(cols2[:, 2:3], cols2[:, 2:3], AF.Exp, scale=-0.5),
                                    waits=[("sca", l6)], inc="sca")
                        l8 = b.emit("vector",
                                    lambda eng: eng.tensor_scalar(otile[:], z2d[:], cols2[:, 2:3], None, ALU.mult),
                                    waits=[("sca", l7), b.fence("dou")], inc="vec")
                        b.emit("gpsimd", _dma(out_d[m * 128:(m + 1) * 128, :], otile[:]),
                               waits=[("vec", l8)], inc="dou", n=16)

                b.emit("gpsimd", None, waits=[b.fence("dou")])
                b.flush(blk, sems, lastw)

    return nc


def _host_layouts(inputs):
    f = np.float32

    def tile_lhsT(w):
        o, i = w.shape
        return np.ascontiguousarray(
            w.reshape(o // 128, 128, i // 128, 128).transpose(0, 3, 2, 1)
        ).reshape(o // 128, 128, i)

    wig = np.asarray(inputs["L0_attn_Wig"], f)
    wfg = np.asarray(inputs["L0_attn_Wfg"], f)
    wifT = np.concatenate([wig, wfg], axis=0).T
    wif_tiled = np.ascontiguousarray(
        wifT.reshape(16, 128, 16).transpose(1, 0, 2)).reshape(128, 256)

    j = np.arange(128)
    cmask = np.where(j[:, None] > j[None, :], np.float32(NEGBIG), np.float32(0.0)).astype(f)

    return {
        "wq_t": tile_lhsT(np.asarray(inputs["L0_attn_Wq"], f)),
        "wk_t": tile_lhsT(np.asarray(inputs["L0_attn_Wk"], f)),
        "wv_t": tile_lhsT(np.asarray(inputs["L0_attn_Wv"], f)),
        "wog_t": tile_lhsT(np.asarray(inputs["L0_attn_Wog"], f)),
        "wif_t": wif_tiled,
        "bif": np.concatenate([np.asarray(inputs["L0_attn_big"], f),
                               np.asarray(inputs["L0_attn_bfg"], f)]).reshape(16, 1),
        "gam": np.ascontiguousarray(np.asarray(inputs["L0_attn_gamma"], f).reshape(D, 1)),
        "woutT": np.ascontiguousarray(np.asarray(inputs["L0_attn_Wout"], f).T),
        "up_t": tile_lhsT(np.asarray(inputs["L0_ffn_up_weight"], f)),
        "dn_t": tile_lhsT(np.asarray(inputs["L0_ffn_down_weight"], f)),
        "upsc": np.ascontiguousarray(np.asarray(inputs["L0_ffn_up_scale"], f).reshape(DH, 1)),
        "dnsc": np.ascontiguousarray(np.asarray(inputs["L0_ffn_down_scale"], f).reshape(D, 1)),
        "osc": np.asarray(inputs["L0_ffn_output_scale"], f).reshape(1, 1),
        "ident": np.eye(128, dtype=f),
        "cmask": np.ascontiguousarray(cmask),
        "ones_r": np.ones((1, 1024), f),
    }


def kernel(**inputs):
    x = np.asarray(inputs["L0_attn_x"], np.float32)
    shared = _host_layouts(inputs)
    nc = build_nc(debug=_DEBUG)
    in_maps = []
    for bi in range(8):
        m = dict(shared)
        m["xT"] = np.ascontiguousarray(x[bi].T)
        m["xn"] = np.ascontiguousarray(x[bi])
        in_maps.append(m)
    res = run_bass_kernel_spmd(nc, in_maps, core_ids=list(range(8)))
    return np.stack([np.asarray(r["out"]) for r in res.results], axis=0)


# version 18
# speedup vs baseline: 1.0287x; 1.0287x over previous
import contextlib
import numpy as np
import concourse.bass as bass
import concourse.mybir as mybir
from concourse.bass_utils import run_bass_kernel_spmd

F32 = mybir.dt.float32
BF16 = mybir.dt.bfloat16
AF = mybir.ActivationFunctionType
ALU = mybir.AluOpType
AXX = mybir.AxisListType.X

D = 2048
S = 2048
NH = 8
DQ = 128
DV = 256
NCH = 16
DH = 8192
NKT = 16
NTT = 16
QSCALE = float(DQ) ** -0.5
EPS_MH = 1e-6
EPS_LN = 1e-5
NEGBIG = -1.0e30

_DEBUG = False

_ENGINES = ("sync", "tensor", "vector", "scalar", "gpsimd")


DMA_SEMS = ("cst", "dmx", "dw0", "dw1", "ds1", "dsg", "dgt", "dmp", "ddc",
            "dgr", "dus", "dds", "dwx", "dsw", "dnl", "dsq",
            "dc0", "dc1", "dsh0", "dsh1", "dwo", "dp0", "dp1", "dsn", "dst",
            "dlt", "du0", "du1", "dsd0", "dsd1", "dhb", "dd0", "dd1", "dd2", "dd3", "dso0", "dso1",
            "do0", "do1", "dou")
ALL_SEMS = DMA_SEMS + ("pe", "vec", "sca", "gps", "bar")


class _Build:

    def __init__(self):
        self.ops = {e: [] for e in _ENGINES}
        self.cnt = {s: 0 for s in ALL_SEMS}

    def fence(self, sem):
        v = self.cnt[sem]
        return (sem, v) if v else None

    def emit(self, eng, fn=None, waits=(), inc=None, n=1):
        tok = None
        if inc is not None:
            self.cnt[inc] += n
            tok = self.cnt[inc]
        ws = []
        for w in waits:
            if w is None:
                continue
            s, v = w
            if v is None:
                continue
            ws.append((s, v))
        self.ops[eng].append((fn, tuple(ws), inc, n))
        return tok

    def barrier(self):
        drains = [self.fence(sm) for sm in DMA_SEMS]
        own = {"tensor": "pe", "vector": "vec", "scalar": "sca", "gpsimd": "gps"}
        tgt = self.cnt["bar"] + len(_ENGINES)
        for e in _ENGINES:
            w = list(drains)
            es = own.get(e)
            if es and self.cnt[es]:
                w.append((es, self.cnt[es]))
            self.emit(e, lambda eng: eng.nop(), waits=w, inc="bar")
        for e in _ENGINES:
            self.emit(e, None, waits=[("bar", tgt)])

    def flush(self, blk, handles, lastw):
        for ename in _ENGINES:
            ops = self.ops[ename]
            if not ops:
                continue

            def body(eng, ops=ops, ename=ename):
                lw = lastw[ename]
                for fn, waits, inc, n in ops:
                    for s, v in waits:
                        if lw.get(s, 0) < v:
                            eng.wait_ge(handles[s], v)
                            lw[s] = v
                    if fn is None:
                        continue
                    ins = fn(eng)
                    if inc is not None:
                        ins.then_inc(handles[inc], n)

            getattr(blk, ename)(body)
            self.ops[ename] = []


def _dma(dst, src):
    return lambda eng, d=dst, s=src: eng.dma_start(out=d, in_=s)


def _dma_nc(dst, src):
    return lambda eng, d=dst, s=src: eng.dma_start(out=d, in_=s, allow_slow_non_contiguous=True)


def build_nc(debug=False):
    nc = bass.Bass("TRN2", target_bir_lowering=False)

    xT_d = nc.dram_tensor("xT", [D, S], F32, kind="ExternalInput").ap()
    xn_d = nc.dram_tensor("xn", [S, D], F32, kind="ExternalInput").ap()
    wq_t = nc.dram_tensor("wq_t", [8, 128, D], F32, kind="ExternalInput").ap()
    wk_t = nc.dram_tensor("wk_t", [8, 128, D], F32, kind="ExternalInput").ap()
    wv_t = nc.dram_tensor("wv_t", [16, 128, D], F32, kind="ExternalInput").ap()
    wog_t = nc.dram_tensor("wog_t", [16, 128, D], F32, kind="ExternalInput").ap()
    wif_t = nc.dram_tensor("wif_t", [128, 256], F32, kind="ExternalInput").ap()
    bif_d = nc.dram_tensor("bif", [16, 1], F32, kind="ExternalInput").ap()
    gam_d = nc.dram_tensor("gam", [D, 1], F32, kind="ExternalInput").ap()
    woutT_d = nc.dram_tensor("woutT", [D, D], F32, kind="ExternalInput").ap()
    up_t = nc.dram_tensor("up_t", [64, 128, D], F32, kind="ExternalInput").ap()
    dn_t = nc.dram_tensor("dn_t", [16, 128, DH], F32, kind="ExternalInput").ap()
    upsc_d = nc.dram_tensor("upsc", [DH, 1], F32, kind="ExternalInput").ap()
    dnsc_d = nc.dram_tensor("dnsc", [D, 1], F32, kind="ExternalInput").ap()
    osc_d = nc.dram_tensor("osc", [1, 1], F32, kind="ExternalInput").ap()
    ident_d = nc.dram_tensor("ident", [128, 128], F32, kind="ExternalInput").ap()
    cmask_d = nc.dram_tensor("cmask", [128, 128], F32, kind="ExternalInput").ap()
    ones_d = nc.dram_tensor("ones_r", [1, 1024], F32, kind="ExternalInput").ap()

    spill_kind = "ExternalOutput" if debug else "Internal"
    qT_d = nc.dram_tensor("qT_s", [NH * DQ, S], F32, kind=spill_kind).ap()
    kT_d = nc.dram_tensor("kT_s", [NH * DQ, S], F32, kind=spill_kind).ap()
    vT_d = nc.dram_tensor("vT_s", [NH * DV, S], F32, kind=spill_kind).ap()
    sgT_d = nc.dram_tensor("sgT_s", [D, S], F32, kind=spill_kind).ap()
    hoT_d = nc.dram_tensor("hoT_s", [D, S], F32, kind=spill_kind).ap()
    ln1T_d = nc.dram_tensor("ln1T_s", [D, S], F32, kind=spill_kind).ap()
    lnh_d = nc.dram_tensor("lnh_s", [NKT, 128, S], BF16, kind="Internal").ap()
    lnl_d = nc.dram_tensor("lnl_s", [NKT, 128, S], BF16, kind="Internal").ap()
    woh_d = nc.dram_tensor("woh_s", [NKT, 128, S], BF16, kind="Internal").ap()
    wol_d = nc.dram_tensor("wol_s", [NKT, 128, S], BF16, kind="Internal").ap()
    dnsg_d = nc.dram_tensor("dnsg_s", [16, 128, DH], BF16, kind="Internal").ap()
    ln1n_d = nc.dram_tensor("ln1n_s", [S, D], F32, kind=spill_kind).ap()
    hidT_h_d = nc.dram_tensor("hidT_h_s", [DH, S], BF16, kind="Internal").ap()
    hidT_l_d = nc.dram_tensor("hidT_l_s", [DH, S], BF16, kind="Internal").ap()
    oT_d = nc.dram_tensor("oT_s", [D, S], F32, kind=spill_kind).ap()
    gate_d = nc.dram_tensor("gate_s", [3, 8, S], F32, kind=spill_kind).ap()
    gch_d = nc.dram_tensor("gch_s", [128, 128], F32, kind=spill_kind).ap()
    negM_d = nc.dram_tensor("negM_s", [128, 128], F32, kind=spill_kind).ap()
    out_d = nc.dram_tensor("out", [S, D], F32, kind="ExternalOutput").ap()

    b = _Build()
    bank_free = {}

    def ps_wait(bank):
        return [bank_free[bank]] if bank in bank_free else []

    with contextlib.ExitStack() as top:
        sems = {}
        for s in ALL_SEMS:
            sems[s] = top.enter_context(nc.semaphore(s))
        ps = [top.enter_context(nc.psum_tensor(f"ps{i}", [128, 512], F32)) for i in range(8)]
        ident = top.enter_context(nc.sbuf_tensor("ident_sb", [128, 128], F32))
        cmask = top.enter_context(nc.sbuf_tensor("cmask_sb", [128, 128], F32))
        ones_sb = top.enter_context(nc.sbuf_tensor("ones_sb", [1, 128], F32))
        lastw = {e: {} for e in _ENGINES}

        with nc.Block() as blk:
            b.emit("sync", _dma(ident[:], ident_d), inc="cst", n=16)
            b.emit("sync", _dma(cmask[:], cmask_d), inc="cst", n=16)
            b.emit("sync", _dma(ones_sb[:], ones_d[0:1, 0:128]), inc="cst", n=16)

            g_st = contextlib.ExitStack()
            wT = g_st.enter_context(nc.sbuf_tensor("wT_sb", [128, 128], F32))
            EscT = g_st.enter_context(nc.sbuf_tensor("EscT_sb", [128, 128], F32))
            enmT = g_st.enter_context(nc.sbuf_tensor("enmT_sb", [128, 128], F32))
            decb = g_st.enter_context(nc.sbuf_tensor("decb_sb", [128, 128], F32))
            gam_sb = g_st.enter_context(nc.sbuf_tensor("gam_sb", [128, 16], F32))

            with contextlib.ExitStack() as st:
                mglob = st.enter_context(nc.sbuf_tensor("mglob_sb", [8, S], F32))
                flc = st.enter_context(nc.sbuf_tensor("flc_sb", [128, 128], F32))
                ic = st.enter_context(nc.sbuf_tensor("ic_sb", [128, 128], F32))
                mcc = st.enter_context(nc.sbuf_tensor("mc_sb", [128, 128], F32))
                beta = st.enter_context(nc.sbuf_tensor("beta_sb", [128, 128], F32))
                gch = st.enter_context(nc.sbuf_tensor("gch_sb", [128, 128], F32))
                Mch = st.enter_context(nc.sbuf_tensor("Mch_sb", [128, 128], F32))
                negM = st.enter_context(nc.sbuf_tensor("negM_sb", [128, 128], F32))
                negMT = st.enter_context(nc.sbuf_tensor("negMT_sb", [128, 1], F32))
                mprev = st.enter_context(nc.sbuf_tensor("mprev_sb", [128, 1], F32))
                wst = st.enter_context(nc.sbuf_tensor("wst_sb", [128, 128], F32))
                Ee = st.enter_context(nc.sbuf_tensor("Ee_sb", [128, 128], F32))
                enm = st.enter_context(nc.sbuf_tensor("enm_sb", [128, 128], F32))
                ET = st.enter_context(nc.sbuf_tensor("ET_sb", [128, 128], F32))
                decrow = st.enter_context(nc.sbuf_tensor("decrow_sb", [1, 128], F32))
                xh_sb = st.enter_context(nc.sbuf_tensor("xh_sb", [128, NKT * S], BF16))
                xl_sb = st.enter_context(nc.sbuf_tensor("xl_sb", [128, NKT * S], BF16))
                xstage = [st.enter_context(nc.sbuf_tensor("xstage0", [128, S], F32))]
                wbuf = [st.enter_context(nc.sbuf_tensor("wbuf0", [128, D], F32))]
                wh = [st.enter_context(nc.sbuf_tensor(f"wh{i}", [128, D], BF16)) for i in range(2)]
                wl = [st.enter_context(nc.sbuf_tensor(f"wl{i}", [128, D], BF16)) for i in range(2)]
                strip = [st.enter_context(nc.sbuf_tensor("strip0", [128, S], F32))]
                wif_sb = st.enter_context(nc.sbuf_tensor("wif_sb", [128, 256], F32))
                wifh = st.enter_context(nc.sbuf_tensor("wifh_sb", [128, 256], BF16))
                wifl = st.enter_context(nc.sbuf_tensor("wifl_sb", [128, 256], BF16))
                bi_sb = st.enter_context(nc.sbuf_tensor("bi_sb", [8, 1], F32))
                bf_sb = st.enter_context(nc.sbuf_tensor("bf_sb", [8, 1], F32))
                bi15 = st.enter_context(nc.sbuf_tensor("bi15_sb", [8, 1], F32))
                bf15 = st.enter_context(nc.sbuf_tensor("bf15_sb", [8, 1], F32))
                isc = st.enter_context(nc.sbuf_tensor("isc_sb", [8, S], F32))
                fsc = st.enter_context(nc.sbuf_tensor("fsc_sb", [8, S], F32))

                xsp = {}
                xsp_tok = {}
                for k in range(NKT):
                    kp = 0
                    b.emit("sync",
                           _dma(xstage[kp][:], xT_d[k * 128:(k + 1) * 128, :]),
                           waits=[xsp.get(kp)], inc="dmx", n=16)
                    f_xs = b.fence("dmx")
                    b.emit("vector",
                           lambda eng, o=xh_sb[:, k * S:(k + 1) * S], i=xstage[kp][:]:
                           eng.tensor_copy(o, i),
                           waits=[f_xs], inc="vec")
                    tk = b.emit("vector",
                                lambda eng, o=xl_sb[:, k * S:(k + 1) * S], i0=xstage[kp][:],
                                i1=xh_sb[:, k * S:(k + 1) * S]:
                                eng.tensor_tensor(o, i0, i1, ALU.subtract),
                                waits=[], inc="vec")
                    xsp[kp] = ("vec", tk)
                    xsp_tok[k] = ("vec", tk)
                f_xT = ("vec", b.cnt["vec"])
                b.emit("sync", _dma(wif_sb[:], wif_t), inc="cst", n=16)
                f_wif = b.fence("cst")
                b.emit("vector", lambda eng: eng.tensor_copy(wifh[:], wif_sb[:]),
                       waits=[f_wif], inc="vec")
                t_wifl = b.emit("vector",
                                lambda eng: eng.tensor_tensor(wifl[:], wif_sb[:], wifh[:], ALU.subtract),
                                waits=[], inc="vec")
                b.emit("sync", _dma(bi_sb[:], bif_d[0:8, :]), inc="cst", n=16)
                b.emit("sync", _dma(bf_sb[:], bif_d[8:16, :]), inc="cst", n=16)
                b.emit("sync",
                       _dma_nc(gam_sb[:].rearrange("d (hh x) -> d hh x", x=1),
                               gam_d.rearrange("(hh d) x -> d hh x", d=128)),
                       inc="cst", n=16)
                f_cst1 = b.fence("cst")
                t_bi15 = b.emit("vector",
                                lambda eng: eng.tensor_scalar(bi15[:], bi_sb[:], 1.0 / 15.0, None, ALU.mult),
                                waits=[f_cst1], inc="vec")
                t_bf15 = b.emit("vector",
                                lambda eng: eng.tensor_scalar(bf15[:], bf_sb[:], 1.0 / 15.0, None, ALU.mult),
                                waits=[f_cst1], inc="vec")

                gtoks = []
                for half, (bb, dstt, btok) in enumerate([(bi15, isc, t_bi15), (bf15, fsc, t_bf15)]):
                    for n in range(4):
                        pp = 2 + half
                        tok = None
                        for k in range(NKT):
                            waits = [xsp_tok[k]]
                            if k == 0:
                                waits += ps_wait(pp) + [("vec", t_wifl)]
                            csl2 = slice(k * 16 + half * 8, k * 16 + half * 8 + 8)
                            rh = xh_sb[:, k * S + n * 512: k * S + n * 512 + 512]
                            rl = xl_sb[:, k * S + n * 512: k * S + n * 512 + 512]
                            b.emit("tensor",
                                   lambda eng, o=ps[pp][0:8, 0:512], l=wifh[:, csl2], r=rh,
                                   st_=(k == 0):
                                   eng.matmul(o, l, r, start=st_, stop=False),
                                   waits=waits)
                            b.emit("tensor",
                                   lambda eng, o=ps[pp][0:8, 0:512], l=wifh[:, csl2], r=rl:
                                   eng.matmul(o, l, r, start=False, stop=False))
                            tok = b.emit("tensor",
                                         lambda eng, o=ps[pp][0:8, 0:512], l=wifl[:, csl2], r=rh,
                                         sp_=(k == NKT - 1):
                                         eng.matmul(o, l, r, start=False, stop=sp_),
                                         inc="pe" if k == NKT - 1 else None)
                        te = b.emit("scalar",
                                    lambda eng, o=dstt[:, n * 512:(n + 1) * 512], i=ps[pp][0:8, 0:512],
                                    bias=bb[:]:
                                    eng.activation(o, i, AF.Tanh, bias=bias, scale=1.0 / 15.0),
                                    waits=[("pe", tok), ("vec", btok)], inc="sca")
                        bank_free[pp] = ("sca", te)
                        gtoks.append(("sca", te))

                t_ii = b.emit("vector",
                              lambda eng: eng.tensor_scalar(isc[:], isc[:], 15.0, None, ALU.mult),
                              waits=gtoks[:4], inc="vec")
                t_sf = b.emit("scalar",
                              lambda eng: eng.activation(fsc[:], fsc[:], AF.Sigmoid, scale=15.0),
                              waits=gtoks[4:], inc="sca")
                t_fl = b.emit("scalar",
                              lambda eng: eng.activation(fsc[:], fsc[:], AF.Ln),
                              waits=[("sca", t_sf)], inc="sca")
                t_m = b.emit("vector",
                             lambda eng: eng.tensor_tensor_scan(mglob[:], fsc[:], isc[:], 0.0, ALU.add, ALU.max),
                             waits=[("sca", t_fl), ("vec", t_ii)], inc="vec")
                b.emit("gpsimd", _dma(gate_d[0], fsc[:]), waits=[("sca", t_fl)], inc="dsg", n=16)
                b.emit("gpsimd", _dma(gate_d[1], isc[:]), waits=[("vec", t_ii)], inc="dsg", n=16)
                b.emit("gpsimd", _dma(gate_d[2], mglob[:]), waits=[("vec", t_m)], inc="dsg", n=16)
                f_dsg = b.fence("dsg")
                b.emit("scalar",
                       _dma(flc[:], gate_d[0].rearrange("h (c t) -> (h c) t", t=128)),
                       waits=[f_dsg], inc="dgt", n=16)
                b.emit("scalar",
                       _dma(ic[:], gate_d[1].rearrange("h (c t) -> (h c) t", t=128)),
                       waits=[f_dsg], inc="dgt", n=16)
                b.emit("scalar",
                       _dma(mcc[:], gate_d[2].rearrange("h (c t) -> (h c) t", t=128)),
                       waits=[f_dsg], inc="dgt", n=16)
                f_gt1 = b.fence("dgt")
                t_beta = b.emit("vector",
                                lambda eng: eng.tensor_tensor_scan(beta[:], flc[:], flc[:], 0.0, ALU.add, ALU.bypass),
                                waits=[f_gt1], inc="vec")
                t_g = b.emit("vector",
                             lambda eng: eng.tensor_sub(gch[:], ic[:], beta[:]),
                             waits=[f_gt1, ("vec", t_beta)], inc="vec")
                t_M = b.emit("vector",
                             lambda eng: eng.tensor_sub(Mch[:], mcc[:], beta[:]),
                             waits=[f_gt1, ("vec", t_beta)], inc="vec")
                t_negM = b.emit("vector",
                                lambda eng: eng.tensor_scalar(negM[:], Mch[:], -1.0, None, ALU.mult),
                                waits=[("vec", t_M)], inc="vec")
                t_negMT = b.emit("vector",
                                 lambda eng: eng.tensor_scalar(negMT[:], Mch[:, 127:128], -1.0, None, ALU.mult),
                                 waits=[("vec", t_M)], inc="vec")
                b.emit("gpsimd", _dma(gch_d, gch[:]), waits=[("vec", t_g)], inc="dsg", n=16)
                b.emit("gpsimd", _dma(negM_d, negM[:]), waits=[("vec", t_negM)], inc="dsg", n=16)
                t_mpz = b.emit("vector", lambda eng: eng.memset(mprev[:], 0.0), inc="vec")
                for h in range(NH):
                    b.emit("scalar",
                           _dma_nc(mprev[h * 16 + 1:h * 16 + 16, 0:1],
                                   mcc[h * 16:h * 16 + 15, 127:128]),
                           waits=[f_gt1, ("vec", t_mpz)], inc="dmp", n=16)
                f_mprev = b.fence("dmp")
                t_wst = b.emit("scalar",
                               lambda eng: eng.activation(wst[:], gch[:], AF.Exp, bias=negMT[:]),
                               waits=[("vec", t_g), ("vec", t_negMT)], inc="sca")
                t_E = b.emit("scalar",
                             lambda eng: eng.activation(Ee[:], Mch[:], AF.Exp, bias=mprev[:], scale=-1.0),
                             waits=[("vec", t_M), f_mprev], inc="sca")
                t_enm = b.emit("scalar",
                               lambda eng: eng.activation(enm[:], mcc[:], AF.Exp, scale=-1.0),
                               waits=[f_gt1], inc="sca")


                projs = [(wq_t, 8, qT_d, "copy"), (wk_t, 8, kT_d, "copy"),
                         (wv_t, 16, vT_d, "copy"), (wog_t, 16, sgT_d, "sigmoid")]
                wbuf_busy = {}
                wsplit_busy = {}
                gidx = 0
                for w_src, n_tiles, dst, mode in projs:
                    for m in range(n_tiles):
                        wp = gidx % 2
                        wb = 0
                        sp = 0
                        gidx += 1
                        b.emit("sync", _dma(wbuf[wb][:], w_src[m]),
                               waits=[wbuf_busy.get(wb)], inc="dw0", n=16)
                        f_w = b.fence("dw0")
                        b.emit("vector",
                               lambda eng, o=wh[wp][:], i=wbuf[wb][:]: eng.tensor_copy(o, i),
                               waits=[f_w, wsplit_busy.get(wp)], inc="vec")
                        t_wsp = b.emit("vector",
                                       lambda eng, o=wl[wp][:], i0=wbuf[wb][:], i1=wh[wp][:]:
                                       eng.tensor_tensor(o, i0, i1, ALU.subtract),
                                       waits=[], inc="vec")
                        wbuf_busy[wb] = ("vec", t_wsp)
                        evac_toks = []
                        for n in range(4):
                            pp = n % 2
                            tok = None
                            for k in range(NKT):
                                waits = [xsp_tok[k]]
                                if k == 0:
                                    waits += ps_wait(pp) + [("vec", t_wsp), b.fence("ds1")]
                                rh = xh_sb[:, k * S + n * 512: k * S + n * 512 + 512]
                                rl = xl_sb[:, k * S + n * 512: k * S + n * 512 + 512]
                                lh = wh[wp][:, k * 128:(k + 1) * 128]
                                ll = wl[wp][:, k * 128:(k + 1) * 128]
                                b.emit("tensor",
                                       lambda eng, o=ps[pp][:, 0:512], l=lh, r=rh, st_=(k == 0):
                                       eng.matmul(o, l, r, start=st_, stop=False),
                                       waits=waits)
                                b.emit("tensor",
                                       lambda eng, o=ps[pp][:, 0:512], l=lh, r=rl:
                                       eng.matmul(o, l, r, start=False, stop=False))
                                tok = b.emit("tensor",
                                             lambda eng, o=ps[pp][:, 0:512], l=ll, r=rh,
                                             sp_=(k == NKT - 1):
                                             eng.matmul(o, l, r, start=False, stop=sp_),
                                             inc="pe" if k == NKT - 1 else None)
                            wsplit_busy[wp] = ("pe", tok)
                            dstr = strip[sp][:, n * 512:(n + 1) * 512]
                            if mode == "sigmoid":
                                te = b.emit("scalar",
                                            lambda eng, o=dstr, i=ps[pp][:, 0:512]:
                                            eng.activation(o, i, AF.Sigmoid),
                                            waits=[("pe", tok)], inc="sca")
                                bank_free[pp] = ("sca", te)
                            elif pp == 0:
                                te = b.emit("scalar",
                                            lambda eng, o=dstr, i=ps[pp][:, 0:512]: eng.copy(o, i),
                                            waits=[("pe", tok)], inc="sca")
                                bank_free[pp] = ("sca", te)
                            else:
                                te = b.emit("vector",
                                            lambda eng, o=dstr, i=ps[pp][:, 0:512]: eng.tensor_copy(o, i),
                                            waits=[("pe", tok)], inc="vec")
                                bank_free[pp] = ("vec", te)
                            evac_toks.append(bank_free[pp])
                        b.emit("gpsimd", _dma(dst[m * 128:(m + 1) * 128, :], strip[sp][:]),
                               waits=evac_toks, inc="ds1", n=16)

                wo_sp_busy = None
                for k in range(NKT):
                    b.emit("sync", _dma(wbuf[0][:], woutT_d[k * 128:(k + 1) * 128, :]),
                           waits=[wbuf_busy.get(0)], inc="dwx", n=16)
                    f_wx = b.fence("dwx")
                    b.emit("vector",
                           lambda eng, o=wh[0][:], i=wbuf[0][:]: eng.tensor_copy(o, i),
                           waits=[f_wx, wo_sp_busy, wsplit_busy.get(0) if k == 0 else None], inc="vec")
                    t_wsp2 = b.emit("vector",
                                    lambda eng, o=wl[0][:], i0=wbuf[0][:], i1=wh[0][:]:
                                    eng.tensor_tensor(o, i0, i1, ALU.subtract),
                                    waits=[], inc="vec")
                    wbuf_busy[0] = ("vec", t_wsp2)
                    b.emit("gpsimd", _dma(woh_d[k], wh[0][:]), waits=[("vec", t_wsp2)], inc="dsw", n=16)
                    b.emit("gpsimd", _dma(wol_d[k], wl[0][:]), waits=[("vec", t_wsp2)], inc="dsw", n=16)
                    wo_sp_busy = b.fence("dsw")
                f_wodram = b.fence("dsw")

                tr_tok = {}
                for idx, (src, dst, wtok) in enumerate([(wst, wT, ("sca", t_wst)),
                                                        (Ee, ET, ("sca", t_E)),
                                                        (enm, enmT, ("sca", t_enm))]):
                    bank = 6 + (idx % 2)
                    tok = b.emit("tensor",
                                 lambda eng, o=ps[bank][:, 0:128], i=src[:]:
                                 eng.transpose(o, i, ident[:]),
                                 waits=ps_wait(bank) + [wtok, f_cst1], inc="pe")
                    te = b.emit("scalar",
                                lambda eng, o=dst[:], i=ps[bank][:, 0:128]: eng.copy(o, i),
                                waits=[("pe", tok)], inc="sca")
                    bank_free[bank] = ("sca", te)
                    tr_tok[dst.name] = ("sca", te)
                tok_wT, tok_ET, tok_enmT = tr_tok["wT_sb"], tr_tok["ET_sb"], tr_tok["enmT_sb"]

                t_EscT = b.emit("vector",
                                lambda eng: eng.tensor_scalar(EscT[:], ET[:], QSCALE, None, ALU.mult),
                                waits=[tok_ET], inc="vec")
                b.emit("sync", _dma(decrow[:], ET[127:128, :]), waits=[tok_ET], inc="ddc", n=16)
                tok = b.emit("tensor",
                             lambda eng: eng.matmul(ps[6][:, 0:128], ones_sb[0:1, 0:128], decrow[:],
                                                    start=True, stop=True),
                             waits=ps_wait(6) + [b.fence("ddc"), f_cst1], inc="pe")
                t_decb = b.emit("scalar",
                                lambda eng: eng.copy(decb[:], ps[6][:, 0:128]),
                                waits=[("pe", tok)], inc="sca")
                bank_free[6] = ("sca", t_decb)

                b.flush(blk, sems, lastw)

            b.barrier()

            with contextlib.ExitStack() as st:
                qc = [st.enter_context(nc.sbuf_tensor(f"qc{i}", [128, NH * 128], F32)) for i in range(2)]
                kc = [st.enter_context(nc.sbuf_tensor(f"kc{i}", [128, NH * 128], F32)) for i in range(2)]
                vTc = [st.enter_context(nc.sbuf_tensor(f"vTc{i}", [128, 2 * NH * 128], F32)) for i in range(2)]
                sgc = [st.enter_context(nc.sbuf_tensor(f"sgc{i}", [128, 2 * NH * 128], F32)) for i in range(2)]
                glhs = [st.enter_context(nc.sbuf_tensor(f"glhs{i}", [2, NH * 128], F32)) for i in range(2)]
                grhs = [st.enter_context(nc.sbuf_tensor(f"grhs{i}", [2, NH * 128], F32)) for i in range(2)]
                CA = st.enter_context(nc.sbuf_tensor("CA_sb", [128, 2 * NH * 264], F32))
                VA = [st.enter_context(nc.sbuf_tensor(f"VA{i}", [128, 264], F32)) for i in range(2)]
                kw = [st.enter_context(nc.sbuf_tensor(f"kw{i}", [128, 128], F32)) for i in range(2)]
                wm = [st.enter_context(nc.sbuf_tensor(f"wm{i}", [128, 128], F32)) for i in range(2)]
                we = [st.enter_context(nc.sbuf_tensor(f"we{i}", [128, 128], F32)) for i in range(2)]
                sw = [st.enter_context(nc.sbuf_tensor(f"sw{i}", [128, 128], F32)) for i in range(2)]
                bsb = [st.enter_context(nc.sbuf_tensor(f"bs{i}", [128, 264], F32)) for i in range(2)]
                hsb = [st.enter_context(nc.sbuf_tensor(f"hs{i}", [128, 264], F32)) for i in range(2)]
                hf = [st.enter_context(nc.sbuf_tensor(f"hf{i}", [128, 256], F32)) for i in range(2)]
                hd = [st.enter_context(nc.sbuf_tensor(f"hd{i}", [128, 256], F32)) for i in range(2)]
                hn = [st.enter_context(nc.sbuf_tensor(f"hn{i}", [128, 256], F32)) for i in range(2)]
                scr = st.enter_context(nc.sbuf_tensor("scr_sb", [128, 16], F32))
                junk = st.enter_context(nc.sbuf_tensor("junk2_sb", [128, 256], F32))
                dnst = [st.enter_context(nc.sbuf_tensor(f"dnst{i}", [128, 4096], F32)) for i in range(2)]
                epsc = st.enter_context(nc.sbuf_tensor("epsc_sb", [128, 1], F32))
                dnsg = [st.enter_context(nc.sbuf_tensor(f"dnsg{i}", [128, 4096], BF16)) for i in range(2)]
                hout = [st.enter_context(nc.sbuf_tensor(f"hout{i}", [128, 2 * NH * 128], F32)) for i in range(2)]

                def ca(par, h):
                    off = (par * NH + h) * 264
                    return CA[:, off:off + 257]

                t_caz = b.emit("vector", lambda eng: eng.memset(CA[:], 0.0), inc="vec")
                t_eps = b.emit("vector", lambda eng: eng.memset(epsc[:], EPS_MH), inc="vec")
                t_va0 = b.emit("vector", lambda eng: eng.memset(VA[0][:, 256:264], 1.0), inc="vec")
                t_va1 = b.emit("vector", lambda eng: eng.memset(VA[1][:, 256:264], 1.0), inc="vec")
                t_gl0 = b.emit("vector", lambda eng: eng.memset(glhs[0][0:1, :], 1.0), inc="vec")
                t_gl1 = b.emit("vector", lambda eng: eng.memset(glhs[1][0:1, :], 1.0), inc="vec")
                b.emit("sync", _dma(grhs[0][1:2, :], ones_d), inc="dgr", n=16)
                b.emit("sync", _dma(grhs[1][1:2, :], ones_d), inc="dgr", n=16)
                f_cst2 = b.fence("dgr")

                chunk_busy = {}
                c_tok = {h: ("vec", t_caz) for h in range(NH)}
                va_busy = {0: ("vec", t_va0), 1: ("vec", t_va1)}
                kw_busy = {}
                sbusy = {}
                gl_busy = {0: ("vec", t_gl0), 1: ("vec", t_gl1)}
                gr_busy = {}
                ld = {}

                dn_busy = {}
                dnsg_busy = {}
                for c in range(NCH):
                    pc = c % 2
                    csl = slice(c * 128, (c + 1) * 128)
                    for half2 in range(2):
                        sidx = 2 * c + half2
                        ft2, hh2 = sidx // 2, sidx % 2
                        dnp = sidx % 2
                        dns = "dnl" if dnp == 0 else "dsq"
                        b.emit("sync", _dma(dnst[dnp][:], dn_t[ft2, :, hh2 * 4096:(hh2 + 1) * 4096]),
                               waits=[dn_busy.get(dnp)], inc=dns, n=16)
                        f_dn = b.fence(dns)
                        t_sgn = b.emit("scalar",
                                       lambda eng, o=dnsg[dnp][:], i=dnst[dnp][:]:
                                       eng.activation(o, i, AF.Sign),
                                       waits=[f_dn, dnsg_busy.get(dnp)], inc="sca")
                        dn_busy[dnp] = ("sca", t_sgn)
                        b.emit("gpsimd",
                               _dma(dnsg_d[ft2, :, hh2 * 4096:(hh2 + 1) * 4096], dnsg[dnp][:]),
                               waits=[("sca", t_sgn)], inc="dsg", n=16)
                        dnsg_busy[dnp] = b.fence("dsg")
                    dcp = "dc0" if pc == 0 else "dc1"
                    b.emit("sync",
                           _dma(qc[pc][:].rearrange("d (hh t) -> d hh t", hh=8),
                                qT_d[:, csl].rearrange("(hh d) t -> d hh t", d=128)),
                           waits=[chunk_busy.get(("q", pc))], inc=dcp, n=16)
                    b.emit("sync",
                           _dma(kc[pc][:].rearrange("d (hh t) -> d hh t", hh=8),
                                kT_d[:, csl].rearrange("(hh d) t -> d hh t", d=128)),
                           waits=[chunk_busy.get(("k", pc))], inc=dcp, n=16)
                    b.emit("sync",
                           _dma(vTc[pc][:].rearrange("d (hh t) -> d hh t", hh=16),
                                vT_d[:, csl].rearrange("(hh d) t -> d hh t", d=128)),
                           waits=[chunk_busy.get(("v", pc))], inc=dcp, n=16)
                    b.emit("sync",
                           _dma(sgc[pc][:].rearrange("d (hh t) -> d hh t", hh=16),
                                sgT_d[:, csl].rearrange("(hh d) t -> d hh t", d=128)),
                           waits=[chunk_busy.get(("sg", pc))], inc=dcp, n=16)
                    b.emit("sync",
                           _dma(glhs[pc][1:2, :].rearrange("o (h t) -> o h t", h=8),
                                gch_d.rearrange("(h c) t -> c h t", c=16)[c:c + 1]),
                           waits=[gl_busy.get(pc)], inc=dcp, n=16)
                    b.emit("sync",
                           _dma(grhs[pc][0:1, :].rearrange("o (h t) -> o h t", h=8),
                                negM_d.rearrange("(h c) t -> c h t", c=16)[c:c + 1]),
                           waits=[gr_busy.get(pc), f_cst2], inc=dcp, n=16)
                    f_c = b.fence(dcp)

                    hb = c % 2
                    cpar = c % 2
                    hout_evac_last = None

                    def stageA(h):
                        hc = h * 16 + c
                        hp = h % 2
                        hsl = slice(h * 128, (h + 1) * 128)
                        bt = 6 + hp
                        bS = 2 + hp
                        b.emit("tensor",
                               lambda eng, o=ps[bt][:, 0:128], i=kc[pc][:, hsl]:
                               eng.transpose(o, i, ident[:]),
                               waits=ps_wait(bt) + [f_c], inc=None)
                        b.emit("tensor",
                               lambda eng, o=ps[bt][:, 128:256], i=vTc[pc][:, (2 * h) * 128:(2 * h + 1) * 128]:
                               eng.transpose(o, i, ident[:]))
                        tok = b.emit("tensor",
                                     lambda eng, o=ps[bt][:, 256:384], i=vTc[pc][:, (2 * h + 1) * 128:(2 * h + 2) * 128]:
                                     eng.transpose(o, i, ident[:]),
                                     inc="pe")
                        tokS = b.emit("tensor",
                                      lambda eng, o=ps[bS][:, 0:128], l=kc[pc][:, hsl], r=qc[pc][:, hsl]:
                                      eng.matmul(o, l, r, start=True, stop=True),
                                      waits=ps_wait(bS) + [f_c], inc=None)
                        tokW = b.emit("tensor",
                                      lambda eng, o=ps[bS][:, 128:256], l=glhs[pc][0:2, hsl], r=grhs[pc][0:2, hsl]:
                                      eng.matmul(o, l, r, start=True, stop=True),
                                      waits=[("vec", t_gl0 if pc == 0 else t_gl1)],
                                      inc="pe")
                        te = b.emit("scalar",
                                    lambda eng, o=kw[hp][:], i=ps[bt][:, 0:128], w=wT[:, hc:hc + 1]:
                                    eng.mul(o, i, w),
                                    waits=[("pe", tok), tok_wT, kw_busy.get(hp)], inc="sca")
                        te2 = b.emit("scalar",
                                     lambda eng, o=VA[hp][:, 0:256], i=ps[bt][:, 128:384]: eng.copy(o, i),
                                     waits=[va_busy.get(hp)], inc="sca")
                        tv = b.emit("vector",
                                    lambda eng, o=wm[hp][:], i0=ps[bS][:, 128:256]:
                                    eng.tensor_add(o, i0, cmask[:]),
                                    waits=[("pe", tokW), f_cst1, sbusy.get(("wm", hp))], inc="vec")
                        ta = b.emit("scalar",
                                    lambda eng, o=we[hp][:], i=wm[hp][:]: eng.activation(o, i, AF.Exp),
                                    waits=[("vec", tv), sbusy.get(("we", hp))], inc="sca")
                        tsw = b.emit("vector",
                                     lambda eng, o=sw[hp][:], i0=ps[bS][:, 0:128], w=we[hp][:]:
                                     eng.scalar_tensor_tensor(o, i0, QSCALE, w, ALU.mult, ALU.mult),
                                     waits=[("sca", ta), sbusy.get(("sw", hp))], inc="vec")
                        bank_free[bS] = ("vec", tsw)
                        sbusy[("wm", hp)] = ("sca", ta)
                        return dict(hc=hc, hp=hp, hsl=hsl, bt=bt,
                                    tok_kw=("sca", te), tok_va=("sca", te2),
                                    stage1_free=("sca", te2), tsw=tsw)

                    def stageB(st1):
                        nonlocal hout_evac_last
                        hc, hp, hsl, bt = st1["hc"], st1["hp"], st1["hsl"], st1["bt"]
                        h = hc // 16
                        tok_kw, tok_va, stage1_free, tsw = (st1["tok_kw"], st1["tok_va"],
                                                            st1["stage1_free"], st1["tsw"])
                        tokA = b.emit("tensor",
                                      lambda eng, o=ps[4][:, 0:257], l=sw[hp][:], r=VA[hp][:, 0:257]:
                                      eng.matmul(o, l, r, start=True, stop=True),
                                      waits=ps_wait(4) + [("vec", tsw), tok_va], inc="pe")
                        tokB = b.emit("tensor",
                                      lambda eng, o=ps[5][:, 0:257], l=qc[pc][:, hsl], r=ca(cpar, h):
                                      eng.matmul(o, l, r, start=True, stop=True),
                                      waits=ps_wait(5) + [f_c, c_tok[h]], inc="pe")
                        cb = hp
                        tokC = b.emit("tensor",
                                      lambda eng, o=ps[cb][:, 0:257], l=kw[hp][:], r=VA[hp][:, 0:257]:
                                      eng.matmul(o, l, r, start=True, stop=True),
                                      waits=ps_wait(cb) + [tok_kw, tok_va], inc="pe")
                        kw_busy[hp] = ("pe", tokC)
                        va_busy[hp] = ("pe", tokC)
                        tb = b.emit("scalar",
                                    lambda eng, o=bsb[hp][:, 0:257], i=ps[5][:, 0:257], e=EscT[:, hc:hc + 1]:
                                    eng.mul(o, i, e),
                                    waits=[("pe", tokB), ("vec", t_EscT), sbusy.get(("bs", hp))], inc="sca")
                        bank_free[5] = ("sca", tb)
                        th = b.emit("vector",
                                    lambda eng, o=hsb[hp][:, 0:257], a=ps[4][:, 0:257], bb=bsb[hp][:, 0:257]:
                                    eng.tensor_add(o, a, bb),
                                    waits=[("pe", tokA), ("sca", tb), sbusy.get(("hs", hp))], inc="vec")
                        bank_free[4] = ("vec", th)
                        sbusy[("bs", hp)] = ("vec", th)
                        tcu = b.emit("vector",
                                     lambda eng, o=ca(1 - cpar, h), i0=ca(cpar, h), dcol=decb[:, hc:hc + 1],
                                     p=ps[cb][:, 0:257]:
                                     eng.scalar_tensor_tensor(o, i0, dcol, p, ALU.mult, ALU.add),
                                     waits=[("pe", tokC), ("sca", t_decb), c_tok[h]], inc="vec")
                        bank_free[cb] = ("vec", tcu)
                        c_tok[h] = ("vec", tcu)
                        td0 = b.emit("scalar",
                                     lambda eng, o=scr[:, hp * 8:hp * 8 + 1], i=hsb[hp][:, 256:257]:
                                     eng.activation(o, i, AF.Abs),
                                     waits=[("vec", th)], inc="sca")
                        td2 = b.emit("vector",
                                     lambda eng, o=scr[:, hp * 8 + 2:hp * 8 + 3], i=scr[:, hp * 8:hp * 8 + 1],
                                     e=enmT[:, hc:hc + 1]:
                                     eng.tensor_scalar(o, i, e, EPS_MH, ALU.max, ALU.add),
                                     waits=[("sca", td0), tok_enmT], inc="vec")
                        td3 = b.emit("vector",
                                     lambda eng, o=scr[:, hp * 8 + 3:hp * 8 + 4], i=scr[:, hp * 8 + 2:hp * 8 + 3]:
                                     eng.reciprocal(o, i),
                                     waits=[("vec", td2)], inc="vec")
                        tf = b.emit("vector",
                                    lambda eng, o=hf[hp][:], i=hsb[hp][:, 0:256], r=scr[:, hp * 8 + 3:hp * 8 + 4]:
                                    eng.tensor_scalar(o, i, r, None, ALU.mult),
                                    waits=[("vec", td3), sbusy.get(("hf", hp))], inc="vec")
                        sbusy[("hs", hp)] = ("vec", tf)
                        tm1 = b.emit("vector",
                                     lambda eng, o=scr[:, hp * 8 + 4:hp * 8 + 5], i=hf[hp][:]:
                                     eng.reduce_sum(o, i, axis=AXX),
                                     waits=[("vec", tf)], inc="vec")
                        tm2 = b.emit("vector",
                                     lambda eng, o=scr[:, hp * 8 + 5:hp * 8 + 6], i=scr[:, hp * 8 + 4:hp * 8 + 5]:
                                     eng.tensor_scalar(o, i, -1.0 / 256.0, None, ALU.mult),
                                     waits=[("vec", tm1)], inc="vec")
                        tm4 = b.emit("scalar",
                                     lambda eng, o=junk[:, 0:256], i=hf[hp][:], nm=scr[:, hp * 8 + 5:hp * 8 + 6],
                                     a=scr[:, hp * 8 + 6:hp * 8 + 7]:
                                     eng.activation(o, i, AF.Square, bias=nm, accum_out=a),
                                     waits=[("vec", tm2)], inc="sca")
                        tm6 = b.emit("scalar",
                                     lambda eng, o=scr[:, hp * 8 + 6:hp * 8 + 7], i=scr[:, hp * 8 + 6:hp * 8 + 7]:
                                     eng.activation(o, i, AF.Ln, bias=epsc[:], scale=1.0 / 256.0),
                                     waits=[("sca", tm4), ("vec", t_eps)], inc="sca")
                        tm7 = b.emit("scalar",
                                     lambda eng, o=scr[:, hp * 8 + 6:hp * 8 + 7], i=scr[:, hp * 8 + 6:hp * 8 + 7]:
                                     eng.activation(o, i, AF.Exp, scale=-0.5),
                                     waits=[("sca", tm6)], inc="sca")
                        tm8 = b.emit("vector",
                                     lambda eng, o=hn[hp][:], i=hf[hp][:], nm=scr[:, hp * 8 + 5:hp * 8 + 6],
                                     r=scr[:, hp * 8 + 6:hp * 8 + 7]:
                                     eng.tensor_scalar(o, i, nm, r, ALU.add, ALU.mult),
                                     waits=[("sca", tm7), sbusy.get(("hn", hp))], inc="vec")
                        sbusy[("hf", hp)] = ("vec", tm8)
                        b.emit("tensor",
                               lambda eng, o=ps[bt][:, 0:128], i=hn[hp][:, 0:128]:
                               eng.transpose(o, i, ident[:]),
                               waits=[stage1_free, ("vec", tm8)], inc=None)
                        tok = b.emit("tensor",
                                     lambda eng, o=ps[bt][:, 128:256], i=hn[hp][:, 128:256]:
                                     eng.transpose(o, i, ident[:]),
                                     inc="pe")
                        f_dsh = b.fence("dsh0" if hb == 0 else "dsh1") if c >= 2 else None
                        hel = None
                        for half in range(2):
                            col = 2 * h + half
                            hel = b.emit("vector",
                                         lambda eng, o=hout[hb][:, col * 128:(col + 1) * 128],
                                         i0=ps[bt][:, half * 128:(half + 1) * 128],
                                         g=gam_sb[:, col:col + 1],
                                         s2=sgc[pc][:, col * 128:(col + 1) * 128]:
                                         eng.scalar_tensor_tensor(o, i0, g, s2, ALU.mult, ALU.mult),
                                         waits=[("pe", tok), f_cst1, f_c, f_dsh],
                                         inc="vec")
                        bank_free[bt] = ("vec", hel)
                        sbusy[("hn", hp)] = ("vec", hel)
                        sbusy[("sw", hp)] = ("pe", tokA)
                        sbusy[("we", hp)] = ("vec", tsw)
                        hout_evac_last = hel

                    prev = None
                    for h in range(NH):
                        cur = stageA(h)
                        if prev is not None:
                            stageB(prev)
                        prev = cur
                    stageB(prev)
                    b.emit("gpsimd",
                           _dma(hoT_d[:, csl].rearrange("(hh d) t -> d hh t", d=128),
                                hout[hb][:].rearrange("d (hh t) -> d hh t", hh=16)),
                           waits=[("vec", hout_evac_last)], inc="dsh0" if hb == 0 else "dsh1", n=16)
                    gl_busy[pc] = ("pe", b.cnt["pe"])
                    gr_busy[pc] = ("pe", b.cnt["pe"])
                    chunk_busy[("q", pc)] = ("pe", b.cnt["pe"])
                    chunk_busy[("k", pc)] = ("pe", b.cnt["pe"])
                    chunk_busy[("v", pc)] = ("pe", b.cnt["pe"])
                    chunk_busy[("sg", pc)] = ("vec", b.cnt["vec"])
                f_dnsg = b.fence("dsg")
                f_hout0 = b.fence("dsh0")
                f_hout1 = b.fence("dsh1")
                b.flush(blk, sems, lastw)
            g_st.close()

            b.barrier()

            with contextlib.ExitStack() as st:
                woh_sb = st.enter_context(nc.sbuf_tensor("woh_sb", [128, NKT * S], BF16))
                wol_sb = st.enter_context(nc.sbuf_tensor("wol_sb", [128, NKT * S], BF16))
                hoT = st.enter_context(nc.sbuf_tensor("hoT_sb", [128, S], F32))
                lnhTt = st.enter_context(nc.sbuf_tensor("lnhTt_sb", [128, S], BF16))
                lnlTt = st.enter_context(nc.sbuf_tensor("lnlTt_sb", [128, S], BF16))
                hoh = [st.enter_context(nc.sbuf_tensor(f"hoh{i}", [128, 2048], BF16)) for i in range(2)]
                hol = [st.enter_context(nc.sbuf_tensor(f"hol{i}", [128, 2048], BF16)) for i in range(2)]
                xres = [st.enter_context(nc.sbuf_tensor("xres0", [128, 2048], F32))]
                ztile = st.enter_context(nc.sbuf_tensor("ztile_sb", [128, 2048], F32))
                zd = ztile
                lnt = st.enter_context(nc.sbuf_tensor("lnt_sb", [128, 2048], F32))
                lnTt = st.enter_context(nc.sbuf_tensor("lnTt_sb", [128, 2048], F32))
                cols = st.enter_context(nc.sbuf_tensor("cols_sb", [128, 4], F32))
                junk = st.enter_context(nc.sbuf_tensor("junk3_sb", [128, 2048], F32))

                wo_tok = {}
                for k in range(NKT):
                    b.emit("sync", _dma(woh_sb[:, k * S:(k + 1) * S], woh_d[k]),
                           waits=[f_wodram], inc="dwo", n=16)
                    b.emit("sync", _dma(wol_sb[:, k * S:(k + 1) * S], wol_d[k]),
                           waits=[f_wodram], inc="dwo", n=16)
                    wo_tok[k] = b.fence("dwo")
                f_wo = wo_tok[NKT - 1]

                ho_busy = {}
                ho_split_busy = {}
                x_busy = {}
                tr_busy = None
                lnTt_busy = None
                for m in range(NTT):
                    mp = m % 2
                    dpp = "dp0" if mp == 0 else "dp1"
                    b.emit("sync",
                           _dma(hoT[:].rearrange("p (fb t) -> p fb t", fb=16),
                                hoT_d[:, m * 128:(m + 1) * 128]
                                .rearrange("(fb p) t -> p fb t", p=128)),
                           waits=[f_hout0, f_hout1, ho_busy.get(0) or f_wo], inc=dpp, n=16)
                    b.emit("sync", _dma(xres[0][:], xn_d[m * 128:(m + 1) * 128, :]),
                           waits=[x_busy.get(0)], inc=dpp, n=16)
                    f_p = b.fence(dpp)
                    b.emit("vector",
                           lambda eng, o=hoh[mp][:], i=hoT[:]: eng.tensor_copy(o, i),
                           waits=[f_p, ho_split_busy.get(mp)], inc="vec")
                    t_hsp = b.emit("vector",
                                   lambda eng, o=hol[mp][:], i0=hoT[:], i1=hoh[mp][:]:
                                   eng.tensor_tensor(o, i0, i1, ALU.subtract),
                                   waits=[], inc="vec")
                    ho_busy[0] = ("vec", t_hsp)
                    ztoks = []
                    for n in range(4):
                        pp = n % 2
                        tok = None
                        for k in range(NKT):
                            waits = [wo_tok[k]]
                            if k == 0:
                                waits += ps_wait(pp) + [("vec", t_hsp)]
                            lh = hoh[mp][:, k * 128:(k + 1) * 128]
                            ll = hol[mp][:, k * 128:(k + 1) * 128]
                            rh = woh_sb[:, k * S + n * 512: k * S + n * 512 + 512]
                            rl = wol_sb[:, k * S + n * 512: k * S + n * 512 + 512]
                            b.emit("tensor",
                                   lambda eng, o=ps[pp][:, 0:512], l=lh, r=rh, st_=(k == 0):
                                   eng.matmul(o, l, r, start=st_, stop=False),
                                   waits=waits)
                            b.emit("tensor",
                                   lambda eng, o=ps[pp][:, 0:512], l=lh, r=rl:
                                   eng.matmul(o, l, r, start=False, stop=False))
                            tok = b.emit("tensor",
                                         lambda eng, o=ps[pp][:, 0:512], l=ll, r=rh,
                                         sp_=(k == NKT - 1):
                                         eng.matmul(o, l, r, start=False, stop=sp_),
                                         inc="pe" if k == NKT - 1 else None)
                        tz = b.emit("vector",
                                    lambda eng, o=ztile[:, n * 512:(n + 1) * 512], a=ps[pp][:, 0:512],
                                    x=xres[0][:, n * 512:(n + 1) * 512]:
                                    eng.tensor_add(o, a, x),
                                    waits=[("pe", tok), f_p], inc="vec")
                        bank_free[pp] = ("vec", tz)
                        ztoks.append(("vec", tz))
                    ho_split_busy[mp] = ("pe", b.cnt["pe"])
                    x_busy[0] = ztoks[-1]
                    l1 = b.emit("vector",
                                lambda eng: eng.reduce_sum(cols[:, 0:1], ztile[:], axis=AXX),
                                waits=ztoks, inc="vec")
                    l2 = b.emit("vector",
                                lambda eng: eng.tensor_scalar(cols[:, 0:1], cols[:, 0:1], 1.0 / D, None, ALU.mult),
                                waits=[("vec", l1)], inc="vec")
                    l3 = b.emit("vector",
                                lambda eng: eng.tensor_scalar(zd[:], ztile[:], cols[:, 0:1], None, ALU.subtract),
                                waits=[("vec", l2)], inc="vec")
                    l4 = b.emit("scalar",
                                lambda eng: eng.activation(junk[:], zd[:], AF.Square, accum_out=cols[:, 1:2]),
                                waits=[("vec", l3)], inc="sca")
                    l5 = b.emit("vector",
                                lambda eng: eng.tensor_scalar(cols[:, 1:2], cols[:, 1:2], 1.0 / D, EPS_LN,
                                                              ALU.mult, ALU.add),
                                waits=[("sca", l4)], inc="vec")
                    l6 = b.emit("scalar",
                                lambda eng: eng.activation(cols[:, 2:3], cols[:, 1:2], AF.Ln),
                                waits=[("vec", l5)], inc="sca")
                    l7 = b.emit("scalar",
                                lambda eng: eng.activation(cols[:, 2:3], cols[:, 2:3], AF.Exp, scale=-0.5),
                                waits=[("sca", l6)], inc="sca")
                    l8 = b.emit("vector",
                                lambda eng: eng.tensor_scalar(lnt[:], zd[:], cols[:, 2:3], None, ALU.mult),
                                waits=[("sca", l7), b.fence("dsn"), tr_busy], inc="vec")
                    b.emit("gpsimd", _dma(ln1n_d[m * 128:(m + 1) * 128, :], lnt[:]),
                           waits=[("vec", l8)], inc="dsn", n=16)
                    trtoks = []
                    for fb in range(NKT):
                        bank = 6 if fb % 2 == 0 else 7
                        tok = b.emit("tensor",
                                     lambda eng, o=ps[bank][:, 0:128], i=lnt[:, fb * 128:(fb + 1) * 128]:
                                     eng.transpose(o, i, ident[:]),
                                     waits=ps_wait(bank) + [("vec", l8), tr_busy, lnTt_busy], inc="pe")
                        te = b.emit("scalar",
                                    lambda eng, o=lnTt[:, fb * 128:(fb + 1) * 128], i=ps[bank][:, 0:128]:
                                    eng.copy(o, i),
                                    waits=[("pe", tok)], inc="sca")
                        bank_free[bank] = ("sca", te)
                        trtoks.append(("sca", te))
                    tr_busy = ("pe", b.cnt["pe"])
                    b.emit("vector",
                           lambda eng, o=lnhTt[:], i=lnTt[:]: eng.tensor_copy(o, i),
                           waits=trtoks + [b.fence("dst")], inc="vec")
                    t_lsp = b.emit("vector",
                                   lambda eng, o=lnlTt[:], i0=lnTt[:], i1=lnhTt[:]:
                                   eng.tensor_tensor(o, i0, i1, ALU.subtract),
                                   waits=[], inc="vec")
                    lnTt_busy = ("vec", t_lsp)
                    b.emit("gpsimd",
                           _dma(lnh_d.rearrange("fb p t -> p fb t")[:, :, m * 128:(m + 1) * 128],
                                lnhTt[:].rearrange("p (fb t) -> p fb t", fb=16)),
                           waits=[("vec", t_lsp)], inc="dst", n=16)
                    b.emit("gpsimd",
                           _dma(lnl_d.rearrange("fb p t -> p fb t")[:, :, m * 128:(m + 1) * 128],
                                lnlTt[:].rearrange("p (fb t) -> p fb t", fb=16)),
                           waits=[("vec", t_lsp)], inc="dst", n=16)
                f_ln = b.fence("dst")
                f_lnn = b.fence("dsn")
                b.flush(blk, sems, lastw)

            b.barrier()

            with contextlib.ExitStack() as st:
                lnh_sb = st.enter_context(nc.sbuf_tensor("lnh_sb", [128, NKT * S], BF16))
                lnl_sb = st.enter_context(nc.sbuf_tensor("lnl_sb", [128, NKT * S], BF16))
                uw = [st.enter_context(nc.sbuf_tensor(f"uw{i}", [128, D], F32)) for i in range(2)]
                usg = [st.enter_context(nc.sbuf_tensor(f"usg{i}", [128, D], BF16)) for i in range(2)]
                hidf = [st.enter_context(nc.sbuf_tensor("hidf0", [128, S], F32))]
                hidh = [st.enter_context(nc.sbuf_tensor(f"hidh{i}", [128, S], BF16)) for i in range(2)]
                hidl = [st.enter_context(nc.sbuf_tensor(f"hidl{i}", [128, S], BF16)) for i in range(2)]
                upsc_sb = st.enter_context(nc.sbuf_tensor("upsc_sb", [128, 64], F32))

                b.emit("sync",
                       _dma_nc(upsc_sb[:].rearrange("p (ht x) -> p ht x", x=1),
                               upsc_d.rearrange("(ht p) x -> p ht x", p=128)),
                       inc="dus", n=16)
                f_us = b.fence("dus")
                ln_tok = {}
                for k in range(NKT):
                    b.emit("sync", _dma(lnh_sb[:, k * S:(k + 1) * S], lnh_d[k]),
                           waits=[f_ln], inc="dlt", n=16)
                    b.emit("sync", _dma(lnl_sb[:, k * S:(k + 1) * S], lnl_d[k]),
                           waits=[f_ln], inc="dlt", n=16)
                    ln_tok[k] = b.fence("dlt")
                f_lt = ln_tok[NKT - 1]

                uw_busy = {}
                usg_busy = {}
                hidf_busy = {}
                for ht in range(64):
                    wp = ht % 2
                    dup = "du0" if wp == 0 else "du1"
                    b.emit("sync", _dma(uw[wp][:], up_t[ht]),
                           waits=[uw_busy.get(wp)], inc=dup, n=16)
                    t_sg2 = b.emit("scalar",
                                   lambda eng, o=usg[wp][:], i=uw[wp][:]: eng.activation(o, i, AF.Sign),
                                   waits=[b.fence(dup), usg_busy.get(wp)], inc="sca")
                    uw_busy[wp] = ("sca", t_sg2)
                    etoks = []
                    for n in range(4):
                        pp = n % 2
                        tok = None
                        for k in range(NKT):
                            waits = [ln_tok[k]]
                            if k == 0:
                                waits += ps_wait(pp) + [("sca", t_sg2), b.fence("dsd0" if wp == 0 else "dsd1")]
                            lg = usg[wp][:, k * 128:(k + 1) * 128]
                            rh = lnh_sb[:, k * S + n * 512: k * S + n * 512 + 512]
                            rl = lnl_sb[:, k * S + n * 512: k * S + n * 512 + 512]
                            b.emit("tensor",
                                   lambda eng, o=ps[pp][:, 0:512], l=lg, r=rh, st_=(k == 0):
                                   eng.matmul(o, l, r, start=st_, stop=False),
                                   waits=waits)
                            tok = b.emit("tensor",
                                         lambda eng, o=ps[pp][:, 0:512], l=lg, r=rl,
                                         sp_=(k == NKT - 1):
                                         eng.matmul(o, l, r, start=False, stop=sp_),
                                         inc="pe" if k == NKT - 1 else None)
                        nsl = slice(n * 512, (n + 1) * 512)
                        te = b.emit("scalar",
                                    lambda eng, o=hidf[0][:, nsl], i=ps[pp][:, 0:512],
                                    sc=upsc_sb[:, ht:ht + 1]:
                                    eng.activation(o, i, AF.Relu, scale=sc),
                                    waits=[("pe", tok), f_us, hidf_busy.get(0)], inc="sca")
                        bank_free[pp] = ("sca", te)
                        b.emit("vector",
                               lambda eng, o=hidh[wp][:, nsl], i=hidf[0][:, nsl]:
                               eng.tensor_copy(o, i),
                               waits=[("sca", te)], inc="vec")
                        tv3 = b.emit("vector",
                                     lambda eng, o=hidl[wp][:, nsl], i0=hidf[0][:, nsl],
                                     i1=hidh[wp][:, nsl]:
                                     eng.tensor_tensor(o, i0, i1, ALU.subtract),
                                     waits=[], inc="vec")
                        etoks.append(("vec", tv3))
                    usg_busy[wp] = ("pe", b.cnt["pe"])
                    hidf_busy[0] = etoks[-1]
                    dsp = "dsd0" if wp == 0 else "dsd1"
                    b.emit("gpsimd", _dma(hidT_h_d[ht * 128:(ht + 1) * 128, :], hidh[wp][:]),
                           waits=etoks, inc=dsp, n=16)
                    b.emit("gpsimd", _dma(hidT_l_d[ht * 128:(ht + 1) * 128, :], hidl[wp][:]),
                           waits=etoks, inc=dsp, n=16)
                f_hid = None
                f_hid0 = b.fence("dsd0")
                f_hid1 = b.fence("dsd1")
                b.flush(blk, sems, lastw)

            b.barrier()

            with contextlib.ExitStack() as st:
                hbh_sb = st.enter_context(nc.sbuf_tensor("hbh_sb", [128, 64 * 512], BF16))
                hbl_sb = st.enter_context(nc.sbuf_tensor("hbl_sb", [128, 64 * 512], BF16))
                dsg = [st.enter_context(nc.sbuf_tensor(f"dsg{i}", [128, 4096], BF16)) for i in range(4)]
                oT = [st.enter_context(nc.sbuf_tensor(f"oT{i}", [128, 512], F32)) for i in range(2)]
                dnsc_sb = st.enter_context(nc.sbuf_tensor("dnsc_sb", [128, 16], F32))
                dnsc2 = st.enter_context(nc.sbuf_tensor("dnsc2_sb", [128, 16], F32))
                osc_sb = st.enter_context(nc.sbuf_tensor("osc_sb", [1, 1], F32))
                oscc = st.enter_context(nc.sbuf_tensor("oscc_sb", [128, 1], F32))
                ob = [st.enter_context(nc.sbuf_tensor("ob0", [128, 2048], F32))]
                lb = [st.enter_context(nc.sbuf_tensor("lb0", [128, 2048], F32))]
                ftile = st.enter_context(nc.sbuf_tensor("ftile_sb", [128, 2048], F32))
                z2 = st.enter_context(nc.sbuf_tensor("z2_sb", [128, 2048], F32))
                z2d = z2
                otile = st.enter_context(nc.sbuf_tensor("otile_sb", [128, 2048], F32))
                cols2 = st.enter_context(nc.sbuf_tensor("cols2_sb", [128, 4], F32))
                junk = otile
                ob_busy = {}
                lb_busy = {}
                f_busy = None

                b.emit("sync",
                       _dma_nc(dnsc_sb[:].rearrange("p (ft x) -> p ft x", x=1),
                               dnsc_d.rearrange("(ft p) x -> p ft x", p=128)),
                       inc="dds", n=16)
                b.emit("sync", _dma(osc_sb[:], osc_d), inc="dds", n=16)
                f_cst5 = b.fence("dds")
                tok = b.emit("tensor",
                             lambda eng: eng.matmul(ps[6][:, 0:1], ones_sb[0:1, 0:128], osc_sb[:],
                                                    start=True, stop=True),
                             waits=ps_wait(6) + [f_cst5], inc="pe")
                t_ob = b.emit("scalar", lambda eng: eng.copy(oscc[:], ps[6][:, 0:1]),
                              waits=[("pe", tok)], inc="sca")
                bank_free[6] = ("sca", t_ob)
                t_d2 = b.emit("vector",
                              lambda eng: eng.tensor_scalar(dnsc2[:], dnsc_sb[:], oscc[:], None, ALU.mult),
                              waits=[f_cst5, ("sca", t_ob)], inc="vec")

                hb_busy = None
                dw_busy = {}
                dsg_busy = {}
                for tc in range(4):
                    hb_tok = {}
                    for kr in range(8):
                        b.emit("sync",
                               _dma(hbh_sb[:, kr * 8 * 512:(kr + 1) * 8 * 512]
                                    .rearrange("p (k t) -> p k t", k=8),
                                    hidT_h_d[kr * 1024:(kr + 1) * 1024, tc * 512:(tc + 1) * 512]
                                    .rearrange("(k p) t -> p k t", p=128)),
                               waits=[f_hid0, f_hid1, hb_busy], inc="dhb", n=16)
                        b.emit("sync",
                               _dma(hbl_sb[:, kr * 8 * 512:(kr + 1) * 8 * 512]
                                    .rearrange("p (k t) -> p k t", k=8),
                                    hidT_l_d[kr * 1024:(kr + 1) * 1024, tc * 512:(tc + 1) * 512]
                                    .rearrange("(k p) t -> p k t", p=128)),
                               waits=[f_hid0, f_hid1, hb_busy], inc="dhb", n=16)
                        hb_tok[kr] = b.fence("dhb")
                    for ft in range(16):
                        pp = ft % 2
                        fb4 = (ft % 2) * 2
                        half_toks = []
                        for hh in range(2):
                            ddh = f"dd{fb4 + hh}"
                            b.emit("sync",
                                   _dma(dsg[fb4 + hh][:], dnsg_d[ft, :, hh * 4096:(hh + 1) * 4096]),
                                   waits=[dsg_busy.get(fb4 + hh), f_dnsg], inc=ddh, n=16)
                            half_toks.append(b.fence(ddh))
                        tok = None
                        for k in range(64):
                            hh = k // 32
                            waits = [hb_tok[k // 8]] if k % 8 == 0 else []
                            if k == 0:
                                waits += ps_wait(pp) + [half_toks[0]]
                            if k == 32:
                                waits += [half_toks[1]]
                            lg = dsg[fb4 + hh][:, (k % 32) * 128:(k % 32 + 1) * 128]
                            b.emit("tensor",
                                   lambda eng, o=ps[pp][:, 0:512], l=lg,
                                   r=hbh_sb[:, k * 512:(k + 1) * 512], st_=(k == 0):
                                   eng.matmul(o, l, r, start=st_, stop=False),
                                   waits=waits)
                            tok = b.emit("tensor",
                                         lambda eng, o=ps[pp][:, 0:512], l=lg,
                                         r=hbl_sb[:, k * 512:(k + 1) * 512], sp_=(k == 63):
                                         eng.matmul(o, l, r, start=False, stop=sp_),
                                         inc="pe" if k == 63 else None)
                        dsg_busy[fb4] = ("pe", tok)
                        dsg_busy[fb4 + 1] = ("pe", tok)
                        op = ft % 2
                        if op == 0:
                            te = b.emit("scalar",
                                        lambda eng, o=oT[op][:], i=ps[pp][:, 0:512], sc=dnsc2[:, ft:ft + 1]:
                                        eng.activation(o, i, AF.Copy, scale=sc),
                                        waits=[("pe", tok), ("vec", t_d2), b.fence("dso0")], inc="sca")
                            bank_free[pp] = ("sca", te)
                        else:
                            te = b.emit("vector",
                                        lambda eng, o=oT[op][:], i=ps[pp][:, 0:512], sc=dnsc2[:, ft:ft + 1]:
                                        eng.tensor_scalar(o, i, sc, None, ALU.mult),
                                        waits=[("pe", tok), ("vec", t_d2), b.fence("dso1")], inc="vec")
                            bank_free[pp] = ("vec", te)
                        b.emit("gpsimd",
                               _dma(oT_d[ft * 128:(ft + 1) * 128, tc * 512:(tc + 1) * 512], oT[op][:]),
                               waits=[bank_free[pp]], inc="dso0" if op == 0 else "dso1", n=16)
                    hb_busy = ("pe", b.cnt["pe"])
                    f_tc0 = b.fence("dso0")
                    f_tc1 = b.fence("dso1")
                    for m in range(4 * tc, 4 * tc + 4):
                        mp = m % 2
                        dop = "do0" if mp == 0 else "do1"
                        b.emit("sync",
                               _dma(ob[0][:].rearrange("p (fb t) -> p fb t", fb=16),
                                    oT_d[:, m * 128:(m + 1) * 128]
                                    .rearrange("(fb p) t -> p fb t", p=128)),
                               waits=[f_tc0, f_tc1, ob_busy.get(0)], inc=dop, n=16)
                        b.emit("sync", _dma(lb[0][:], ln1n_d[m * 128:(m + 1) * 128, :]),
                               waits=[f_lnn, lb_busy.get(0)], inc=dop, n=16)
                        f_p6 = b.fence(dop)
                        trtoks = []
                        for fb in range(NKT):
                            bank = 6 if fb % 2 == 0 else 7
                            tok = b.emit("tensor",
                                         lambda eng, o=ps[bank][:, 0:128], i=ob[0][:, fb * 128:(fb + 1) * 128]:
                                         eng.transpose(o, i, ident[:]),
                                         waits=ps_wait(bank) + [f_p6, f_busy], inc="pe")
                            te = b.emit("scalar",
                                        lambda eng, o=ftile[:, fb * 128:(fb + 1) * 128], i=ps[bank][:, 0:128]:
                                        eng.copy(o, i),
                                        waits=[("pe", tok)], inc="sca")
                            bank_free[bank] = ("sca", te)
                            trtoks.append(("sca", te))
                        ob_busy[0] = ("pe", b.cnt["pe"])
                        tz = b.emit("vector",
                                    lambda eng, l=lb[0][:]: eng.tensor_add(z2[:], ftile[:], l),
                                    waits=trtoks + [f_p6], inc="vec")
                        f_busy = ("vec", tz)
                        lb_busy[0] = ("vec", tz)
                        l1 = b.emit("vector",
                                    lambda eng: eng.reduce_sum(cols2[:, 0:1], z2[:], axis=AXX),
                                    waits=[("vec", tz)], inc="vec")
                        l2 = b.emit("vector",
                                    lambda eng: eng.tensor_scalar(cols2[:, 0:1], cols2[:, 0:1], 1.0 / D, None, ALU.mult),
                                    waits=[("vec", l1)], inc="vec")
                        l3 = b.emit("vector",
                                    lambda eng: eng.tensor_scalar(z2d[:], z2[:], cols2[:, 0:1], None, ALU.subtract),
                                    waits=[("vec", l2)], inc="vec")
                        l4 = b.emit("scalar",
                                    lambda eng: eng.activation(junk[:], z2d[:], AF.Square, accum_out=cols2[:, 1:2]),
                                    waits=[("vec", l3), b.fence("dou")], inc="sca")
                        l5 = b.emit("vector",
                                    lambda eng: eng.tensor_scalar(cols2[:, 1:2], cols2[:, 1:2], 1.0 / D, EPS_LN,
                                                                  ALU.mult, ALU.add),
                                    waits=[("sca", l4)], inc="vec")
                        l6 = b.emit("scalar",
                                    lambda eng: eng.activation(cols2[:, 2:3], cols2[:, 1:2], AF.Ln),
                                    waits=[("vec", l5)], inc="sca")
                        l7 = b.emit("scalar",
                                    lambda eng: eng.activation(cols2[:, 2:3], cols2[:, 2:3], AF.Exp, scale=-0.5),
                                    waits=[("sca", l6)], inc="sca")
                        l8 = b.emit("vector",
                                    lambda eng: eng.tensor_scalar(otile[:], z2d[:], cols2[:, 2:3], None, ALU.mult),
                                    waits=[("sca", l7), b.fence("dou")], inc="vec")
                        b.emit("gpsimd", _dma(out_d[m * 128:(m + 1) * 128, :], otile[:]),
                               waits=[("vec", l8)], inc="dou", n=16)

                b.emit("gpsimd", None, waits=[b.fence("dou")])
                b.flush(blk, sems, lastw)

    return nc


def _host_layouts(inputs):
    f = np.float32

    def tile_lhsT(w):
        o, i = w.shape
        return np.ascontiguousarray(
            w.reshape(o // 128, 128, i // 128, 128).transpose(0, 3, 2, 1)
        ).reshape(o // 128, 128, i)

    wig = np.asarray(inputs["L0_attn_Wig"], f)
    wfg = np.asarray(inputs["L0_attn_Wfg"], f)
    wifT = np.concatenate([wig, wfg], axis=0).T
    wif_tiled = np.ascontiguousarray(
        wifT.reshape(16, 128, 16).transpose(1, 0, 2)).reshape(128, 256)

    j = np.arange(128)
    cmask = np.where(j[:, None] > j[None, :], np.float32(NEGBIG), np.float32(0.0)).astype(f)

    return {
        "wq_t": tile_lhsT(np.asarray(inputs["L0_attn_Wq"], f)),
        "wk_t": tile_lhsT(np.asarray(inputs["L0_attn_Wk"], f)),
        "wv_t": tile_lhsT(np.asarray(inputs["L0_attn_Wv"], f)),
        "wog_t": tile_lhsT(np.asarray(inputs["L0_attn_Wog"], f)),
        "wif_t": wif_tiled,
        "bif": np.concatenate([np.asarray(inputs["L0_attn_big"], f),
                               np.asarray(inputs["L0_attn_bfg"], f)]).reshape(16, 1),
        "gam": np.ascontiguousarray(np.asarray(inputs["L0_attn_gamma"], f).reshape(D, 1)),
        "woutT": np.ascontiguousarray(np.asarray(inputs["L0_attn_Wout"], f).T),
        "up_t": tile_lhsT(np.asarray(inputs["L0_ffn_up_weight"], f)),
        "dn_t": tile_lhsT(np.asarray(inputs["L0_ffn_down_weight"], f)),
        "upsc": np.ascontiguousarray(np.asarray(inputs["L0_ffn_up_scale"], f).reshape(DH, 1)),
        "dnsc": np.ascontiguousarray(np.asarray(inputs["L0_ffn_down_scale"], f).reshape(D, 1)),
        "osc": np.asarray(inputs["L0_ffn_output_scale"], f).reshape(1, 1),
        "ident": np.eye(128, dtype=f),
        "cmask": np.ascontiguousarray(cmask),
        "ones_r": np.ones((1, 1024), f),
    }


def kernel(**inputs):
    x = np.asarray(inputs["L0_attn_x"], np.float32)
    shared = _host_layouts(inputs)
    nc = build_nc(debug=_DEBUG)
    in_maps = []
    for bi in range(8):
        m = dict(shared)
        m["xT"] = np.ascontiguousarray(x[bi].T)
        m["xn"] = np.ascontiguousarray(x[bi])
        in_maps.append(m)
    res = run_bass_kernel_spmd(nc, in_maps, core_ids=list(range(8)))
    return np.stack([np.asarray(r["out"]) for r in res.results], axis=0)


# version 19
# speedup vs baseline: 1.0484x; 1.0191x over previous
import contextlib
import numpy as np
import concourse.bass as bass
import concourse.mybir as mybir
from concourse.bass_utils import run_bass_kernel_spmd

F32 = mybir.dt.float32
BF16 = mybir.dt.bfloat16
AF = mybir.ActivationFunctionType
ALU = mybir.AluOpType
AXX = mybir.AxisListType.X

D = 2048
S = 2048
NH = 8
DQ = 128
DV = 256
NCH = 16
DH = 8192
NKT = 16
NTT = 16
QSCALE = float(DQ) ** -0.5
EPS_MH = 1e-6
EPS_LN = 1e-5
NEGBIG = -1.0e30

_DEBUG = False

_ENGINES = ("sync", "tensor", "vector", "scalar", "gpsimd")


DMA_SEMS = ("cst", "dmx", "dw0", "dw1", "ds1", "dsg", "dgt", "dmp", "ddc",
            "dgr", "dus", "dds", "dwx", "dsw", "dnl", "dsq",
            "dc0", "dc1", "dsh0", "dsh1", "dwo", "dp0", "dp1", "dsn", "dst",
            "dlt", "du0", "du1", "dsd0", "dsd1", "dhb", "dd0", "dd1", "dd2", "dd3", "dso0", "dso1",
            "do0", "do1", "dou")
ALL_SEMS = DMA_SEMS + ("pe", "vec", "sca", "gps", "bar")


class _Build:

    def __init__(self):
        self.ops = {e: [] for e in _ENGINES}
        self.cnt = {s: 0 for s in ALL_SEMS}

    def fence(self, sem):
        v = self.cnt[sem]
        return (sem, v) if v else None

    def emit(self, eng, fn=None, waits=(), inc=None, n=1):
        tok = None
        if inc is not None:
            self.cnt[inc] += n
            tok = self.cnt[inc]
        ws = []
        for w in waits:
            if w is None:
                continue
            s, v = w
            if v is None:
                continue
            ws.append((s, v))
        self.ops[eng].append((fn, tuple(ws), inc, n))
        return tok

    def barrier(self):
        drains = [self.fence(sm) for sm in DMA_SEMS]
        own = {"tensor": "pe", "vector": "vec", "scalar": "sca", "gpsimd": "gps"}
        tgt = self.cnt["bar"] + len(_ENGINES)
        for e in _ENGINES:
            w = list(drains)
            es = own.get(e)
            if es and self.cnt[es]:
                w.append((es, self.cnt[es]))
            self.emit(e, lambda eng: eng.nop(), waits=w, inc="bar")
        for e in _ENGINES:
            self.emit(e, None, waits=[("bar", tgt)])

    def flush(self, blk, handles, lastw):
        for ename in _ENGINES:
            ops = self.ops[ename]
            if not ops:
                continue

            def body(eng, ops=ops, ename=ename):
                lw = lastw[ename]
                for fn, waits, inc, n in ops:
                    for s, v in waits:
                        if lw.get(s, 0) < v:
                            eng.wait_ge(handles[s], v)
                            lw[s] = v
                    if fn is None:
                        continue
                    ins = fn(eng)
                    if inc is not None:
                        ins.then_inc(handles[inc], n)

            getattr(blk, ename)(body)
            self.ops[ename] = []


def _dma(dst, src):
    return lambda eng, d=dst, s=src: eng.dma_start(out=d, in_=s)


def _dma_nc(dst, src):
    return lambda eng, d=dst, s=src: eng.dma_start(out=d, in_=s, allow_slow_non_contiguous=True)


def build_nc(debug=False):
    nc = bass.Bass("TRN2", target_bir_lowering=False)

    xT_d = nc.dram_tensor("xT", [D, S], F32, kind="ExternalInput").ap()
    xn_d = nc.dram_tensor("xn", [S, D], F32, kind="ExternalInput").ap()
    wq_t = nc.dram_tensor("wq_t", [8, 128, D], F32, kind="ExternalInput").ap()
    wk_t = nc.dram_tensor("wk_t", [8, 128, D], F32, kind="ExternalInput").ap()
    wv_t = nc.dram_tensor("wv_t", [16, 128, D], F32, kind="ExternalInput").ap()
    wog_t = nc.dram_tensor("wog_t", [16, 128, D], F32, kind="ExternalInput").ap()
    wif_t = nc.dram_tensor("wif_t", [128, 256], F32, kind="ExternalInput").ap()
    bif_d = nc.dram_tensor("bif", [16, 1], F32, kind="ExternalInput").ap()
    gam_d = nc.dram_tensor("gam", [D, 1], F32, kind="ExternalInput").ap()
    woutT_d = nc.dram_tensor("woutT", [D, D], F32, kind="ExternalInput").ap()
    up_t = nc.dram_tensor("up_t", [64, 128, D], F32, kind="ExternalInput").ap()
    dn_t = nc.dram_tensor("dn_t", [16, 128, DH], F32, kind="ExternalInput").ap()
    upsc_d = nc.dram_tensor("upsc", [DH, 1], F32, kind="ExternalInput").ap()
    dnsc_d = nc.dram_tensor("dnsc", [D, 1], F32, kind="ExternalInput").ap()
    osc_d = nc.dram_tensor("osc", [1, 1], F32, kind="ExternalInput").ap()
    ident_d = nc.dram_tensor("ident", [128, 128], F32, kind="ExternalInput").ap()
    cmask_d = nc.dram_tensor("cmask", [128, 128], F32, kind="ExternalInput").ap()
    ones_d = nc.dram_tensor("ones_r", [1, 1024], F32, kind="ExternalInput").ap()

    spill_kind = "ExternalOutput" if debug else "Internal"
    qT_d = nc.dram_tensor("qT_s", [NH * DQ, S], F32, kind=spill_kind).ap()
    kT_d = nc.dram_tensor("kT_s", [NH * DQ, S], F32, kind=spill_kind).ap()
    vT_d = nc.dram_tensor("vT_s", [NH * DV, S], F32, kind=spill_kind).ap()
    sgT_d = nc.dram_tensor("sgT_s", [D, S], F32, kind=spill_kind).ap()
    hoT_d = nc.dram_tensor("hoT_s", [D, S], F32, kind=spill_kind).ap()
    ln1T_d = nc.dram_tensor("ln1T_s", [D, S], F32, kind=spill_kind).ap()
    lnh_d = nc.dram_tensor("lnh_s", [NKT, 128, S], BF16, kind="Internal").ap()
    lnl_d = nc.dram_tensor("lnl_s", [NKT, 128, S], BF16, kind="Internal").ap()
    woh_d = nc.dram_tensor("woh_s", [NKT, 128, S], BF16, kind="Internal").ap()
    wol_d = nc.dram_tensor("wol_s", [NKT, 128, S], BF16, kind="Internal").ap()
    dnsg_d = nc.dram_tensor("dnsg_s", [16, 128, DH], BF16, kind="Internal").ap()
    ln1n_d = nc.dram_tensor("ln1n_s", [S, D], F32, kind=spill_kind).ap()
    hidT_h_d = nc.dram_tensor("hidT_h_s", [DH, S], BF16, kind="Internal").ap()
    hidT_l_d = nc.dram_tensor("hidT_l_s", [DH, S], BF16, kind="Internal").ap()
    oT_d = nc.dram_tensor("oT_s", [D, S], F32, kind=spill_kind).ap()
    gate_d = nc.dram_tensor("gate_s", [3, 8, S], F32, kind=spill_kind).ap()
    gch_d = nc.dram_tensor("gch_s", [128, 128], F32, kind=spill_kind).ap()
    negM_d = nc.dram_tensor("negM_s", [128, 128], F32, kind=spill_kind).ap()
    out_d = nc.dram_tensor("out", [S, D], F32, kind="ExternalOutput").ap()

    b = _Build()
    bank_free = {}

    def ps_wait(bank):
        return [bank_free[bank]] if bank in bank_free else []

    with contextlib.ExitStack() as top:
        sems = {}
        for s in ALL_SEMS:
            sems[s] = top.enter_context(nc.semaphore(s))
        ps = [top.enter_context(nc.psum_tensor(f"ps{i}", [128, 512], F32)) for i in range(8)]
        ident = top.enter_context(nc.sbuf_tensor("ident_sb", [128, 128], F32))
        cmask = top.enter_context(nc.sbuf_tensor("cmask_sb", [128, 128], F32))
        ones_sb = top.enter_context(nc.sbuf_tensor("ones_sb", [1, 128], F32))
        lastw = {e: {} for e in _ENGINES}

        with nc.Block() as blk:
            b.emit("sync", _dma(ident[:], ident_d), inc="cst", n=16)
            b.emit("sync", _dma(cmask[:], cmask_d), inc="cst", n=16)
            b.emit("sync", _dma(ones_sb[:], ones_d[0:1, 0:128]), inc="cst", n=16)

            g_st = contextlib.ExitStack()
            wT = g_st.enter_context(nc.sbuf_tensor("wT_sb", [128, 128], F32))
            EscT = g_st.enter_context(nc.sbuf_tensor("EscT_sb", [128, 128], F32))
            enmT = g_st.enter_context(nc.sbuf_tensor("enmT_sb", [128, 128], F32))
            decb = g_st.enter_context(nc.sbuf_tensor("decb_sb", [128, 128], F32))
            gam_sb = g_st.enter_context(nc.sbuf_tensor("gam_sb", [128, 16], F32))

            with contextlib.ExitStack() as st:
                mglob = st.enter_context(nc.sbuf_tensor("mglob_sb", [8, S], F32))
                flc = st.enter_context(nc.sbuf_tensor("flc_sb", [128, 128], F32))
                ic = st.enter_context(nc.sbuf_tensor("ic_sb", [128, 128], F32))
                mcc = st.enter_context(nc.sbuf_tensor("mc_sb", [128, 128], F32))
                beta = st.enter_context(nc.sbuf_tensor("beta_sb", [128, 128], F32))
                gch = st.enter_context(nc.sbuf_tensor("gch_sb", [128, 128], F32))
                Mch = st.enter_context(nc.sbuf_tensor("Mch_sb", [128, 128], F32))
                negM = st.enter_context(nc.sbuf_tensor("negM_sb", [128, 128], F32))
                negMT = st.enter_context(nc.sbuf_tensor("negMT_sb", [128, 1], F32))
                mprev = st.enter_context(nc.sbuf_tensor("mprev_sb", [128, 1], F32))
                wst = st.enter_context(nc.sbuf_tensor("wst_sb", [128, 128], F32))
                Ee = st.enter_context(nc.sbuf_tensor("Ee_sb", [128, 128], F32))
                enm = st.enter_context(nc.sbuf_tensor("enm_sb", [128, 128], F32))
                ET = st.enter_context(nc.sbuf_tensor("ET_sb", [128, 128], F32))
                decrow = st.enter_context(nc.sbuf_tensor("decrow_sb", [1, 128], F32))
                xh_sb = st.enter_context(nc.sbuf_tensor("xh_sb", [128, NKT * S], BF16))
                xl_sb = st.enter_context(nc.sbuf_tensor("xl_sb", [128, NKT * S], BF16))
                xstage = [st.enter_context(nc.sbuf_tensor("xstage0", [128, S], F32))]
                wbuf = [st.enter_context(nc.sbuf_tensor("wbuf0", [128, D], F32))]
                wh = [st.enter_context(nc.sbuf_tensor(f"wh{i}", [128, D], BF16)) for i in range(2)]
                wl = [st.enter_context(nc.sbuf_tensor(f"wl{i}", [128, D], BF16)) for i in range(2)]
                strip = [st.enter_context(nc.sbuf_tensor("strip0", [128, S], F32))]
                wif_sb = st.enter_context(nc.sbuf_tensor("wif_sb", [128, 256], F32))
                wifh = st.enter_context(nc.sbuf_tensor("wifh_sb", [128, 256], BF16))
                wifl = st.enter_context(nc.sbuf_tensor("wifl_sb", [128, 256], BF16))
                bi_sb = st.enter_context(nc.sbuf_tensor("bi_sb", [8, 1], F32))
                bf_sb = st.enter_context(nc.sbuf_tensor("bf_sb", [8, 1], F32))
                bi15 = st.enter_context(nc.sbuf_tensor("bi15_sb", [8, 1], F32))
                bf15 = st.enter_context(nc.sbuf_tensor("bf15_sb", [8, 1], F32))
                isc = st.enter_context(nc.sbuf_tensor("isc_sb", [8, S], F32))
                fsc = st.enter_context(nc.sbuf_tensor("fsc_sb", [8, S], F32))

                xsp = {}
                xsp_tok = {}
                for k in range(NKT):
                    kp = 0
                    b.emit("sync",
                           _dma(xstage[kp][:], xT_d[k * 128:(k + 1) * 128, :]),
                           waits=[xsp.get(kp)], inc="dmx", n=16)
                    f_xs = b.fence("dmx")
                    b.emit("vector",
                           lambda eng, o=xh_sb[:, k * S:(k + 1) * S], i=xstage[kp][:]:
                           eng.tensor_copy(o, i),
                           waits=[f_xs], inc="vec")
                    tk = b.emit("vector",
                                lambda eng, o=xl_sb[:, k * S:(k + 1) * S], i0=xstage[kp][:],
                                i1=xh_sb[:, k * S:(k + 1) * S]:
                                eng.tensor_tensor(o, i0, i1, ALU.subtract),
                                waits=[], inc="vec")
                    xsp[kp] = ("vec", tk)
                    xsp_tok[k] = ("vec", tk)
                f_xT = ("vec", b.cnt["vec"])
                b.emit("sync", _dma(wif_sb[:], wif_t), inc="cst", n=16)
                f_wif = b.fence("cst")
                b.emit("vector", lambda eng: eng.tensor_copy(wifh[:], wif_sb[:]),
                       waits=[f_wif], inc="vec")
                t_wifl = b.emit("vector",
                                lambda eng: eng.tensor_tensor(wifl[:], wif_sb[:], wifh[:], ALU.subtract),
                                waits=[], inc="vec")
                b.emit("sync", _dma(bi_sb[:], bif_d[0:8, :]), inc="cst", n=16)
                b.emit("sync", _dma(bf_sb[:], bif_d[8:16, :]), inc="cst", n=16)
                b.emit("sync",
                       _dma_nc(gam_sb[:].rearrange("d (hh x) -> d hh x", x=1),
                               gam_d.rearrange("(hh d) x -> d hh x", d=128)),
                       inc="cst", n=16)
                f_cst1 = b.fence("cst")
                t_bi15 = b.emit("vector",
                                lambda eng: eng.tensor_scalar(bi15[:], bi_sb[:], 1.0 / 15.0, None, ALU.mult),
                                waits=[f_cst1], inc="vec")
                t_bf15 = b.emit("vector",
                                lambda eng: eng.tensor_scalar(bf15[:], bf_sb[:], 1.0 / 15.0, None, ALU.mult),
                                waits=[f_cst1], inc="vec")

                gtoks = []
                for half, (bb, dstt, btok) in enumerate([(bi15, isc, t_bi15), (bf15, fsc, t_bf15)]):
                    for n in range(4):
                        pp = 2 + half
                        tok = None
                        for k in range(NKT):
                            waits = [xsp_tok[k]]
                            if k == 0:
                                waits += ps_wait(pp) + [("vec", t_wifl)]
                            csl2 = slice(k * 16 + half * 8, k * 16 + half * 8 + 8)
                            rh = xh_sb[:, k * S + n * 512: k * S + n * 512 + 512]
                            rl = xl_sb[:, k * S + n * 512: k * S + n * 512 + 512]
                            b.emit("tensor",
                                   lambda eng, o=ps[pp][0:8, 0:512], l=wifh[:, csl2], r=rh,
                                   st_=(k == 0):
                                   eng.matmul(o, l, r, start=st_, stop=False),
                                   waits=waits)
                            b.emit("tensor",
                                   lambda eng, o=ps[pp][0:8, 0:512], l=wifh[:, csl2], r=rl:
                                   eng.matmul(o, l, r, start=False, stop=False))
                            tok = b.emit("tensor",
                                         lambda eng, o=ps[pp][0:8, 0:512], l=wifl[:, csl2], r=rh,
                                         sp_=(k == NKT - 1):
                                         eng.matmul(o, l, r, start=False, stop=sp_),
                                         inc="pe" if k == NKT - 1 else None)
                        te = b.emit("scalar",
                                    lambda eng, o=dstt[:, n * 512:(n + 1) * 512], i=ps[pp][0:8, 0:512],
                                    bias=bb[:]:
                                    eng.activation(o, i, AF.Tanh, bias=bias, scale=1.0 / 15.0),
                                    waits=[("pe", tok), ("vec", btok)], inc="sca")
                        bank_free[pp] = ("sca", te)
                        gtoks.append(("sca", te))

                t_ii = b.emit("vector",
                              lambda eng: eng.tensor_scalar(isc[:], isc[:], 15.0, None, ALU.mult),
                              waits=gtoks[:4], inc="vec")
                t_sf = b.emit("scalar",
                              lambda eng: eng.activation(fsc[:], fsc[:], AF.Sigmoid, scale=15.0),
                              waits=gtoks[4:], inc="sca")
                t_fl = b.emit("scalar",
                              lambda eng: eng.activation(fsc[:], fsc[:], AF.Ln),
                              waits=[("sca", t_sf)], inc="sca")
                t_m = b.emit("vector",
                             lambda eng: eng.tensor_tensor_scan(mglob[:], fsc[:], isc[:], 0.0, ALU.add, ALU.max),
                             waits=[("sca", t_fl), ("vec", t_ii)], inc="vec")
                b.emit("gpsimd", _dma(gate_d[0], fsc[:]), waits=[("sca", t_fl)], inc="dsg", n=16)
                b.emit("gpsimd", _dma(gate_d[1], isc[:]), waits=[("vec", t_ii)], inc="dsg", n=16)
                b.emit("gpsimd", _dma(gate_d[2], mglob[:]), waits=[("vec", t_m)], inc="dsg", n=16)
                f_dsg = b.fence("dsg")
                b.emit("scalar",
                       _dma(flc[:], gate_d[0].rearrange("h (c t) -> (h c) t", t=128)),
                       waits=[f_dsg], inc="dgt", n=16)
                b.emit("scalar",
                       _dma(ic[:], gate_d[1].rearrange("h (c t) -> (h c) t", t=128)),
                       waits=[f_dsg], inc="dgt", n=16)
                b.emit("scalar",
                       _dma(mcc[:], gate_d[2].rearrange("h (c t) -> (h c) t", t=128)),
                       waits=[f_dsg], inc="dgt", n=16)
                f_gt1 = b.fence("dgt")
                t_beta = b.emit("vector",
                                lambda eng: eng.tensor_tensor_scan(beta[:], flc[:], flc[:], 0.0, ALU.add, ALU.bypass),
                                waits=[f_gt1], inc="vec")
                t_g = b.emit("vector",
                             lambda eng: eng.tensor_sub(gch[:], ic[:], beta[:]),
                             waits=[f_gt1, ("vec", t_beta)], inc="vec")
                t_M = b.emit("vector",
                             lambda eng: eng.tensor_sub(Mch[:], mcc[:], beta[:]),
                             waits=[f_gt1, ("vec", t_beta)], inc="vec")
                t_negM = b.emit("vector",
                                lambda eng: eng.tensor_scalar(negM[:], Mch[:], -1.0, None, ALU.mult),
                                waits=[("vec", t_M)], inc="vec")
                t_negMT = b.emit("vector",
                                 lambda eng: eng.tensor_scalar(negMT[:], Mch[:, 127:128], -1.0, None, ALU.mult),
                                 waits=[("vec", t_M)], inc="vec")
                b.emit("gpsimd", _dma(gch_d, gch[:]), waits=[("vec", t_g)], inc="dsg", n=16)
                b.emit("gpsimd", _dma(negM_d, negM[:]), waits=[("vec", t_negM)], inc="dsg", n=16)
                t_mpz = b.emit("vector", lambda eng: eng.memset(mprev[:], 0.0), inc="vec")
                for h in range(NH):
                    b.emit("scalar",
                           _dma_nc(mprev[h * 16 + 1:h * 16 + 16, 0:1],
                                   mcc[h * 16:h * 16 + 15, 127:128]),
                           waits=[f_gt1, ("vec", t_mpz)], inc="dmp", n=16)
                f_mprev = b.fence("dmp")
                t_wst = b.emit("scalar",
                               lambda eng: eng.activation(wst[:], gch[:], AF.Exp, bias=negMT[:]),
                               waits=[("vec", t_g), ("vec", t_negMT)], inc="sca")
                t_E = b.emit("scalar",
                             lambda eng: eng.activation(Ee[:], Mch[:], AF.Exp, bias=mprev[:], scale=-1.0),
                             waits=[("vec", t_M), f_mprev], inc="sca")
                t_enm = b.emit("scalar",
                               lambda eng: eng.activation(enm[:], mcc[:], AF.Exp, scale=-1.0),
                               waits=[f_gt1], inc="sca")


                projs = [(wq_t, 8, qT_d, "copy"), (wk_t, 8, kT_d, "copy"),
                         (wv_t, 16, vT_d, "copy"), (wog_t, 16, sgT_d, "sigmoid")]
                wbuf_busy = {}
                wsplit_busy = {}
                gidx = 0
                for w_src, n_tiles, dst, mode in projs:
                    for m in range(n_tiles):
                        wp = gidx % 2
                        wb = 0
                        sp = 0
                        gidx += 1
                        b.emit("sync", _dma(wbuf[wb][:], w_src[m]),
                               waits=[wbuf_busy.get(wb)], inc="dw0", n=16)
                        f_w = b.fence("dw0")
                        b.emit("vector",
                               lambda eng, o=wh[wp][:], i=wbuf[wb][:]: eng.tensor_copy(o, i),
                               waits=[f_w, wsplit_busy.get(wp)], inc="vec")
                        t_wsp = b.emit("vector",
                                       lambda eng, o=wl[wp][:], i0=wbuf[wb][:], i1=wh[wp][:]:
                                       eng.tensor_tensor(o, i0, i1, ALU.subtract),
                                       waits=[], inc="vec")
                        wbuf_busy[wb] = ("vec", t_wsp)
                        evac_toks = []
                        for n in range(4):
                            pp = n % 2
                            tok = None
                            for k in range(NKT):
                                waits = [xsp_tok[k]]
                                if k == 0:
                                    waits += ps_wait(pp) + [("vec", t_wsp), b.fence("ds1")]
                                rh = xh_sb[:, k * S + n * 512: k * S + n * 512 + 512]
                                rl = xl_sb[:, k * S + n * 512: k * S + n * 512 + 512]
                                lh = wh[wp][:, k * 128:(k + 1) * 128]
                                ll = wl[wp][:, k * 128:(k + 1) * 128]
                                b.emit("tensor",
                                       lambda eng, o=ps[pp][:, 0:512], l=lh, r=rh, st_=(k == 0):
                                       eng.matmul(o, l, r, start=st_, stop=False),
                                       waits=waits)
                                b.emit("tensor",
                                       lambda eng, o=ps[pp][:, 0:512], l=lh, r=rl:
                                       eng.matmul(o, l, r, start=False, stop=False))
                                tok = b.emit("tensor",
                                             lambda eng, o=ps[pp][:, 0:512], l=ll, r=rh,
                                             sp_=(k == NKT - 1):
                                             eng.matmul(o, l, r, start=False, stop=sp_),
                                             inc="pe" if k == NKT - 1 else None)
                            wsplit_busy[wp] = ("pe", tok)
                            dstr = strip[sp][:, n * 512:(n + 1) * 512]
                            if mode == "sigmoid":
                                te = b.emit("scalar",
                                            lambda eng, o=dstr, i=ps[pp][:, 0:512]:
                                            eng.activation(o, i, AF.Sigmoid),
                                            waits=[("pe", tok)], inc="sca")
                                bank_free[pp] = ("sca", te)
                            elif pp == 0:
                                te = b.emit("scalar",
                                            lambda eng, o=dstr, i=ps[pp][:, 0:512]: eng.copy(o, i),
                                            waits=[("pe", tok)], inc="sca")
                                bank_free[pp] = ("sca", te)
                            else:
                                te = b.emit("vector",
                                            lambda eng, o=dstr, i=ps[pp][:, 0:512]: eng.tensor_copy(o, i),
                                            waits=[("pe", tok)], inc="vec")
                                bank_free[pp] = ("vec", te)
                            evac_toks.append(bank_free[pp])
                        b.emit("gpsimd", _dma(dst[m * 128:(m + 1) * 128, :], strip[sp][:]),
                               waits=evac_toks, inc="ds1", n=16)

                wo_sp_busy = None
                for k in range(NKT):
                    b.emit("sync", _dma(wbuf[0][:], woutT_d[k * 128:(k + 1) * 128, :]),
                           waits=[wbuf_busy.get(0)], inc="dwx", n=16)
                    f_wx = b.fence("dwx")
                    b.emit("vector",
                           lambda eng, o=wh[0][:], i=wbuf[0][:]: eng.tensor_copy(o, i),
                           waits=[f_wx, wo_sp_busy, wsplit_busy.get(0) if k == 0 else None], inc="vec")
                    t_wsp2 = b.emit("vector",
                                    lambda eng, o=wl[0][:], i0=wbuf[0][:], i1=wh[0][:]:
                                    eng.tensor_tensor(o, i0, i1, ALU.subtract),
                                    waits=[], inc="vec")
                    wbuf_busy[0] = ("vec", t_wsp2)
                    b.emit("gpsimd", _dma(woh_d[k], wh[0][:]), waits=[("vec", t_wsp2)], inc="dsw", n=16)
                    b.emit("gpsimd", _dma(wol_d[k], wl[0][:]), waits=[("vec", t_wsp2)], inc="dsw", n=16)
                    wo_sp_busy = b.fence("dsw")
                f_wodram = b.fence("dsw")

                tr_tok = {}
                for idx, (src, dst, wtok) in enumerate([(wst, wT, ("sca", t_wst)),
                                                        (Ee, ET, ("sca", t_E)),
                                                        (enm, enmT, ("sca", t_enm))]):
                    bank = 6 + (idx % 2)
                    tok = b.emit("tensor",
                                 lambda eng, o=ps[bank][:, 0:128], i=src[:]:
                                 eng.transpose(o, i, ident[:]),
                                 waits=ps_wait(bank) + [wtok, f_cst1], inc="pe")
                    te = b.emit("scalar",
                                lambda eng, o=dst[:], i=ps[bank][:, 0:128]: eng.copy(o, i),
                                waits=[("pe", tok)], inc="sca")
                    bank_free[bank] = ("sca", te)
                    tr_tok[dst.name] = ("sca", te)
                tok_wT, tok_ET, tok_enmT = tr_tok["wT_sb"], tr_tok["ET_sb"], tr_tok["enmT_sb"]

                t_EscT = b.emit("vector",
                                lambda eng: eng.tensor_scalar(EscT[:], ET[:], QSCALE, None, ALU.mult),
                                waits=[tok_ET], inc="vec")
                b.emit("sync", _dma(decrow[:], ET[127:128, :]), waits=[tok_ET], inc="ddc", n=16)
                tok = b.emit("tensor",
                             lambda eng: eng.matmul(ps[6][:, 0:128], ones_sb[0:1, 0:128], decrow[:],
                                                    start=True, stop=True),
                             waits=ps_wait(6) + [b.fence("ddc"), f_cst1], inc="pe")
                t_decb = b.emit("scalar",
                                lambda eng: eng.copy(decb[:], ps[6][:, 0:128]),
                                waits=[("pe", tok)], inc="sca")
                bank_free[6] = ("sca", t_decb)

                b.flush(blk, sems, lastw)

            b.barrier()

            with contextlib.ExitStack() as st:
                qc = [st.enter_context(nc.sbuf_tensor(f"qc{i}", [128, NH * 128], F32)) for i in range(2)]
                kc = [st.enter_context(nc.sbuf_tensor(f"kc{i}", [128, NH * 128], F32)) for i in range(2)]
                vTc = [st.enter_context(nc.sbuf_tensor(f"vTc{i}", [128, 2 * NH * 128], F32)) for i in range(2)]
                sgc = [st.enter_context(nc.sbuf_tensor(f"sgc{i}", [128, 2 * NH * 128], F32)) for i in range(2)]
                glhs = [st.enter_context(nc.sbuf_tensor(f"glhs{i}", [2, NH * 128], F32)) for i in range(2)]
                grhs = [st.enter_context(nc.sbuf_tensor(f"grhs{i}", [2, NH * 128], F32)) for i in range(2)]
                CA = st.enter_context(nc.sbuf_tensor("CA_sb", [128, 2 * NH * 264], F32))
                VA = [st.enter_context(nc.sbuf_tensor(f"VA{i}", [128, 264], F32)) for i in range(2)]
                kw = [st.enter_context(nc.sbuf_tensor(f"kw{i}", [128, 128], F32)) for i in range(2)]
                wm = [st.enter_context(nc.sbuf_tensor(f"wm{i}", [128, 128], F32)) for i in range(2)]
                we = [st.enter_context(nc.sbuf_tensor(f"we{i}", [128, 128], F32)) for i in range(2)]
                sw = [st.enter_context(nc.sbuf_tensor(f"sw{i}", [128, 128], F32)) for i in range(2)]
                bsb = [st.enter_context(nc.sbuf_tensor(f"bs{i}", [128, 264], F32)) for i in range(2)]
                hsb = [st.enter_context(nc.sbuf_tensor(f"hs{i}", [128, 264], F32)) for i in range(2)]
                hf = [st.enter_context(nc.sbuf_tensor(f"hf{i}", [128, 256], F32)) for i in range(2)]
                hd = [st.enter_context(nc.sbuf_tensor(f"hd{i}", [128, 256], F32)) for i in range(2)]
                hn = [st.enter_context(nc.sbuf_tensor(f"hn{i}", [128, 256], F32)) for i in range(2)]
                scr = st.enter_context(nc.sbuf_tensor("scr_sb", [128, 16], F32))
                junk = st.enter_context(nc.sbuf_tensor("junk2_sb", [128, 256], F32))
                dnst = [st.enter_context(nc.sbuf_tensor(f"dnst{i}", [128, 4096], F32)) for i in range(2)]
                epsc = st.enter_context(nc.sbuf_tensor("epsc_sb", [128, 1], F32))
                dnsg = [st.enter_context(nc.sbuf_tensor(f"dnsg{i}", [128, 4096], BF16)) for i in range(2)]
                hout = [st.enter_context(nc.sbuf_tensor(f"hout{i}", [128, 2 * NH * 128], F32)) for i in range(2)]

                def ca(par, h):
                    off = (par * NH + h) * 264
                    return CA[:, off:off + 257]

                t_caz = b.emit("vector", lambda eng: eng.memset(CA[:], 0.0), inc="vec")
                t_eps = b.emit("vector", lambda eng: eng.memset(epsc[:], EPS_MH), inc="vec")
                t_va0 = b.emit("vector", lambda eng: eng.memset(VA[0][:, 256:264], 1.0), inc="vec")
                t_va1 = b.emit("vector", lambda eng: eng.memset(VA[1][:, 256:264], 1.0), inc="vec")
                t_gl0 = b.emit("vector", lambda eng: eng.memset(glhs[0][0:1, :], 1.0), inc="vec")
                t_gl1 = b.emit("vector", lambda eng: eng.memset(glhs[1][0:1, :], 1.0), inc="vec")
                b.emit("sync", _dma(grhs[0][1:2, :], ones_d), inc="dgr", n=16)
                b.emit("sync", _dma(grhs[1][1:2, :], ones_d), inc="dgr", n=16)
                f_cst2 = b.fence("dgr")

                chunk_busy = {}
                c_tok = {h: ("vec", t_caz) for h in range(NH)}
                va_busy = {0: ("vec", t_va0), 1: ("vec", t_va1)}
                kw_busy = {}
                sbusy = {}
                gl_busy = {0: ("vec", t_gl0), 1: ("vec", t_gl1)}
                gr_busy = {}
                ld = {}

                dn_busy = {}
                dnsg_busy = {}
                for c in range(NCH):
                    pc = c % 2
                    csl = slice(c * 128, (c + 1) * 128)
                    for half2 in range(2):
                        sidx = 2 * c + half2
                        ft2, hh2 = sidx // 2, sidx % 2
                        dnp = sidx % 2
                        dns = "dnl" if dnp == 0 else "dsq"
                        b.emit("sync", _dma(dnst[dnp][:], dn_t[ft2, :, hh2 * 4096:(hh2 + 1) * 4096]),
                               waits=[dn_busy.get(dnp)], inc=dns, n=16)
                        f_dn = b.fence(dns)
                        t_sgn = b.emit("scalar",
                                       lambda eng, o=dnsg[dnp][:], i=dnst[dnp][:]:
                                       eng.activation(o, i, AF.Sign),
                                       waits=[f_dn, dnsg_busy.get(dnp)], inc="sca")
                        dn_busy[dnp] = ("sca", t_sgn)
                        b.emit("gpsimd",
                               _dma(dnsg_d[ft2, :, hh2 * 4096:(hh2 + 1) * 4096], dnsg[dnp][:]),
                               waits=[("sca", t_sgn)], inc="dsg", n=16)
                        dnsg_busy[dnp] = b.fence("dsg")
                    dcp = "dc0" if pc == 0 else "dc1"
                    b.emit("sync",
                           _dma(qc[pc][:].rearrange("d (hh t) -> d hh t", hh=8),
                                qT_d[:, csl].rearrange("(hh d) t -> d hh t", d=128)),
                           waits=[chunk_busy.get(("q", pc))], inc=dcp, n=16)
                    b.emit("sync",
                           _dma(kc[pc][:].rearrange("d (hh t) -> d hh t", hh=8),
                                kT_d[:, csl].rearrange("(hh d) t -> d hh t", d=128)),
                           waits=[chunk_busy.get(("k", pc))], inc=dcp, n=16)
                    b.emit("sync",
                           _dma(vTc[pc][:].rearrange("d (hh t) -> d hh t", hh=16),
                                vT_d[:, csl].rearrange("(hh d) t -> d hh t", d=128)),
                           waits=[chunk_busy.get(("v", pc))], inc=dcp, n=16)
                    b.emit("sync",
                           _dma(sgc[pc][:].rearrange("d (hh t) -> d hh t", hh=16),
                                sgT_d[:, csl].rearrange("(hh d) t -> d hh t", d=128)),
                           waits=[chunk_busy.get(("sg", pc))], inc=dcp, n=16)
                    b.emit("sync",
                           _dma(glhs[pc][1:2, :].rearrange("o (h t) -> o h t", h=8),
                                gch_d.rearrange("(h c) t -> c h t", c=16)[c:c + 1]),
                           waits=[gl_busy.get(pc)], inc=dcp, n=16)
                    b.emit("sync",
                           _dma(grhs[pc][0:1, :].rearrange("o (h t) -> o h t", h=8),
                                negM_d.rearrange("(h c) t -> c h t", c=16)[c:c + 1]),
                           waits=[gr_busy.get(pc), f_cst2], inc=dcp, n=16)
                    f_c = b.fence(dcp)

                    hb = c % 2
                    cpar = c % 2
                    hout_evac_last = None

                    def stageA(h):
                        hc = h * 16 + c
                        hp = h % 2
                        hsl = slice(h * 128, (h + 1) * 128)
                        bt = 6 + hp
                        bS = 2 + hp
                        b.emit("tensor",
                               lambda eng, o=ps[bt][:, 0:128], i=kc[pc][:, hsl]:
                               eng.transpose(o, i, ident[:]),
                               waits=ps_wait(bt) + [f_c], inc=None)
                        b.emit("tensor",
                               lambda eng, o=ps[bt][:, 128:256], i=vTc[pc][:, (2 * h) * 128:(2 * h + 1) * 128]:
                               eng.transpose(o, i, ident[:]))
                        tok = b.emit("tensor",
                                     lambda eng, o=ps[bt][:, 256:384], i=vTc[pc][:, (2 * h + 1) * 128:(2 * h + 2) * 128]:
                                     eng.transpose(o, i, ident[:]),
                                     inc="pe")
                        tokS = b.emit("tensor",
                                      lambda eng, o=ps[bS][:, 0:128], l=kc[pc][:, hsl], r=qc[pc][:, hsl]:
                                      eng.matmul(o, l, r, start=True, stop=True),
                                      waits=ps_wait(bS) + [f_c], inc=None)
                        tokW = b.emit("tensor",
                                      lambda eng, o=ps[bS][:, 128:256], l=glhs[pc][0:2, hsl], r=grhs[pc][0:2, hsl]:
                                      eng.matmul(o, l, r, start=True, stop=True),
                                      waits=[("vec", t_gl0 if pc == 0 else t_gl1)],
                                      inc="pe")
                        te = b.emit("scalar",
                                    lambda eng, o=kw[hp][:], i=ps[bt][:, 0:128], w=wT[:, hc:hc + 1]:
                                    eng.mul(o, i, w),
                                    waits=[("pe", tok), tok_wT, kw_busy.get(hp)], inc="sca")
                        te2 = b.emit("scalar",
                                     lambda eng, o=VA[hp][:, 0:256], i=ps[bt][:, 128:384]: eng.copy(o, i),
                                     waits=[va_busy.get(hp)], inc="sca")
                        tv = b.emit("vector",
                                    lambda eng, o=wm[hp][:], i0=ps[bS][:, 128:256]:
                                    eng.tensor_add(o, i0, cmask[:]),
                                    waits=[("pe", tokW), f_cst1, sbusy.get(("wm", hp))], inc="vec")
                        ta = b.emit("scalar",
                                    lambda eng, o=we[hp][:], i=wm[hp][:]: eng.activation(o, i, AF.Exp),
                                    waits=[("vec", tv), sbusy.get(("we", hp))], inc="sca")
                        tsw = b.emit("vector",
                                     lambda eng, o=sw[hp][:], i0=ps[bS][:, 0:128], w=we[hp][:]:
                                     eng.scalar_tensor_tensor(o, i0, QSCALE, w, ALU.mult, ALU.mult),
                                     waits=[("sca", ta), sbusy.get(("sw", hp))], inc="vec")
                        bank_free[bS] = ("vec", tsw)
                        sbusy[("wm", hp)] = ("sca", ta)
                        return dict(hc=hc, hp=hp, hsl=hsl, bt=bt,
                                    tok_kw=("sca", te), tok_va=("sca", te2),
                                    stage1_free=("sca", te2), tsw=tsw)

                    def stageB(st1):
                        nonlocal hout_evac_last
                        hc, hp, hsl, bt = st1["hc"], st1["hp"], st1["hsl"], st1["bt"]
                        h = hc // 16
                        tok_kw, tok_va, stage1_free, tsw = (st1["tok_kw"], st1["tok_va"],
                                                            st1["stage1_free"], st1["tsw"])
                        tokA = b.emit("tensor",
                                      lambda eng, o=ps[4][:, 0:257], l=sw[hp][:], r=VA[hp][:, 0:257]:
                                      eng.matmul(o, l, r, start=True, stop=True),
                                      waits=ps_wait(4) + [("vec", tsw), tok_va], inc="pe")
                        tokB = b.emit("tensor",
                                      lambda eng, o=ps[5][:, 0:257], l=qc[pc][:, hsl], r=ca(cpar, h):
                                      eng.matmul(o, l, r, start=True, stop=True),
                                      waits=ps_wait(5) + [f_c, c_tok[h]], inc="pe")
                        cb = hp
                        tokC = b.emit("tensor",
                                      lambda eng, o=ps[cb][:, 0:257], l=kw[hp][:], r=VA[hp][:, 0:257]:
                                      eng.matmul(o, l, r, start=True, stop=True),
                                      waits=ps_wait(cb) + [tok_kw, tok_va], inc="pe")
                        kw_busy[hp] = ("pe", tokC)
                        va_busy[hp] = ("pe", tokC)
                        tb = b.emit("scalar",
                                    lambda eng, o=bsb[hp][:, 0:257], i=ps[5][:, 0:257], e=EscT[:, hc:hc + 1]:
                                    eng.mul(o, i, e),
                                    waits=[("pe", tokB), ("vec", t_EscT), sbusy.get(("bs", hp))], inc="sca")
                        bank_free[5] = ("sca", tb)
                        th = b.emit("vector",
                                    lambda eng, o=hsb[hp][:, 0:257], a=ps[4][:, 0:257], bb=bsb[hp][:, 0:257]:
                                    eng.tensor_add(o, a, bb),
                                    waits=[("pe", tokA), ("sca", tb), sbusy.get(("hs", hp))], inc="vec")
                        bank_free[4] = ("vec", th)
                        sbusy[("bs", hp)] = ("vec", th)
                        tcu = b.emit("vector",
                                     lambda eng, o=ca(1 - cpar, h), i0=ca(cpar, h), dcol=decb[:, hc:hc + 1],
                                     p=ps[cb][:, 0:257]:
                                     eng.scalar_tensor_tensor(o, i0, dcol, p, ALU.mult, ALU.add),
                                     waits=[("pe", tokC), ("sca", t_decb), c_tok[h]], inc="vec")
                        bank_free[cb] = ("vec", tcu)
                        c_tok[h] = ("vec", tcu)
                        td0 = b.emit("scalar",
                                     lambda eng, o=scr[:, hp * 8:hp * 8 + 1], i=hsb[hp][:, 256:257]:
                                     eng.activation(o, i, AF.Abs),
                                     waits=[("vec", th)], inc="sca")
                        td2 = b.emit("vector",
                                     lambda eng, o=scr[:, hp * 8 + 2:hp * 8 + 3], i=scr[:, hp * 8:hp * 8 + 1],
                                     e=enmT[:, hc:hc + 1]:
                                     eng.tensor_scalar(o, i, e, EPS_MH, ALU.max, ALU.add),
                                     waits=[("sca", td0), tok_enmT], inc="vec")
                        td3 = b.emit("vector",
                                     lambda eng, o=scr[:, hp * 8 + 3:hp * 8 + 4], i=scr[:, hp * 8 + 2:hp * 8 + 3]:
                                     eng.reciprocal(o, i),
                                     waits=[("vec", td2)], inc="vec")
                        tf = b.emit("vector",
                                    lambda eng, o=hf[hp][:], i=hsb[hp][:, 0:256], r=scr[:, hp * 8 + 3:hp * 8 + 4]:
                                    eng.tensor_scalar(o, i, r, None, ALU.mult),
                                    waits=[("vec", td3), sbusy.get(("hf", hp))], inc="vec")
                        sbusy[("hs", hp)] = ("vec", tf)
                        tm1 = b.emit("vector",
                                     lambda eng, o=scr[:, hp * 8 + 4:hp * 8 + 5], i=hf[hp][:]:
                                     eng.reduce_sum(o, i, axis=AXX),
                                     waits=[("vec", tf)], inc="vec")
                        tm2 = b.emit("vector",
                                     lambda eng, o=scr[:, hp * 8 + 5:hp * 8 + 6], i=scr[:, hp * 8 + 4:hp * 8 + 5]:
                                     eng.tensor_scalar(o, i, -1.0 / 256.0, None, ALU.mult),
                                     waits=[("vec", tm1)], inc="vec")
                        tm4 = b.emit("scalar",
                                     lambda eng, o=junk[:, 0:256], i=hf[hp][:], nm=scr[:, hp * 8 + 5:hp * 8 + 6],
                                     a=scr[:, hp * 8 + 6:hp * 8 + 7]:
                                     eng.activation(o, i, AF.Square, bias=nm, accum_out=a),
                                     waits=[("vec", tm2)], inc="sca")
                        tm6 = b.emit("scalar",
                                     lambda eng, o=scr[:, hp * 8 + 6:hp * 8 + 7], i=scr[:, hp * 8 + 6:hp * 8 + 7]:
                                     eng.activation(o, i, AF.Ln, bias=epsc[:], scale=1.0 / 256.0),
                                     waits=[("sca", tm4), ("vec", t_eps)], inc="sca")
                        tm7 = b.emit("scalar",
                                     lambda eng, o=scr[:, hp * 8 + 6:hp * 8 + 7], i=scr[:, hp * 8 + 6:hp * 8 + 7]:
                                     eng.activation(o, i, AF.Exp, scale=-0.5),
                                     waits=[("sca", tm6)], inc="sca")
                        tm8 = b.emit("vector",
                                     lambda eng, o=hn[hp][:], i=hf[hp][:], nm=scr[:, hp * 8 + 5:hp * 8 + 6],
                                     r=scr[:, hp * 8 + 6:hp * 8 + 7]:
                                     eng.tensor_scalar(o, i, nm, r, ALU.add, ALU.mult),
                                     waits=[("sca", tm7), sbusy.get(("hn", hp))], inc="vec")
                        sbusy[("hf", hp)] = ("vec", tm8)
                        b.emit("tensor",
                               lambda eng, o=ps[bt][:, 0:128], i=hn[hp][:, 0:128]:
                               eng.transpose(o, i, ident[:]),
                               waits=[stage1_free, ("vec", tm8)], inc=None)
                        tok = b.emit("tensor",
                                     lambda eng, o=ps[bt][:, 128:256], i=hn[hp][:, 128:256]:
                                     eng.transpose(o, i, ident[:]),
                                     inc="pe")
                        f_dsh = b.fence("dsh0" if hb == 0 else "dsh1") if c >= 2 else None
                        hel = None
                        for half in range(2):
                            col = 2 * h + half
                            hel = b.emit("vector",
                                         lambda eng, o=hout[hb][:, col * 128:(col + 1) * 128],
                                         i0=ps[bt][:, half * 128:(half + 1) * 128],
                                         g=gam_sb[:, col:col + 1],
                                         s2=sgc[pc][:, col * 128:(col + 1) * 128]:
                                         eng.scalar_tensor_tensor(o, i0, g, s2, ALU.mult, ALU.mult),
                                         waits=[("pe", tok), f_cst1, f_c, f_dsh],
                                         inc="vec")
                        bank_free[bt] = ("vec", hel)
                        sbusy[("hn", hp)] = ("vec", hel)
                        sbusy[("sw", hp)] = ("pe", tokA)
                        sbusy[("we", hp)] = ("vec", tsw)
                        hout_evac_last = hel

                    prev = None
                    for h in range(NH):
                        cur = stageA(h)
                        if prev is not None:
                            stageB(prev)
                        prev = cur
                    stageB(prev)
                    b.emit("gpsimd",
                           _dma(hoT_d[:, csl].rearrange("(hh d) t -> d hh t", d=128),
                                hout[hb][:].rearrange("d (hh t) -> d hh t", hh=16)),
                           waits=[("vec", hout_evac_last)], inc="dsh0" if hb == 0 else "dsh1", n=16)
                    gl_busy[pc] = ("pe", b.cnt["pe"])
                    gr_busy[pc] = ("pe", b.cnt["pe"])
                    chunk_busy[("q", pc)] = ("pe", b.cnt["pe"])
                    chunk_busy[("k", pc)] = ("pe", b.cnt["pe"])
                    chunk_busy[("v", pc)] = ("pe", b.cnt["pe"])
                    chunk_busy[("sg", pc)] = ("vec", b.cnt["vec"])
                f_dnsg = b.fence("dsg")
                f_hout0 = b.fence("dsh0")
                f_hout1 = b.fence("dsh1")
                b.flush(blk, sems, lastw)
            g_st.close()

            b.barrier()

            with contextlib.ExitStack() as st:
                woh_sb = st.enter_context(nc.sbuf_tensor("woh_sb", [128, NKT * S], BF16))
                wol_sb = st.enter_context(nc.sbuf_tensor("wol_sb", [128, NKT * S], BF16))
                hoT = st.enter_context(nc.sbuf_tensor("hoT_sb", [128, S], F32))
                lnhTt = st.enter_context(nc.sbuf_tensor("lnhTt_sb", [128, S], BF16))
                lnlTt = st.enter_context(nc.sbuf_tensor("lnlTt_sb", [128, S], BF16))
                hoh = [st.enter_context(nc.sbuf_tensor(f"hoh{i}", [128, 2048], BF16)) for i in range(2)]
                hol = [st.enter_context(nc.sbuf_tensor(f"hol{i}", [128, 2048], BF16)) for i in range(2)]
                xres = [st.enter_context(nc.sbuf_tensor("xres0", [128, 2048], F32))]
                ztile2 = [st.enter_context(nc.sbuf_tensor(f"ztile{i}", [128, 2048], F32)) for i in range(2)]
                lnt = st.enter_context(nc.sbuf_tensor("lnt_sb", [128, 2048], F32))
                lnTt = st.enter_context(nc.sbuf_tensor("lnTt_sb", [128, 2048], F32))
                cols = st.enter_context(nc.sbuf_tensor("cols_sb", [128, 4], F32))
                junk = lnt

                wo_tok = {}
                for k in range(NKT):
                    b.emit("sync", _dma(woh_sb[:, k * S:(k + 1) * S], woh_d[k]),
                           waits=[f_wodram], inc="dwo", n=16)
                    b.emit("sync", _dma(wol_sb[:, k * S:(k + 1) * S], wol_d[k]),
                           waits=[f_wodram], inc="dwo", n=16)
                    wo_tok[k] = b.fence("dwo")
                f_wo = wo_tok[NKT - 1]

                ho_busy = {}
                ho_split_busy = {}
                x_busy = {}
                tr_busy = None
                lnTt_busy = None
                ztoks_by = {}
                def p3A(m):
                    mp = m % 2
                    dpp = "dp0" if mp == 0 else "dp1"
                    b.emit("sync",
                           _dma(hoT[:].rearrange("p (fb t) -> p fb t", fb=16),
                                hoT_d[:, m * 128:(m + 1) * 128]
                                .rearrange("(fb p) t -> p fb t", p=128)),
                           waits=[f_hout0, f_hout1, ho_busy.get(0) or f_wo], inc=dpp, n=16)
                    b.emit("sync", _dma(xres[0][:], xn_d[m * 128:(m + 1) * 128, :]),
                           waits=[x_busy.get(0)], inc=dpp, n=16)
                    f_p = b.fence(dpp)
                    b.emit("vector",
                           lambda eng, o=hoh[mp][:], i=hoT[:]: eng.tensor_copy(o, i),
                           waits=[f_p, ho_split_busy.get(mp)], inc="vec")
                    t_hsp = b.emit("vector",
                                   lambda eng, o=hol[mp][:], i0=hoT[:], i1=hoh[mp][:]:
                                   eng.tensor_tensor(o, i0, i1, ALU.subtract),
                                   waits=[], inc="vec")
                    ho_busy[0] = ("vec", t_hsp)
                    ztoks = []
                    for n in range(4):
                        pp = n % 2
                        tok = None
                        for k in range(NKT):
                            waits = [wo_tok[k]]
                            if k == 0:
                                waits += ps_wait(pp) + [("vec", t_hsp)]
                            lh = hoh[mp][:, k * 128:(k + 1) * 128]
                            ll = hol[mp][:, k * 128:(k + 1) * 128]
                            rh = woh_sb[:, k * S + n * 512: k * S + n * 512 + 512]
                            rl = wol_sb[:, k * S + n * 512: k * S + n * 512 + 512]
                            b.emit("tensor",
                                   lambda eng, o=ps[pp][:, 0:512], l=lh, r=rh, st_=(k == 0):
                                   eng.matmul(o, l, r, start=st_, stop=False),
                                   waits=waits)
                            b.emit("tensor",
                                   lambda eng, o=ps[pp][:, 0:512], l=lh, r=rl:
                                   eng.matmul(o, l, r, start=False, stop=False))
                            tok = b.emit("tensor",
                                         lambda eng, o=ps[pp][:, 0:512], l=ll, r=rh,
                                         sp_=(k == NKT - 1):
                                         eng.matmul(o, l, r, start=False, stop=sp_),
                                         inc="pe" if k == NKT - 1 else None)
                        tz = b.emit("vector",
                                    lambda eng, o=ztile2[m % 2][:, n * 512:(n + 1) * 512], a=ps[pp][:, 0:512],
                                    x=xres[0][:, n * 512:(n + 1) * 512]:
                                    eng.tensor_add(o, a, x),
                                    waits=[("pe", tok), f_p], inc="vec")
                        bank_free[pp] = ("vec", tz)
                        ztoks.append(("vec", tz))
                    ho_split_busy[mp] = ("pe", b.cnt["pe"])
                    x_busy[0] = ztoks[-1]
                    ztoks_by[m] = ztoks
                    return m

                def p3B(m):
                    nonlocal tr_busy, lnTt_busy
                    l1 = b.emit("vector",
                                lambda eng, z=None: eng.reduce_sum(cols[:, 0:1], ztile2[m % 2][:], axis=AXX),
                                waits=ztoks_by[m], inc="vec")
                    l2 = b.emit("vector",
                                lambda eng: eng.tensor_scalar(cols[:, 0:1], cols[:, 0:1], 1.0 / D, None, ALU.mult),
                                waits=[("vec", l1)], inc="vec")
                    l3 = b.emit("vector",
                                lambda eng: eng.tensor_scalar(ztile2[m % 2][:], ztile2[m % 2][:], cols[:, 0:1], None, ALU.subtract),
                                waits=[("vec", l2)], inc="vec")
                    l4 = b.emit("scalar",
                                lambda eng: eng.activation(junk[:], ztile2[m % 2][:], AF.Square, accum_out=cols[:, 1:2]),
                                waits=[("vec", l3), tr_busy, b.fence("dsn")], inc="sca")
                    l5 = b.emit("vector",
                                lambda eng: eng.tensor_scalar(cols[:, 1:2], cols[:, 1:2], 1.0 / D, EPS_LN,
                                                              ALU.mult, ALU.add),
                                waits=[("sca", l4)], inc="vec")
                    l6 = b.emit("scalar",
                                lambda eng: eng.activation(cols[:, 2:3], cols[:, 1:2], AF.Ln),
                                waits=[("vec", l5)], inc="sca")
                    l7 = b.emit("scalar",
                                lambda eng: eng.activation(cols[:, 2:3], cols[:, 2:3], AF.Exp, scale=-0.5),
                                waits=[("sca", l6)], inc="sca")
                    l8 = b.emit("vector",
                                lambda eng: eng.tensor_scalar(lnt[:], ztile2[m % 2][:], cols[:, 2:3], None, ALU.mult),
                                waits=[("sca", l7), b.fence("dsn"), tr_busy], inc="vec")
                    b.emit("gpsimd", _dma(ln1n_d[m * 128:(m + 1) * 128, :], lnt[:]),
                           waits=[("vec", l8)], inc="dsn", n=16)
                    trtoks = []
                    for fb in range(NKT):
                        bank = 6 if fb % 2 == 0 else 7
                        tok = b.emit("tensor",
                                     lambda eng, o=ps[bank][:, 0:128], i=lnt[:, fb * 128:(fb + 1) * 128]:
                                     eng.transpose(o, i, ident[:]),
                                     waits=ps_wait(bank) + [("vec", l8), tr_busy, lnTt_busy], inc="pe")
                        te = b.emit("scalar",
                                    lambda eng, o=lnTt[:, fb * 128:(fb + 1) * 128], i=ps[bank][:, 0:128]:
                                    eng.copy(o, i),
                                    waits=[("pe", tok)], inc="sca")
                        bank_free[bank] = ("sca", te)
                        trtoks.append(("sca", te))
                    tr_busy = ("pe", b.cnt["pe"])
                    b.emit("vector",
                           lambda eng, o=lnhTt[:], i=lnTt[:]: eng.tensor_copy(o, i),
                           waits=trtoks + [b.fence("dst")], inc="vec")
                    t_lsp = b.emit("vector",
                                   lambda eng, o=lnlTt[:], i0=lnTt[:], i1=lnhTt[:]:
                                   eng.tensor_tensor(o, i0, i1, ALU.subtract),
                                   waits=[], inc="vec")
                    lnTt_busy = ("vec", t_lsp)
                    b.emit("gpsimd",
                           _dma(lnh_d.rearrange("fb p t -> p fb t")[:, :, m * 128:(m + 1) * 128],
                                lnhTt[:].rearrange("p (fb t) -> p fb t", fb=16)),
                           waits=[("vec", t_lsp)], inc="dst", n=16)
                    b.emit("gpsimd",
                           _dma(lnl_d.rearrange("fb p t -> p fb t")[:, :, m * 128:(m + 1) * 128],
                                lnlTt[:].rearrange("p (fb t) -> p fb t", fb=16)),
                           waits=[("vec", t_lsp)], inc="dst", n=16)

                prevm = None
                for m in range(NTT):
                    p3A(m)
                    if prevm is not None:
                        p3B(prevm)
                    prevm = m
                p3B(prevm)
                f_ln = b.fence("dst")
                f_lnn = b.fence("dsn")
                b.flush(blk, sems, lastw)

            b.barrier()

            with contextlib.ExitStack() as st:
                lnh_sb = st.enter_context(nc.sbuf_tensor("lnh_sb", [128, NKT * S], BF16))
                lnl_sb = st.enter_context(nc.sbuf_tensor("lnl_sb", [128, NKT * S], BF16))
                uw = [st.enter_context(nc.sbuf_tensor(f"uw{i}", [128, D], F32)) for i in range(2)]
                usg = [st.enter_context(nc.sbuf_tensor(f"usg{i}", [128, D], BF16)) for i in range(2)]
                hidf = [st.enter_context(nc.sbuf_tensor("hidf0", [128, S], F32))]
                hidh = [st.enter_context(nc.sbuf_tensor(f"hidh{i}", [128, S], BF16)) for i in range(2)]
                hidl = [st.enter_context(nc.sbuf_tensor(f"hidl{i}", [128, S], BF16)) for i in range(2)]
                upsc_sb = st.enter_context(nc.sbuf_tensor("upsc_sb", [128, 64], F32))

                b.emit("sync",
                       _dma_nc(upsc_sb[:].rearrange("p (ht x) -> p ht x", x=1),
                               upsc_d.rearrange("(ht p) x -> p ht x", p=128)),
                       inc="dus", n=16)
                f_us = b.fence("dus")
                ln_tok = {}
                for k in range(NKT):
                    b.emit("sync", _dma(lnh_sb[:, k * S:(k + 1) * S], lnh_d[k]),
                           waits=[f_ln], inc="dlt", n=16)
                    b.emit("sync", _dma(lnl_sb[:, k * S:(k + 1) * S], lnl_d[k]),
                           waits=[f_ln], inc="dlt", n=16)
                    ln_tok[k] = b.fence("dlt")
                f_lt = ln_tok[NKT - 1]

                uw_busy = {}
                usg_busy = {}
                hidf_busy = {}
                for ht in range(64):
                    wp = ht % 2
                    dup = "du0" if wp == 0 else "du1"
                    b.emit("sync", _dma(uw[wp][:], up_t[ht]),
                           waits=[uw_busy.get(wp)], inc=dup, n=16)
                    t_sg2 = b.emit("scalar",
                                   lambda eng, o=usg[wp][:], i=uw[wp][:]: eng.activation(o, i, AF.Sign),
                                   waits=[b.fence(dup), usg_busy.get(wp)], inc="sca")
                    uw_busy[wp] = ("sca", t_sg2)
                    etoks = []
                    for n in range(4):
                        pp = n % 2
                        tok = None
                        for k in range(NKT):
                            waits = [ln_tok[k]]
                            if k == 0:
                                waits += ps_wait(pp) + [("sca", t_sg2), b.fence("dsd0" if wp == 0 else "dsd1")]
                            lg = usg[wp][:, k * 128:(k + 1) * 128]
                            rh = lnh_sb[:, k * S + n * 512: k * S + n * 512 + 512]
                            rl = lnl_sb[:, k * S + n * 512: k * S + n * 512 + 512]
                            b.emit("tensor",
                                   lambda eng, o=ps[pp][:, 0:512], l=lg, r=rh, st_=(k == 0):
                                   eng.matmul(o, l, r, start=st_, stop=False),
                                   waits=waits)
                            tok = b.emit("tensor",
                                         lambda eng, o=ps[pp][:, 0:512], l=lg, r=rl,
                                         sp_=(k == NKT - 1):
                                         eng.matmul(o, l, r, start=False, stop=sp_),
                                         inc="pe" if k == NKT - 1 else None)
                        nsl = slice(n * 512, (n + 1) * 512)
                        te = b.emit("scalar",
                                    lambda eng, o=hidf[0][:, nsl], i=ps[pp][:, 0:512],
                                    sc=upsc_sb[:, ht:ht + 1]:
                                    eng.activation(o, i, AF.Relu, scale=sc),
                                    waits=[("pe", tok), f_us, hidf_busy.get(0)], inc="sca")
                        bank_free[pp] = ("sca", te)
                        b.emit("vector",
                               lambda eng, o=hidh[wp][:, nsl], i=hidf[0][:, nsl]:
                               eng.tensor_copy(o, i),
                               waits=[("sca", te)], inc="vec")
                        tv3 = b.emit("vector",
                                     lambda eng, o=hidl[wp][:, nsl], i0=hidf[0][:, nsl],
                                     i1=hidh[wp][:, nsl]:
                                     eng.tensor_tensor(o, i0, i1, ALU.subtract),
                                     waits=[], inc="vec")
                        etoks.append(("vec", tv3))
                    usg_busy[wp] = ("pe", b.cnt["pe"])
                    hidf_busy[0] = etoks[-1]
                    dsp = "dsd0" if wp == 0 else "dsd1"
                    b.emit("gpsimd", _dma(hidT_h_d[ht * 128:(ht + 1) * 128, :], hidh[wp][:]),
                           waits=etoks, inc=dsp, n=16)
                    b.emit("gpsimd", _dma(hidT_l_d[ht * 128:(ht + 1) * 128, :], hidl[wp][:]),
                           waits=etoks, inc=dsp, n=16)
                f_hid = None
                f_hid0 = b.fence("dsd0")
                f_hid1 = b.fence("dsd1")
                b.flush(blk, sems, lastw)

            b.barrier()

            with contextlib.ExitStack() as st:
                hbh_sb = st.enter_context(nc.sbuf_tensor("hbh_sb", [128, 64 * 512], BF16))
                hbl_sb = st.enter_context(nc.sbuf_tensor("hbl_sb", [128, 64 * 512], BF16))
                dsg = [st.enter_context(nc.sbuf_tensor(f"dsg{i}", [128, 4096], BF16)) for i in range(4)]
                oT = [st.enter_context(nc.sbuf_tensor(f"oT{i}", [128, 512], F32)) for i in range(2)]
                dnsc_sb = st.enter_context(nc.sbuf_tensor("dnsc_sb", [128, 16], F32))
                dnsc2 = st.enter_context(nc.sbuf_tensor("dnsc2_sb", [128, 16], F32))
                osc_sb = st.enter_context(nc.sbuf_tensor("osc_sb", [1, 1], F32))
                oscc = st.enter_context(nc.sbuf_tensor("oscc_sb", [128, 1], F32))
                ob = [st.enter_context(nc.sbuf_tensor("ob0", [128, 2048], F32))]
                lb = [st.enter_context(nc.sbuf_tensor("lb0", [128, 2048], F32))]
                ftile = st.enter_context(nc.sbuf_tensor("ftile_sb", [128, 2048], F32))
                z2 = st.enter_context(nc.sbuf_tensor("z2_sb", [128, 2048], F32))
                z2d = z2
                otile = st.enter_context(nc.sbuf_tensor("otile_sb", [128, 2048], F32))
                cols2 = st.enter_context(nc.sbuf_tensor("cols2_sb", [128, 4], F32))
                junk = otile
                ob_busy = {}
                lb_busy = {}
                f_busy = None

                b.emit("sync",
                       _dma_nc(dnsc_sb[:].rearrange("p (ft x) -> p ft x", x=1),
                               dnsc_d.rearrange("(ft p) x -> p ft x", p=128)),
                       inc="dds", n=16)
                b.emit("sync", _dma(osc_sb[:], osc_d), inc="dds", n=16)
                f_cst5 = b.fence("dds")
                tok = b.emit("tensor",
                             lambda eng: eng.matmul(ps[6][:, 0:1], ones_sb[0:1, 0:128], osc_sb[:],
                                                    start=True, stop=True),
                             waits=ps_wait(6) + [f_cst5], inc="pe")
                t_ob = b.emit("scalar", lambda eng: eng.copy(oscc[:], ps[6][:, 0:1]),
                              waits=[("pe", tok)], inc="sca")
                bank_free[6] = ("sca", t_ob)
                t_d2 = b.emit("vector",
                              lambda eng: eng.tensor_scalar(dnsc2[:], dnsc_sb[:], oscc[:], None, ALU.mult),
                              waits=[f_cst5, ("sca", t_ob)], inc="vec")

                hb_busy = None
                dw_busy = {}
                dsg_busy = {}
                for tc in range(4):
                    hb_tok = {}
                    for kr in range(8):
                        b.emit("sync",
                               _dma(hbh_sb[:, kr * 8 * 512:(kr + 1) * 8 * 512]
                                    .rearrange("p (k t) -> p k t", k=8),
                                    hidT_h_d[kr * 1024:(kr + 1) * 1024, tc * 512:(tc + 1) * 512]
                                    .rearrange("(k p) t -> p k t", p=128)),
                               waits=[f_hid0, f_hid1, hb_busy], inc="dhb", n=16)
                        b.emit("sync",
                               _dma(hbl_sb[:, kr * 8 * 512:(kr + 1) * 8 * 512]
                                    .rearrange("p (k t) -> p k t", k=8),
                                    hidT_l_d[kr * 1024:(kr + 1) * 1024, tc * 512:(tc + 1) * 512]
                                    .rearrange("(k p) t -> p k t", p=128)),
                               waits=[f_hid0, f_hid1, hb_busy], inc="dhb", n=16)
                        hb_tok[kr] = b.fence("dhb")
                    for ft in range(16):
                        pp = ft % 2
                        fb4 = (ft % 2) * 2
                        half_toks = []
                        for hh in range(2):
                            ddh = f"dd{fb4 + hh}"
                            b.emit("sync",
                                   _dma(dsg[fb4 + hh][:], dnsg_d[ft, :, hh * 4096:(hh + 1) * 4096]),
                                   waits=[dsg_busy.get(fb4 + hh), f_dnsg], inc=ddh, n=16)
                            half_toks.append(b.fence(ddh))
                        tok = None
                        for k in range(64):
                            hh = k // 32
                            waits = [hb_tok[k // 8]] if k % 8 == 0 else []
                            if k == 0:
                                waits += ps_wait(pp) + [half_toks[0]]
                            if k == 32:
                                waits += [half_toks[1]]
                            lg = dsg[fb4 + hh][:, (k % 32) * 128:(k % 32 + 1) * 128]
                            b.emit("tensor",
                                   lambda eng, o=ps[pp][:, 0:512], l=lg,
                                   r=hbh_sb[:, k * 512:(k + 1) * 512], st_=(k == 0):
                                   eng.matmul(o, l, r, start=st_, stop=False),
                                   waits=waits)
                            tok = b.emit("tensor",
                                         lambda eng, o=ps[pp][:, 0:512], l=lg,
                                         r=hbl_sb[:, k * 512:(k + 1) * 512], sp_=(k == 63):
                                         eng.matmul(o, l, r, start=False, stop=sp_),
                                         inc="pe" if k == 63 else None)
                        dsg_busy[fb4] = ("pe", tok)
                        dsg_busy[fb4 + 1] = ("pe", tok)
                        op = ft % 2
                        if op == 0:
                            te = b.emit("scalar",
                                        lambda eng, o=oT[op][:], i=ps[pp][:, 0:512], sc=dnsc2[:, ft:ft + 1]:
                                        eng.activation(o, i, AF.Copy, scale=sc),
                                        waits=[("pe", tok), ("vec", t_d2), b.fence("dso0")], inc="sca")
                            bank_free[pp] = ("sca", te)
                        else:
                            te = b.emit("vector",
                                        lambda eng, o=oT[op][:], i=ps[pp][:, 0:512], sc=dnsc2[:, ft:ft + 1]:
                                        eng.tensor_scalar(o, i, sc, None, ALU.mult),
                                        waits=[("pe", tok), ("vec", t_d2), b.fence("dso1")], inc="vec")
                            bank_free[pp] = ("vec", te)
                        b.emit("gpsimd",
                               _dma(oT_d[ft * 128:(ft + 1) * 128, tc * 512:(tc + 1) * 512], oT[op][:]),
                               waits=[bank_free[pp]], inc="dso0" if op == 0 else "dso1", n=16)
                    hb_busy = ("pe", b.cnt["pe"])
                    f_tc0 = b.fence("dso0")
                    f_tc1 = b.fence("dso1")
                    for m in range(4 * tc, 4 * tc + 4):
                        mp = m % 2
                        dop = "do0" if mp == 0 else "do1"
                        b.emit("sync",
                               _dma(ob[0][:].rearrange("p (fb t) -> p fb t", fb=16),
                                    oT_d[:, m * 128:(m + 1) * 128]
                                    .rearrange("(fb p) t -> p fb t", p=128)),
                               waits=[f_tc0, f_tc1, ob_busy.get(0)], inc=dop, n=16)
                        b.emit("sync", _dma(lb[0][:], ln1n_d[m * 128:(m + 1) * 128, :]),
                               waits=[f_lnn, lb_busy.get(0)], inc=dop, n=16)
                        f_p6 = b.fence(dop)
                        trtoks = []
                        for fb in range(NKT):
                            bank = 6 if fb % 2 == 0 else 7
                            tok = b.emit("tensor",
                                         lambda eng, o=ps[bank][:, 0:128], i=ob[0][:, fb * 128:(fb + 1) * 128]:
                                         eng.transpose(o, i, ident[:]),
                                         waits=ps_wait(bank) + [f_p6, f_busy], inc="pe")
                            te = b.emit("scalar",
                                        lambda eng, o=ftile[:, fb * 128:(fb + 1) * 128], i=ps[bank][:, 0:128]:
                                        eng.copy(o, i),
                                        waits=[("pe", tok)], inc="sca")
                            bank_free[bank] = ("sca", te)
                            trtoks.append(("sca", te))
                        ob_busy[0] = ("pe", b.cnt["pe"])
                        tz = b.emit("vector",
                                    lambda eng, l=lb[0][:]: eng.tensor_add(z2[:], ftile[:], l),
                                    waits=trtoks + [f_p6], inc="vec")
                        f_busy = ("vec", tz)
                        lb_busy[0] = ("vec", tz)
                        l1 = b.emit("vector",
                                    lambda eng: eng.reduce_sum(cols2[:, 0:1], z2[:], axis=AXX),
                                    waits=[("vec", tz)], inc="vec")
                        l2 = b.emit("vector",
                                    lambda eng: eng.tensor_scalar(cols2[:, 0:1], cols2[:, 0:1], 1.0 / D, None, ALU.mult),
                                    waits=[("vec", l1)], inc="vec")
                        l3 = b.emit("vector",
                                    lambda eng: eng.tensor_scalar(z2d[:], z2[:], cols2[:, 0:1], None, ALU.subtract),
                                    waits=[("vec", l2)], inc="vec")
                        l4 = b.emit("scalar",
                                    lambda eng: eng.activation(junk[:], z2d[:], AF.Square, accum_out=cols2[:, 1:2]),
                                    waits=[("vec", l3), b.fence("dou")], inc="sca")
                        l5 = b.emit("vector",
                                    lambda eng: eng.tensor_scalar(cols2[:, 1:2], cols2[:, 1:2], 1.0 / D, EPS_LN,
                                                                  ALU.mult, ALU.add),
                                    waits=[("sca", l4)], inc="vec")
                        l6 = b.emit("scalar",
                                    lambda eng: eng.activation(cols2[:, 2:3], cols2[:, 1:2], AF.Ln),
                                    waits=[("vec", l5)], inc="sca")
                        l7 = b.emit("scalar",
                                    lambda eng: eng.activation(cols2[:, 2:3], cols2[:, 2:3], AF.Exp, scale=-0.5),
                                    waits=[("sca", l6)], inc="sca")
                        l8 = b.emit("vector",
                                    lambda eng: eng.tensor_scalar(otile[:], z2d[:], cols2[:, 2:3], None, ALU.mult),
                                    waits=[("sca", l7), b.fence("dou")], inc="vec")
                        b.emit("gpsimd", _dma(out_d[m * 128:(m + 1) * 128, :], otile[:]),
                               waits=[("vec", l8)], inc="dou", n=16)

                b.emit("gpsimd", None, waits=[b.fence("dou")])
                b.flush(blk, sems, lastw)

    return nc


def _host_layouts(inputs):
    f = np.float32

    def tile_lhsT(w):
        o, i = w.shape
        return np.ascontiguousarray(
            w.reshape(o // 128, 128, i // 128, 128).transpose(0, 3, 2, 1)
        ).reshape(o // 128, 128, i)

    wig = np.asarray(inputs["L0_attn_Wig"], f)
    wfg = np.asarray(inputs["L0_attn_Wfg"], f)
    wifT = np.concatenate([wig, wfg], axis=0).T
    wif_tiled = np.ascontiguousarray(
        wifT.reshape(16, 128, 16).transpose(1, 0, 2)).reshape(128, 256)

    j = np.arange(128)
    cmask = np.where(j[:, None] > j[None, :], np.float32(NEGBIG), np.float32(0.0)).astype(f)

    return {
        "wq_t": tile_lhsT(np.asarray(inputs["L0_attn_Wq"], f)),
        "wk_t": tile_lhsT(np.asarray(inputs["L0_attn_Wk"], f)),
        "wv_t": tile_lhsT(np.asarray(inputs["L0_attn_Wv"], f)),
        "wog_t": tile_lhsT(np.asarray(inputs["L0_attn_Wog"], f)),
        "wif_t": wif_tiled,
        "bif": np.concatenate([np.asarray(inputs["L0_attn_big"], f),
                               np.asarray(inputs["L0_attn_bfg"], f)]).reshape(16, 1),
        "gam": np.ascontiguousarray(np.asarray(inputs["L0_attn_gamma"], f).reshape(D, 1)),
        "woutT": np.ascontiguousarray(np.asarray(inputs["L0_attn_Wout"], f).T),
        "up_t": tile_lhsT(np.asarray(inputs["L0_ffn_up_weight"], f)),
        "dn_t": tile_lhsT(np.asarray(inputs["L0_ffn_down_weight"], f)),
        "upsc": np.ascontiguousarray(np.asarray(inputs["L0_ffn_up_scale"], f).reshape(DH, 1)),
        "dnsc": np.ascontiguousarray(np.asarray(inputs["L0_ffn_down_scale"], f).reshape(D, 1)),
        "osc": np.asarray(inputs["L0_ffn_output_scale"], f).reshape(1, 1),
        "ident": np.eye(128, dtype=f),
        "cmask": np.ascontiguousarray(cmask),
        "ones_r": np.ones((1, 1024), f),
    }


def kernel(**inputs):
    x = np.asarray(inputs["L0_attn_x"], np.float32)
    shared = _host_layouts(inputs)
    nc = build_nc(debug=_DEBUG)
    in_maps = []
    for bi in range(8):
        m = dict(shared)
        m["xT"] = np.ascontiguousarray(x[bi].T)
        m["xn"] = np.ascontiguousarray(x[bi])
        in_maps.append(m)
    res = run_bass_kernel_spmd(nc, in_maps, core_ids=list(range(8)))
    return np.stack([np.asarray(r["out"]) for r in res.results], axis=0)
